# version 07
# baseline (speedup 1.0000x reference)
import os
import sys
import numpy as np
from contextlib import ExitStack
import concourse.bass as bass
import concourse.mybir as mybir
from concourse.bass_utils import run_bass_kernel_spmd

F32 = mybir.dt.float32
F16 = mybir.dt.float16
BF16 = mybir.dt.bfloat16
I32 = mybir.dt.int32
ALU = mybir.AluOpType
AF = mybir.ActivationFunctionType
AX = mybir.AxisListType

NCORES = 8
D = 1024
T = 2048
TT = 1024
NTT = T // TT
DI = 2048
NDC = DI // 128
NS = 16
DTR = 64
NH = 16
DH = 64
H = 4096
EPS = 1e-5
MAGIC = 3.0 * 2.0 ** 22

ENGINES = ("sync", "vector", "scalar", "tensor", "gpsimd")

SIM_DRAINS = (os.environ.get("BASS_SIM_FP64") == "1"
              or os.environ.get("KERNEL_SIM_DRAINS") == "1")

_DEBUG_TAPS = []


class Sem:
    __slots__ = ("name", "h", "n")

    def __init__(self, name, h):
        self.name = name
        self.h = h
        self.n = 0


class Build:

    def __init__(self):
        self.nc = bass.Bass("TRN2", target_bir_lowering=False)
        self.st = ExitStack()
        self.q = {e: [] for e in ENGINES}
        self.dirty = {e: ([], []) for e in ENGINES}
        self.opct = {e: 0 for e in ENGINES}
        self.phase = "init"
        self.drainct = {}
        self.dma_touch = {}
        self.sems = []
        self.ntap = 0

    def sem(self, name):
        s = Sem(name, self.st.enter_context(self.nc.semaphore(name)))
        self.sems.append(s)
        return s

    def sbuf(self, name, shape, dt=F32):
        return self.st.enter_context(self.nc.sbuf_tensor(name, shape, dt))

    def psum(self, name, shape=(128, 512)):
        return self.st.enter_context(self.nc.psum_tensor(name, list(shape), F32))

    def on(self, eng, fn):
        self.q[eng].append(fn)

    @staticmethod
    def _ivs(aps):
        out = []
        for a in aps:
            t = getattr(a, "tensor", None)
            if t is None:
                continue
            try:
                if str(a.space) not in ("MemorySpace.SBUF", "MemorySpace.PSUM"):
                    continue
            except Exception:
                pass
            dims = a.ap
            width = dims[0][0] if dims else 1
            off = a.offset % max(1, width) if width > 0 else a.offset
            ext = 1
            for st_, ct in dims[1:]:
                ext += abs(st_) * (ct - 1)
            out.append((t.name, off - (ext if any(st_ < 0 for st_, _ in dims[1:]) else 0),
                        off + ext))
        return out

    def _overlap(self, a, bset, eng=None):
        for cur in a:
            (n1, l1, h1) = cur[:3]
            for rec in bset:
                (n2, l2, h2) = rec[:3]
                if eng is not None and not SIM_DRAINS:
                    if self.opct[eng] - rec[3] > 1:
                        continue
                if n1 == n2 and l1 < h2 and l2 < h1:
                    return True
        return False

    def _pre(self, eng, reads, writes, wait, track=True):
        need_drain = False
        wait = list(wait)
        if track:
            best = {}
            for (nm, lo, hi) in self._ivs(list(reads) + list(writes)):
                for (l2, h2, sm, tg) in self.dma_touch.get(nm, ()):
                    if lo < h2 and l2 < hi:
                        if best.get(id(sm), (None, -1))[1] < tg:
                            best[id(sm)] = (sm, tg)
            for sm, tg in best.values():
                wait.append((sm, tg))
        if track:
            self.opct[eng] += 1
            ct = self.opct[eng]
            R = [iv + (ct,) for iv in self._ivs(reads)]
            Wv = [iv + (ct,) for iv in self._ivs(writes)]
            tr, tw = self.dirty[eng]
            need_drain = (self._overlap(R, tw, eng) or self._overlap(Wv, tw, eng)
                          or self._overlap(Wv, tr, eng))
            if need_drain:
                k = (self.phase, eng)
                if os.environ.get("KERNEL_DRAIN_REPORT") == "2":
                    f = sys._getframe(2)
                    k = (self.phase, eng, f.f_lineno, sys._getframe(3).f_lineno)
                self.drainct[k] = self.drainct.get(k, 0) + 1
                tr.clear()
                tw.clear()
                tr, tw = self.dirty[eng]
            tr.extend(R)
            tw.extend(Wv)
        wl = [(s.h, v) for (s, v) in wait]
        if wait:
            for s, v in wait:
                assert v <= s.n, f"wait on {s.name}>={v} but planned only {s.n}"

        def pre(e):
            for h, v in wl:
                e.wait_ge(h, v)
            if need_drain:
                e.drain()
        return pre

    def _fin(self, inst, inc):
        if inc is not None:
            inst.then_inc(inc.h, 1)

    def _bump(self, inc):
        if inc is not None:
            inc.n += 1

    def tt(self, out, a, b, op, wait=(), inc=None):
        pre = self._pre("vector", [a, b], [out], wait)
        self._bump(inc)
        def f(e):
            pre(e)
            self._fin(e.tensor_tensor(out, a, b, op), inc)
        self.on("vector", f)

    def ts(self, out, a, s1, s2=None, op0=ALU.mult, op1=None, wait=(), inc=None):
        rd = [a] + [x for x in (s1, s2) if hasattr(x, "tensor")]
        pre = self._pre("vector", rd, [out], wait)
        self._bump(inc)
        if op1 is None:
            def f(e):
                pre(e)
                self._fin(e.tensor_scalar(out, a, s1, None, op0), inc)
        else:
            def f(e):
                pre(e)
                self._fin(e.tensor_scalar(out, a, s1, s2, op0, op1), inc)
        self.on("vector", f)

    def stt(self, out, in0, scalar, in1, op0, op1, wait=(), inc=None):
        rd = [in0, in1] + ([scalar] if hasattr(scalar, "tensor") else [])
        pre = self._pre("vector", rd, [out], wait)
        self._bump(inc)
        def f(e):
            pre(e)
            self._fin(e.scalar_tensor_tensor(out, in0, scalar, in1, op0, op1), inc)
        self.on("vector", f)

    def scan(self, out, a, b, init, wait=(), inc=None):
        pre = self._pre("vector", [a, b, init], [out], wait)
        self._bump(inc)
        def f(e):
            pre(e)
            self._fin(e.tensor_tensor_scan(out, a, b, init, ALU.mult, ALU.add), inc)
        self.on("vector", f)

    def ttr(self, out, in0, in1, accum, wait=(), inc=None):
        pre = self._pre("vector", [in0, in1], [out, accum], wait)
        self._bump(inc)
        def f(e):
            pre(e)
            self._fin(e.tensor_tensor_reduce(out, in0, in1, 1.0, 0.0,
                                             ALU.mult, ALU.add, accum), inc)
        self.on("vector", f)

    def vcopy(self, out, src, wait=(), inc=None):
        pre = self._pre("vector", [src], [out], wait)
        self._bump(inc)
        def f(e):
            pre(e)
            self._fin(e.tensor_copy(out, src), inc)
        self.on("vector", f)

    def vred(self, out, src, op, absval=None, wait=(), inc=None):
        pre = self._pre("vector", [src], [out], wait)
        self._bump(inc)
        def f(e):
            pre(e)
            self._fin(e.tensor_reduce(out, src, AX.X, op, apply_absolute_value=absval), inc)
        self.on("vector", f)

    def vrecip(self, out, src, wait=(), inc=None):
        pre = self._pre("vector", [src], [out], wait)
        self._bump(inc)
        def f(e):
            pre(e)
            self._fin(e.reciprocal(out, src), inc)
        self.on("vector", f)

    def memset(self, out, val, eng="vector", wait=(), inc=None):
        pre = self._pre(eng, [], [out], wait)
        self._bump(inc)
        def f(e):
            pre(e)
            self._fin(e.memset(out, val), inc)
        self.on(eng, f)

    def act(self, out, src, func, bias=None, scale=None, wait=(), inc=None):
        rd = [src] + [x for x in (bias, scale) if hasattr(x, "tensor")]
        pre = self._pre("scalar", rd, [out], wait)
        self._bump(inc)
        kw = {}
        if bias is not None:
            kw["bias"] = bias
        if scale is not None:
            kw["scale"] = scale
        def f(e):
            pre(e)
            self._fin(e.activation(out, src, func, **kw), inc)
        self.on("scalar", f)

    def scopy(self, out, src, wait=(), inc=None):
        pre = self._pre("scalar", [src], [out], wait)
        self._bump(inc)
        def f(e):
            pre(e)
            self._fin(e.activation(out, src, AF.Copy), inc)
        self.on("scalar", f)

    def gtt(self, out, a, b, op, wait=(), inc=None):
        pre = self._pre("gpsimd", [a, b], [out], wait)
        self._bump(inc)
        def f(e):
            pre(e)
            self._fin(e.tensor_tensor(out, a, b, op), inc)
        self.on("gpsimd", f)

    def dma(self, eng, out, src, wait=(), inc=None, guard=True):
        w = list(wait)
        if guard and getattr(self, "guard_sems", None):
            have = {}
            for sm, v in w:
                have[id(sm)] = max(have.get(id(sm), 0), v)
            for sm in self.guard_sems:
                if sm.n > 0 and have.get(id(sm), -1) < sm.n:
                    w.append((sm, sm.n))
        pre = self._pre(eng, [src], [out], w, track=False)
        if inc is not None:
            inc.n += 16
            for iv in self._ivs([src, out]):
                self.dma_touch.setdefault(iv[0], []).append(
                    (iv[1], iv[2], inc, inc.n))
        def f(e):
            pre(e)
            d = e.dma_start(out, src)
            if inc is not None:
                d.then_inc(inc.h, 16)
        self.on(eng, f)

    def coll(self, kind, op, ins, outs, wait=(), inc=None):
        pre = self._pre("gpsimd", [], [], wait)
        self._bump(inc)
        def f(e):
            pre(e)
            c = e.collective_compute(kind, op, replica_groups=[list(range(NCORES))],
                                     ins=ins, outs=outs)
            if inc is not None:
                c.then_inc(inc.h, 1)
        self.on("gpsimd", f)

    def mm(self, out, pairs, wait=(), inc=None, ldw_once=False):
        rd = [p for pr in pairs for p in pr]
        pre = self._pre("tensor", rd, [out], wait)
        self._bump(inc)
        n = len(pairs)
        def f(e):
            pre(e)
            last = None
            for i, (l, r) in enumerate(pairs):
                if not (ldw_once and i > 0):
                    e.ldweights(l)
                last = e.matmul(out, l, r, start=(i == 0), stop=(i == n - 1))
            self._fin(last, inc)
        self.on("tensor", f)

    def barrier(self, sems):
        targets = [(s, s.n) for s in sems]
        for eng in ENGINES:
            for s, v in targets:
                self.wait(eng, s, v)

    def wait(self, eng, sem, val):
        assert val <= sem.n, f"{eng} wait {sem.name}>={val}, planned {sem.n}"
        def f(e):
            e.wait_ge(sem.h, val)
        self.on(eng, f)

    def finalize(self):
        blk = self.st.enter_context(self.nc.Block())
        qs = self.q

        @blk.sync
        def _(e):
            for f in qs["sync"]:
                f(e)

        @blk.vector
        def _(e):
            for f in qs["vector"]:
                f(e)

        @blk.scalar
        def _(e):
            for f in qs["scalar"]:
                f(e)

        @blk.tensor
        def _(e):
            for f in qs["tensor"]:
                f(e)

        @blk.gpsimd
        def _(e):
            for f in qs["gpsimd"]:
                f(e)


def _chunked(vec, p=128):
    n = vec.shape[0]
    return np.ascontiguousarray(vec.reshape(n // p, p).T)


def _qkvb_perm():
    perm = []
    for c0 in (0, DH, 2 * DH):
        for h in range(NH):
            for i in range(DH):
                perm.append(h * (3 * DH + 1) + c0 + i)
    for h in range(NH):
        perm.append(h * (3 * DH + 1) + 3 * DH)
    return np.array(perm)


def stage_inputs(inputs):
    f32 = np.float32
    g = {}
    g["w_in_f"] = np.ascontiguousarray(inputs["L0_attn_fwd_in_w"].T)
    g["w_in_b"] = np.ascontiguousarray(inputs["L0_attn_bwd_in_w"].T)
    g["w_xp_f"] = np.ascontiguousarray(inputs["L0_attn_fwd_xproj_w"].T)
    g["w_xp_b"] = np.ascontiguousarray(inputs["L0_attn_bwd_xproj_w"].T)
    g["w_dt_f"] = np.ascontiguousarray(inputs["L0_attn_fwd_dt_w"].T)
    g["w_dt_b"] = np.ascontiguousarray(inputs["L0_attn_bwd_dt_w"].T)
    g["w_out_f"] = np.ascontiguousarray(inputs["L0_attn_fwd_out_w"].T)
    g["w_out_b"] = np.ascontiguousarray(inputs["L0_attn_bwd_out_w"].T)
    g["w_mrg"] = np.ascontiguousarray(inputs["L0_attn_merge_w"].T)
    g["w1"] = np.ascontiguousarray(inputs["L0_ffn_W1"].T)
    g["w2"] = np.ascontiguousarray(inputs["L0_ffn_W2"].T)
    g["wqkv"] = np.ascontiguousarray(inputs["L1_attn_W_qkvb"][:, _qkvb_perm()])
    g["wo"] = np.ascontiguousarray(inputs["L1_attn_W_o"])
    g["wl1"] = np.ascontiguousarray(inputs["L1_ffn_weight"].T)
    for nm, key in [("convb_f", "L0_attn_fwd_conv_b"), ("convb_b", "L0_attn_bwd_conv_b"),
                    ("dtb_f", "L0_attn_fwd_dt_b"), ("dtb_b", "L0_attn_bwd_dt_b"),
                    ("dvec_f", "L0_attn_fwd_D"), ("dvec_b", "L0_attn_bwd_D"),
                    ("b1c", "L0_ffn_b1"), ("b2c", "L0_ffn_b2"),
                    ("lng1", "L0_attn_ln_g"), ("lnb1", "L0_attn_ln_b"),
                    ("mrgb", "L0_attn_merge_b"),
                    ("lng3", "L1_attn_ln_g"), ("lnb3", "L1_attn_ln_b"),
                    ("gam", "L1_ffn_gamma"), ("bet", "L1_ffn_beta")]:
        g[nm] = _chunked(np.asarray(inputs[key], f32))
    for nm, key in [("convw_f", "L0_attn_fwd_conv_w"), ("convw_b", "L0_attn_bwd_conv_w")]:
        w = np.asarray(inputs[key], f32)
        g[nm] = np.ascontiguousarray(w.reshape(NDC, 128, 4).transpose(1, 0, 2).reshape(128, NDC * 4))
    for nm, key in [("alog_f", "L0_attn_fwd_A_log"), ("alog_b", "L0_attn_bwd_A_log")]:
        a = np.asarray(inputs[key], f32)
        g[nm] = np.ascontiguousarray(a.reshape(NDC, 128, NS).transpose(1, 0, 2).reshape(128, NDC * NS))
    g["ones16"] = np.ones((128, 128), np.float16)
    g["cvec"] = np.tile(np.array([[EPS, 1.0, MAGIC, -MAGIC + 127.0, 0.0, 0.0, 0.0, 0.0]], np.float32), (128, 1))
    bd2 = np.zeros((128, 128), np.float16)
    bd2[:64, :64] = 1.0
    bd2[64:, 64:] = 1.0
    g["bd2"] = bd2
    selb = np.zeros((128, NS * 128), np.float16)
    for s_ in range(NS):
        selb[s_, s_ * 128:(s_ + 1) * 128] = 1.0
    g["selb"] = selb
    selh = np.zeros((128, 8 * 128), np.float16)
    for ht in range(8):
        selh[2 * ht, ht * 128:ht * 128 + 64] = 1.0
        selh[2 * ht + 1, ht * 128 + 64:(ht + 1) * 128] = 1.0
    g["selh"] = selh
    return g


def kernel(**inputs):
    g = stage_inputs(inputs)
    x = np.asarray(inputs["L0_attn_x"], np.float32)
    in_maps = []
    for c in range(NCORES):
        m = dict(g)
        m["x_dm"] = np.ascontiguousarray(x[c].T)
        in_maps.append(m)
    nc = build_program()
    res = run_bass_kernel_spmd(nc, in_maps, core_ids=list(range(NCORES)))
    out = np.empty((NCORES, T, D), np.float32)
    for c in range(NCORES):
        oc = np.asarray(res.results[c]["out_dm"])
        out_c = oc.reshape(D, NCORES, T // NCORES).transpose(1, 2, 0)
        out[:, c * (T // NCORES):(c + 1) * (T // NCORES), :] = out_c
    return out


def build_program():
    b = Build()
    nc = b.nc
    emit_all(b)
    b.finalize()
    if os.environ.get("KERNEL_DRAIN_REPORT") == "1":
        tot = 0
        for k, v in sorted(b.drainct.items(), key=lambda x: -x[1]):
            print("DRAIN %-32s %-8s %5d" % (k[0], k[1], v))
            tot += v
        print("DRAIN TOTAL", tot)
    return nc



def emit_all(b: Build):
    nc = b.nc
    AP = bass.AP

    def din(name, shape, dt=F32):
        return nc.dram_tensor(name, list(shape), dt, kind="ExternalInput").ap()

    X = din("x_dm", [D, T])
    OUT = nc.dram_tensor("out_dm", [D, T], F32, kind="ExternalOutput").ap()
    W = {}
    for nm, shape in [("w_in_f", (D, 2 * DI)), ("w_in_b", (D, 2 * DI)),
                      ("w_xp_f", (DI, 96)), ("w_xp_b", (DI, 96)),
                      ("w_dt_f", (DTR, DI)), ("w_dt_b", (DTR, DI)),
                      ("w_out_f", (DI, D)), ("w_out_b", (DI, D)),
                      ("w_mrg", (DI, D)), ("w1", (D, H)), ("w2", (H, D)),
                      ("wqkv", (D, 3088)), ("wo", (D, D)), ("wl1", (D, D))]:
        W[nm] = din(nm, shape)
    pp_d = {}
    for nm, w in [("convw_f", NDC * 4), ("convw_b", NDC * 4), ("convb_f", NDC),
                  ("convb_b", NDC), ("dtb_f", NDC), ("dtb_b", NDC),
                  ("dvec_f", NDC), ("dvec_b", NDC), ("alog_f", NDC * NS),
                  ("alog_b", NDC * NS), ("b1c", 32), ("b2c", 8), ("lng1", 8),
                  ("lnb1", 8), ("mrgb", 8), ("lng3", 8), ("lnb3", 8),
                  ("gam", 8), ("bet", 8)]:
        pp_d[nm] = din(nm, (128, w))
    ONES = din("ones16", (128, 128), F16)
    BD2 = din("bd2", (128, 128), F16)
    CVEC = din("cvec", (128, 8), F32)
    SELB = din("selb", (128, NS * 128), F16)
    SELH = din("selh", (128, 8 * 128), F16)

    dr = lambda nm, shape, dt=F16: nc.dram_tensor(nm, list(shape), dt)
    sp_xn = dr("sp_xn", [D, T]); sp_xm = dr("sp_xm", [D, T])
    sp_xi = dr("sp_xi", [DI, TT]); sp_silz = dr("sp_silz", [DI, TT])
    sp_yf = dr("sp_yf", [D, T]); sp_s = dr("sp_s", [D, T])
    sp_h = dr("sp_h", [H, T]); sp_wq2 = dr("sp_wq2", [H, D])
    sp_g = dr("sp_g", [D, T]); sp_outs = dr("sp_outs", [D, T])
    sp_g2 = dr("sp_g2", [D, T])
    sp_x2 = dr("sp_x2", [D, T])
    bounce = dr("bounce", [4, 128], F32)
    ar_i = [dr("ar%d_i" % i, [1, 2], F32) for i in range(3)]
    ar_o = [dr("ar%d_o" % i, [1, 2], F32) for i in range(3)]
    ag_i = dr("ag_i", [D * T], F16)
    ag_o = dr("ag_o", [NCORES * D * T], F16)

    A = b.sbuf("A", [128, NDC * (TT + 4)], F16)
    B = b.sbuf("B", [128, NDC * TT], F16)
    C = b.sbuf("C", [128, NDC * TT], F16)
    Dm = b.sbuf("Dm", [128, NDC * TT], F16)
    E = b.sbuf("E", [128, NDC * TT], F16)
    Fp = b.sbuf("Fp", [128, 5120], F32)
    def dslot(i, n=1):
        return Dm[:, i * TT:(i + n) * TT]
    def eslot(i, n=1):
        return E[:, i * 2048:(i + n) * 2048]
    XWc = NDC * TT
    OFF_M, OFF_A, OFF_C, OFF_R16 = 10240, 11264, 12288, 13312
    OFF_WDT = 3584
    def rep_ap(off, lo, n):
        return bass.AP(E, off + lo, [[XWc, 128], [1, n]])
    def rep_bc(off):
        return bass.AP(E, off, [[XWc, 128], [0, 8], [1, TT]])
    def r16row(j, lo, n, np_=1):
        return bass.AP(E, j * XWc + OFF_R16 + lo, [[XWc, np_], [1, n]])

    pp = {nm: b.sbuf("p_" + nm, [128, pp_d[nm].shape[1]], F32) for nm in pp_d}
    pAn = {d: b.sbuf("pAn" + d, [128, NDC * NS], F32) for d in "fb"}
    c_one = b.sbuf("c_one", [128, 128], F16)
    c_bd2 = b.sbuf("c_bd2", [128, 128], F16)
    c_selb = b.sbuf("c_selb", [128, NS * 128], F16)
    c_selh = b.sbuf("c_selh", [128, 8 * 128], F16)
    t_mm1 = b.sbuf("t_mm1", [128, 4], F32)
    t_hm = b.sbuf("t_hm", [128, 4], F32)
    t_sc = b.sbuf("t_sc", [1, 32], F32)
    t_srep = b.sbuf("t_srep", [128, 16], F32)
    t_sone = b.sbuf("t_sone", [1, 128], F32)
    t_st = b.sbuf("t_st", [128, NDC * NS], F32)
    t_halo = b.sbuf("t_halo", [128, NDC * 4], F16)
    t_ar = b.sbuf("t_ar", [1, 2], F32)
    t_cst = b.sbuf("t_cst", [128, 8], F32)
    t_wst = b.sbuf("t_wst", [128, 64], F32)
    t_scr = b.sbuf("t_scr", [128, 1024], F16)
    t_red = b.sbuf("t_red", [1, 128], F32)

    PS = [b.psum("ps%d" % i) for i in range(8)]

    LD = b.sem("LD"); ST = b.sem("ST"); GW = b.sem("GW")
    VE = b.sem("VE"); SE = b.sem("SE"); TE = b.sem("TE"); CC = b.sem("CC")
    GT = b.sem("GT")
    b.guard_sems = [LD, ST, GW, VE, SE, TE]
    bank_free = {i: None for i in range(8)}

    def mm_bank(i, out, pairs, wait=(), inc=None, ldw_once=False,
                start=None, stop=None, no_ldw=False):
        w = list(wait)
        if start is None or start:
            if bank_free[i] is not None:
                w.append(bank_free[i])
                bank_free[i] = None
        rd = [p for pr in pairs for p in pr]
        pre = b._pre("tensor", rd, [out], w)
        b._bump(inc)
        n = len(pairs)
        st_ = True if start is None else start
        sp_ = True if stop is None else stop
        def f(e):
            pre(e)
            last = None
            for j, (l, r) in enumerate(pairs):
                if not no_ldw and not (ldw_once and j > 0):
                    e.ldweights(l)
                last = e.matmul(out, l, r,
                                start=(st_ and j == 0),
                                stop=(sp_ and j == n - 1))
            if inc is not None:
                last.then_inc(inc.h, 1)
        b.on("tensor", f)

    def bank_release(i, sem):
        bank_free[i] = (sem, sem.n)

    def bfree(tile, off, n_rep, inner):
        return AP(tile.tensor if hasattr(tile, "tensor") else tile,
                  (tile.offset if hasattr(tile, "offset") else 0) + off,
                  [[_pstep(tile), 128], [0, n_rep], [1, inner]])

    def _pstep(t):
        if hasattr(t, "ap"):
            return t.ap[0][0]
        return t.shape[1]

    taps = {}
    S_taps_marker = True
    for (nm, shape, dt) in _DEBUG_TAPS:
        taps[nm] = nc.dram_tensor(nm, list(shape), dt, kind="ExternalOutput").ap()

    def tap(nm, src_ap, wait=()):
        if nm in taps:
            b.dma("sync", taps[nm], src_ap, wait=wait, inc=ST)

    b.dma("sync", c_one[:, :], ONES, inc=LD)
    b.dma("sync", c_bd2[:, :], BD2, inc=LD)
    b.dma("sync", t_cst[:, :], CVEC, inc=LD)
    b.dma("sync", c_selb[:, :], SELB, inc=LD)
    b.dma("sync", c_selh[:, :], SELH, inc=LD)
    nc.const_aps.register(F32, EPS, t_cst[:, 0:1])
    nc.const_aps.register(F32, 1.0, t_cst[:, 1:2])
    nc.const_aps.register(F32, MAGIC, t_cst[:, 2:3])
    nc.const_aps.register(F32, -MAGIC + 127.0, t_cst[:, 3:4])
    for nm in pp:
        b.dma("sync", pp[nm][:, :], pp_d[nm], inc=LD)
    consts_ld = LD.n
    b.memset(t_sone[:, :], 1.0, eng="vector", inc=VE)
    b.memset(t_st[:, :], 0.0, eng="vector", inc=VE)
    b.memset(t_sc[:, :], 0.0, eng="vector", inc=VE)
    b.wait("scalar", LD, consts_ld)
    for d in "fb":
        b.act(Fp[:, 0:NDC * NS], pp["alog_" + d][:, :], AF.Exp,
              wait=[(VE, VE.n)], inc=SE)
        b.ts(pAn[d][:, :], Fp[:, 0:NDC * NS], -1.0, wait=[(SE, SE.n)], inc=VE)

    S = dict(locals())
    S["b"] = b
    b.phase = "ln1"; phase_ln1(S)
    b.phase = "wstats"; phase_wstats(S)
    for d_ in "fb":
        for tl_ in range(NTT):
            b.phase = "mamba"; phase_mamba(S, d_, tl_)
    phase_merge_check = None
    b.phase = "ffn"; phase_ffn(S)
    b.phase = "x1ag"; phase_x1_ag(S)
    b.phase = "l1"; phase_l1(S)

    b.wait("sync", ST, ST.n)
    b.wait("sync", LD, LD.n)
    b.wait("gpsimd", GW, GW.n)
    b.wait("gpsimd", GT, GT.n)
    b.wait("vector", VE, VE.n)
    b.wait("scalar", SE, SE.n)
    b.wait("tensor", TE, TE.n)
    b.wait("gpsimd", CC, CC.n)


def _v(S, *names):
    return [S[n] for n in names]


def ap3(th, off, pstep, dims):
    return bass.AP(th, off, [[pstep, 128]] + dims)


def ln_stats(S, pairs_x, pairs_sq, with_rc, wx=(), wsq=()):
    b = S["b"]
    PS = S["PS"]; VE = S["VE"]; SE = S["SE"]; TE = S["TE"]
    Fp = S["Fp"]
    mm_bank, bank_release = S["mm_bank"], S["bank_release"]
    rep_ap = S["rep_ap"]
    OFF_M, OFF_A, OFF_C = S["OFF_M"], S["OFF_A"], S["OFF_C"]
    for f in range(2):
        sb_, qb, b2 = 6 + f, 4 + f, 0 + f
        sA = Fp[:, 0:512]
        sB = Fp[:, 512:1024]
        mm_bank(sb_, PS[sb_][:, 0:512], pairs_x(f), wait=list(wx), inc=TE,
                ldw_once=True)
        mm_bank(qb, PS[qb][:, 0:512], pairs_sq(f), wait=list(wsq), inc=TE,
                ldw_once=True)
        te_s = TE.n
        mmw = [(TE, te_s)]
        bf = S["bank_free"]
        if bf[b2] is not None:
            mmw.append(bf[b2]); bf[b2] = None
        b.ts(sA, PS[sb_][:, 0:512], 1.0 / D, wait=mmw)
        b.vcopy(rep_ap(OFF_M, f * 512, 512), sA)
        b.tt(PS[b2][:, 0:512], sA, sA, ALU.mult)
        b.ts(sB, PS[qb][:, 0:512], 1.0 / D)
        b.tt(PS[qb][:, 0:512], sB, PS[b2][:, 0:512], ALU.subtract, inc=VE)
        b.act(PS[b2][:, 0:512], PS[qb][:, 0:512], AF.Sqrt, bias=EPS,
              wait=[(VE, VE.n)], inc=SE)
        b.vrecip(sA, PS[b2][:, 0:512], wait=[(SE, SE.n)])
        b.vcopy(rep_ap(OFF_A, f * 512, 512), sA, inc=VE)
        if with_rc:
            b.tt(sB, sA, sA, ALU.mult)
            b.tt(PS[b2][:, 0:512], PS[qb][:, 0:512], sB, ALU.mult, inc=VE)
            b.act(PS[qb][:, 0:512], PS[b2][:, 0:512], AF.Sqrt, bias=EPS,
                  wait=[(VE, VE.n)], inc=SE)
            b.vrecip(sB, PS[qb][:, 0:512], wait=[(SE, SE.n)])
            b.tt(PS[b2][:, 0:512], sA, sB, ALU.mult)
            b.vcopy(rep_ap(OFF_C, f * 512, 512), PS[b2][:, 0:512], inc=VE)
        for i in (sb_, qb, b2):
            bank_release(i, VE)


def phase_ln1(S):
    b = S["b"]; AP = bass.AP
    A, B, C, Dm, Fp = _v(S, "A", "B", "C", "Dm", "Fp")
    X, sp_xn, sp_xm = _v(S, "X", "sp_xn", "sp_xm")
    PS, c_one, E = _v(S, "PS", "c_one", "E")
    t_mm1 = S["t_mm1"]
    rep_ap, rep_bc = _v(S, "rep_ap", "rep_bc")
    OFF_M, OFF_A, OFF_C = _v(S, "OFF_M", "OFF_A", "OFF_C")
    pp, t_ar, t_red, bounce = _v(S, "pp", "t_ar", "t_red", "bounce")
    LD, ST, GW, VE, SE, TE, CC = _v(S, "LD", "ST", "GW", "VE", "SE", "TE", "CC")
    mm_bank, bank_release = S["mm_bank"], S["bank_release"]
    ar_i, ar_o = S["ar_i"], S["ar_o"]
    tap = S["tap"]
    AW = NDC * (TT + 4)

    for tl in range(NTT):
        x16 = ap3(A, 0, AW, [[TT, 8], [1, TT]])
        sq = ap3(A, 8 * TT, AW, [[TT, 8], [1, TT]])
        cen = ap3(B, 0, NDC * TT, [[TT, 8], [1, TT]])
        xnb = ap3(B, 8 * TT, NDC * TT, [[TT, 8], [1, TT]])
        xm0 = ap3(C, 0, NDC * TT, [[TT, 8], [1, TT]])
        xmb = ap3(C, 8 * TT, NDC * TT, [[TT, 8], [1, TT]])
        b.dma("gpsimd", x16, AP(X.tensor, tl * TT, [[T, 128], [128 * T, 8], [1, TT]]),
              inc=GW)
        gw_x = GW.n
        b.wait("scalar", GW, gw_x)
        b.act(sq, x16, AF.Square, inc=SE)
        se_sq = SE.n
        px = lambda f: [(c_one[:, 0:128],
                         AP(A, ch * TT + f * 512, [[AW, 128], [1, 512]]))
                        for ch in range(8)]
        psq = lambda f: [(c_one[:, 0:128],
                          AP(A, 8 * TT + ch * TT + f * 512, [[AW, 128], [1, 512]]))
                         for ch in range(8)]
        ln_stats(S, px, psq, True,
                 wx=[(GW, gw_x), (LD, S["consts_ld"])], wsq=[(SE, se_sq)])
        b.wait("vector", TE, TE.n)
        b.tt(cen, x16, rep_bc(OFF_M), ALU.subtract)
        b.tt(xnb, cen, rep_bc(OFF_A), ALU.mult, inc=VE)
        b.dma("sync", AP(sp_xn, tl * TT, [[T, 128], [128 * T, 8], [1, TT]]),
              xnb, wait=[(VE, VE.n)], inc=ST)
        xnb_flat = AP(B, 8 * TT, [[NDC * TT, 128], [1, 8 * TT]])
        b.vred(t_mm1[:, 2 * tl:2 * tl + 1], xnb_flat, ALU.min)
        b.vred(t_mm1[:, 2 * tl + 1:2 * tl + 2], xnb_flat, ALU.max)
        b.tt(xm0, cen, rep_bc(OFF_C), ALU.mult)
        for ch in range(8):
            b.ts(AP(C, 8 * TT + ch * TT, [[NDC * TT, 128], [1, TT]]),
                 AP(C, ch * TT, [[NDC * TT, 128], [1, TT]]),
                 pp["lng1"][:, ch:ch + 1], pp["lnb1"][:, ch:ch + 1],
                 ALU.mult, ALU.add, inc=(VE if ch == 7 else None))
        b.dma("sync", AP(sp_xm, tl * TT, [[T, 128], [128 * T, 8], [1, TT]]),
              xmb, wait=[(VE, VE.n)], inc=ST)
        tap("tap_xn_t%d" % tl, xnb, wait=[(VE, VE.n)])
        tap("tap_xm_t%d" % tl, xmb, wait=[(VE, VE.n)])
    b.tt(t_mm1[:, 0:1], t_mm1[:, 0:1], t_mm1[:, 2:3], ALU.min)
    b.tt(t_mm1[:, 1:2], t_mm1[:, 1:2], t_mm1[:, 3:4], ALU.max, inc=VE)
    b.dma("sync", AP(bounce, 0, [[1, 1], [1, 128]]), t_mm1[:, 0:1],
          wait=[(VE, VE.n)], inc=ST)
    b.dma("sync", AP(bounce, 128, [[1, 1], [1, 128]]), t_mm1[:, 1:2],
          wait=[(VE, VE.n)], inc=ST)
    b.dma("sync", t_red[0:1, :], bounce.ap()[0:1, :], wait=[(ST, ST.n)], inc=LD)
    b.vred(t_ar[0:1, 0:1], t_red[0:1, :], ALU.min, wait=[(LD, LD.n)], inc=VE)
    b.ts(t_ar[0:1, 0:1], t_ar[0:1, 0:1], -1.0, inc=VE)
    b.dma("sync", t_red[0:1, :], bounce.ap()[1:2, :], wait=[(ST, ST.n)], inc=LD)
    b.vred(t_ar[0:1, 1:2], t_red[0:1, :], ALU.max, wait=[(LD, LD.n)], inc=VE)
    b.dma("sync", ar_i[0].ap()[0:1, :], t_ar[0:1, :], wait=[(VE, VE.n)], inc=ST)
    st_ar = ST.n
    b.wait("gpsimd", ST, st_ar)
    b.coll("AllReduce", ALU.max, [ar_i[0].ap().opt()], [ar_o[0].ap().opt()], inc=CC)
    S["cc_ar1"] = CC.n


def phase_wstats(S):
    b = S["b"]; AP = bass.AP
    Fp, W = _v(S, "Fp", "W")
    GW, VE, SE, ST = _v(S, "GW", "VE", "SE", "ST")
    t_wst = S["t_wst"]
    slots = []
    k = 0
    ve_slot = {}
    kq = []
    for nm, rows, cols, red in [("w1", D, H, ALU.max), ("w2", H, D, ALU.max),
                                ("wl1", D, D, ALU.add)]:
        ctile = min(cols, 1024)
        per_row = cols // ctile
        for i in range(rows // 128):
            for j in range(per_row):
                kq.append((nm, cols, ctile, i, j, red))
    for q, (nm, cols, ctile, i, j, red) in enumerate(kq):
        sl = q % 4
        buf = Fp[:, 1024 + sl * 1024:1024 + sl * 1024 + ctile]
        wv = [(VE, ve_slot[sl])] if sl in ve_slot else []
        b.dma("gpsimd" if sl % 2 == 0 else "sync", buf,
              AP(W[nm].tensor, i * 128 * cols + j * ctile,
                 [[cols, 128], [1, ctile]]),
              wait=wv, inc=(GW if sl % 2 == 0 else ST))
        gw = (GW if sl % 2 == 0 else ST).n
        if q % 2 == 0:
            b.vred(t_wst[:, k:k + 1], buf, red, absval=True,
                   wait=[(GW if sl % 2 == 0 else ST, gw)], inc=VE)
        else:
            b.vred(t_wst[:, 60:61], buf, red, absval=True,
                   wait=[(GW if sl % 2 == 0 else ST, gw)])
            b.tt(t_wst[:, k:k + 1], t_wst[:, k:k + 1], t_wst[:, 60:61],
                 red, inc=VE)
            k += 1
        ve_slot[sl] = VE.n
    for nm in ("w1", "w2", "wl1"):
        slots.append((nm, k))
    S["wst_slots"] = slots
    S["ve_wst"] = VE.n


def phase_mamba(S, d, tl):
    b = S["b"]; AP = bass.AP
    A, B, C, Dm, E, Fp = _v(S, "A", "B", "C", "Dm", "E", "Fp")
    PS, c_one, c_selb, pp, pAn = _v(S, "PS", "c_one", "c_selb", "pp", "pAn")
    t_st, t_halo = _v(S, "t_st", "t_halo")
    OFF_WDT = S["OFF_WDT"]
    W, sp_xm, sp_xi, sp_silz, sp_yf, sp_s = _v(S, "W", "sp_xm", "sp_xi",
                                               "sp_silz", "sp_yf", "sp_s")
    sp_xn = S["sp_xn"]
    LD, ST, GW, VE, SE, TE = _v(S, "LD", "ST", "GW", "VE", "SE", "TE")
    GT = S["GT"]
    mm_bank, bank_release, tap = S["mm_bank"], S["bank_release"], S["tap"]
    AW = NDC * (TT + 4)
    BW = CW = DW = EW = NDC * TT
    sfx = "_" + d
    b.barrier([LD, ST, GW, VE, SE, TE])
    axi = lambda ch, lo, n: AP(A, ch * (TT + 4) + lo, [[AW, 128], [1, n]])
    adl = lambda ch, lo, n: AP(A, ch * TT + lo, [[AW, 128], [1, n]])
    bsl = lambda ch, lo, n: AP(B, ch * TT + lo, [[BW, 128], [1, n]])
    csl = lambda ch, lo, n: AP(C, ch * TT + lo, [[CW, 128], [1, n]])
    dsl = lambda i, lo, n: AP(Dm, i * TT + lo, [[DW, 128], [1, n]])
    esl = lambda i, lo, n: AP(E, i * TT + lo, [[EW, 128], [1, n]])

    b.phase = b.phase.split(':')[0] + ':load_mamba_input_t'
    src_tl = tl if d == "f" else (NTT - 1 - tl)
    xs_raw = ap3(C, 0, CW, [[TT, 8], [1, TT]])
    b.dma("sync", xs_raw,
          AP(sp_xm, src_tl * TT, [[T, 128], [128 * T, 8], [1, TT]]), inc=LD)
    ld_xs = LD.n
    if d == "b":
        b.vcopy(ap3(C, 8 * TT, CW, [[TT, 8], [1, TT]]),
                AP(C, TT - 1, [[CW, 128], [TT, 8], [-1, TT]]),
                wait=[(LD, ld_xs)], inc=VE)
        xs_off = 8 * TT
        ve_xs = VE.n
        if tl == 0:
            tap("tap_xsrev_b0", ap3(C, 8 * TT, CW, [[TT, 8], [1, TT]]),
                wait=[(VE, ve_xs)])
    else:
        xs_off = 0
        ve_xs = None

    b.phase = b.phase.split(':')[0] + ':in_proj_xz_in_w_xs'
    wname = "w_in" + sfx
    wslots = [(0, None), (TT, None)]
    ve_xi = {}
    se_z = {}
    for po in range(32):
        eoff, busy = wslots[po % 2]
        wwait = [] if busy is None else [(TE, busy)]
        b.dma("gpsimd", ap3(E, eoff, EW, [[128, 8], [1, 128]]),
              AP(W[wname].tensor, po * 128,
                 [[2 * DI, 128], [128 * 2 * DI, 8], [1, 128]]),
              wait=wwait, inc=GW)
        gw_w = GW.n
        for f in range(2):
            bk = (po % 2) * 2 + f
            xw = [(GW, gw_w), (LD, ld_xs)]
            if ve_xs is not None:
                xw.append((VE, ve_xs))
            pairs = [(AP(E, eoff + pin * 128, [[EW, 128], [1, 128]]),
                      AP(C, xs_off + pin * TT + f * 512, [[CW, 128], [1, 512]]))
                     for pin in range(8)]
            mm_bank(bk, PS[bk][:, 0:512], pairs, wait=xw, inc=TE)
            te_g = TE.n
            if po < 16:
                b.vcopy(axi(po, 4 + f * 512, 512), PS[bk][:, 0:512],
                        wait=[(TE, te_g)], inc=VE)
                ve_xi[(po, f)] = VE.n
                bank_release(bk, VE)
            else:
                b.act(bsl(po - 16, f * 512, 512), PS[bk][:, 0:512], AF.Silu,
                      wait=[(TE, te_g)], inc=SE)
                se_z[(po - 16, f)] = SE.n
                bank_release(bk, SE)
        wslots[po % 2] = (eoff, TE.n)
    te_inp = TE.n
    for ch in range(NDC):
        b.dma("sync", AP(sp_silz, ch * 128 * TT, [[TT, 128], [1, TT]]),
              bsl(ch, 0, TT), wait=[(SE, se_z[(ch, 1)])], inc=ST)
    st_silz = ST.n

    b.phase = b.phase.split(':')[0] + ':conv_4_tap_causal'
    se_act = {}
    for ch in range(NDC):
        if tl == 0:
            b.memset(axi(ch, 0, 4), 0.0)
        else:
            b.vcopy(axi(ch, 0, 4), t_halo[:, ch * 4:(ch + 1) * 4])
        cw = pp["convw" + sfx]
        wx = [(VE, ve_xi[(ch, 1)]), (TE, te_inp)]
        if ch > 0:
            wx.append((SE, se_act[ch - 1]))
        b.ts(dsl(10, 0, TT), axi(ch, 1, TT), cw[:, ch * 4 + 0:ch * 4 + 1],
             wait=wx, op0=ALU.mult)
        b.ts(dsl(11, 0, TT), axi(ch, 2, TT), cw[:, ch * 4 + 1:ch * 4 + 2])
        b.ts(dsl(12, 0, TT), axi(ch, 3, TT), cw[:, ch * 4 + 2:ch * 4 + 3])
        b.ts(dsl(13, 0, TT), axi(ch, 4, TT), cw[:, ch * 4 + 3:ch * 4 + 4])
        b.tt(dsl(14, 0, TT), dsl(10, 0, TT), dsl(11, 0, TT), ALU.add)
        b.tt(dsl(15, 0, TT), dsl(12, 0, TT), dsl(13, 0, TT), ALU.add)
        b.vcopy(t_halo[:, ch * 4:(ch + 1) * 4], axi(ch, TT + 0, 4))
        b.tt(dsl(10, 0, TT), dsl(14, 0, TT), dsl(15, 0, TT), ALU.add, inc=VE)
        b.act(csl(ch, 0, TT), dsl(10, 0, TT), AF.Silu,
              bias=pp["convb" + sfx][:, ch:ch + 1], wait=[(VE, VE.n)], inc=SE)
        se_act[ch] = SE.n

    if d == "f" and tl == 0:
        tap("tap_xi_f0", ap3(C, 0, CW, [[TT, NDC], [1, TT]]),
            wait=[(SE, SE.n)])
    b.phase = b.phase.split(':')[0] + ':xproj_dt_64_b_16_c'
    b.dma("gpsimd", ap3(E, 2 * TT, EW, [[96, 16], [1, 96]]),
          AP(W["w_xp" + sfx].tensor, 0, [[96, 128], [128 * 96, 16], [1, 96]]),
          inc=GW)
    gw_xp = GW.n
    ve_xp = {}
    for f in range(2):
        pairs = [(AP(E, 2 * TT + pin * 96, [[EW, 128], [1, 64]]),
                  csl(pin, f * 512, 512)) for pin in range(NDC)]
        mm_bank(4, PS[4][0:64, 0:512], pairs,
                wait=[(GW, gw_xp), (SE, se_act[NDC - 1])], inc=TE)
        b.vcopy(bass.AP(Dm, 8 * TT + f * 512, [[DW, 64], [1, 512]]),
                PS[4][0:64, 0:512], wait=[(TE, TE.n)], inc=VE)
        bank_release(4, VE)
        pairs = [(AP(E, 2 * TT + pin * 96 + 64, [[EW, 128], [1, 16]]),
                  csl(pin, f * 512, 512)) for pin in range(NDC)]
        mm_bank(5, PS[5][0:16, 0:512], pairs, wait=[(GW, gw_xp)], inc=TE)
        b.vcopy(bass.AP(Dm, 9 * TT + f * 512, [[DW, 16], [1, 512]]),
                PS[5][0:16, 0:512], wait=[(TE, TE.n)], inc=VE)
        bank_release(5, VE)
        pairs = [(AP(E, 2 * TT + pin * 96 + 80, [[EW, 128], [1, 16]]),
                  csl(pin, f * 512, 512)) for pin in range(NDC)]
        mm_bank(5, PS[5][0:16, 0:512], pairs, wait=[(GW, gw_xp)], inc=TE)
        b.vcopy(bass.AP(Dm, 15 * TT + f * 512, [[DW, 16], [1, 512]]),
                PS[5][0:16, 0:512], wait=[(TE, TE.n)], inc=VE)
        bank_release(5, VE)
        ve_xp[f] = VE.n

    b.phase = b.phase.split(':')[0] + ':dt_proj_softplus_a'
    if tl == 0:
        b.dma("gpsimd", AP(E, OFF_WDT, [[EW, 64], [1, DI]]),
              AP(W["w_dt" + sfx].tensor, 0, [[DI, 64], [1, DI]]),
              wait=[(TE, TE.n)], inc=GW)
        S["gw_wdt_" + d] = GW.n
    gw_dt = S["gw_wdt_" + d]
    se_dl = {}
    for po in range(NDC):
        for f in range(2):
            bk = (po % 2) * 2 + f
            pairs = [(AP(E, OFF_WDT + po * 128, [[EW, 64], [1, 128]]),
                      bass.AP(Dm, 8 * TT + f * 512, [[DW, 64], [1, 512]]))]
            mm_bank(bk, PS[bk][:, 0:512], pairs,
                    wait=[(GW, gw_dt), (VE, ve_xp[f])], inc=TE)
            te_g = TE.n
            fpo = (po % 2) * 512
            b.act(Fp[:, fpo:fpo + 512], PS[bk][:, 0:512], AF.Exp,
                  bias=pp["dtb" + sfx][:, po:po + 1], wait=[(TE, te_g)])
            b.act(adl(po, f * 512, 512), Fp[:, fpo:fpo + 512], AF.Ln,
                  bias=1.0, inc=SE)
            se_dl[(po, f)] = SE.n
            bank_release(bk, SE)

    if d == "f" and tl == 0:
        tap("tap_dl_f0", ap3(A, 0, AW, [[TT, NDC], [1, TT]]), wait=[(SE, SE.n)])
        tap("tap_Brow_f0", bass.AP(Dm, 9 * TT, [[DW, 16], [1, TT]]),
            wait=[(VE, VE.n)])
        tap("tap_Crow_f0", bass.AP(Dm, 15 * TT, [[DW, 16], [1, TT]]),
            wait=[(VE, VE.n)])
    b.phase = b.phase.split(':')[0] + ':deltax_delta_xi_ac'
    ve_dx = {}
    for ch in range(NDC):
        b.tt(bsl(ch, 0, TT), adl(ch, 0, TT), csl(ch, 0, TT), ALU.mult,
             wait=[(SE, se_dl[(ch, 1)]), (ST, st_silz)], inc=VE)
        ve_dx[ch] = VE.n

    b.phase = b.phase.split(':')[0] + ':spill_xi_act_then'
    b.dma("sync", AP(sp_xi, 0, [[TT, 128], [128 * TT, NDC], [1, TT]]),
          ap3(C, 0, CW, [[TT, NDC], [1, TT]]),
          wait=[(SE, se_act[NDC - 1])], inc=ST)
    st_xi = ST.n
    if tl == 0:
        b.memset(t_st[:, :], 0.0)
    scan_ve = {}
    sa_se = {}
    gtt_gw = {}
    for s in range(NS):
        crw = ([(GT, gtt_gw[(NDC - 1, s - 1)])] if s > 0 else
               ([(GT, S["last_gtt"])] if S.get("last_gtt") else []))
        for f in range(2):
            mm_bank(6, PS[6][:, 0:512],
                    [(c_selb[0:16, s * 128:(s + 1) * 128],
                      bass.AP(Dm, 9 * TT + f * 512, [[DW, 16], [1, 512]]))],
                    wait=[(VE, ve_xp[f])], inc=TE)
            b.vcopy(dsl(6, f * 512, 512), PS[6][:, 0:512], wait=[(TE, TE.n)], inc=VE)
            bank_release(6, VE)
            mm_bank(7, PS[7][:, 0:512],
                    [(c_selb[0:16, s * 128:(s + 1) * 128],
                      bass.AP(Dm, 15 * TT + f * 512, [[DW, 16], [1, 512]]))],
                    wait=[(VE, ve_xp[f])], inc=TE)
            b.vcopy(dsl(7, f * 512, 512), PS[7][:, 0:512],
                    wait=[(TE, TE.n)] + crw, inc=VE)
            bank_release(7, VE)
        sc_ve = {}
        for i in range(NDC + 3):
            if i < NDC:
                dc = i
                pr = dc % 2
                aw = [(VE, sc_ve[(dc - 2, s)] if dc >= 2 else VE.n)]
                b.act(dsl(0 + pr, 0, TT), adl(dc, 0, TT), AF.Exp,
                      scale=pAn[d][:, dc * NS + s:dc * NS + s + 1],
                      wait=aw + [(SE, se_dl[(dc, 1)])], inc=SE)
                sa_se[(dc, s)] = SE.n
                b.tt(dsl(2 + pr, 0, TT), bsl(dc, 0, TT), dsl(6, 0, TT), ALU.mult,
                     wait=[(VE, ve_dx[dc])] if s == 0 else ())
            if 1 <= i <= NDC:
                dc = i - 1
                pr = dc % 2
                col = dc * NS + s
                if dc >= 2:
                    gww = [(GT, gtt_gw[(dc - 2, s)])]
                elif s > 0:
                    gww = [(GT, gtt_gw[(dc + NDC - 2, s - 1)])]
                else:
                    gww = [(GT, S["last_gtt"])] if S.get("last_gtt") else []
                b.scan(dsl(4 + pr, 0, TT), dsl(0 + pr, 0, TT), dsl(2 + pr, 0, TT),
                       t_st[:, col:col + 1],
                       wait=[(SE, sa_se[(dc, s)])] + gww, inc=VE)
                sc_ve[(dc, s)] = VE.n
            if 2 <= i <= NDC + 1:
                dc = i - 2
                pr = dc % 2
                col = dc * NS + s
                b.vcopy(t_st[:, col:col + 1], dsl(4 + pr, TT - 1, 1))
            if 3 <= i <= NDC + 2:
                dc = i - 3
                pr = dc % 2
                tslot = dsl(8, 0, TT) if pr == 0 else dsl(14, 0, TT)
                yw = [(GT, gtt_gw[(dc, s)])]
                if s == 0:
                    b.vcopy(csl(dc, 0, TT), tslot,
                            wait=yw + ([(ST, st_xi)] if dc == 0 else []), inc=VE)
                else:
                    b.tt(csl(dc, 0, TT), csl(dc, 0, TT), tslot, ALU.add,
                         wait=yw, inc=VE)
                scan_ve[(dc, s)] = VE.n
            if 1 <= i <= NDC:
                dc = i - 1
                pr = dc % 2
                tslot = dsl(8, 0, TT) if pr == 0 else dsl(14, 0, TT)
                gw_wait = ([(VE, scan_ve[(dc - 2, s)])] if dc >= 2 else
                           [(VE, sc_ve[(dc, s)])])
                b.gtt(tslot, dsl(4 + pr, 0, TT), dsl(7, 0, TT), ALU.mult,
                      wait=gw_wait, inc=GT)
                gtt_gw[(dc, s)] = GT.n
    S["last_gtt"] = gtt_gw[(NDC - 1, NS - 1)]
    b.phase = b.phase.split(':')[0] + ':gating_y3_y_xi_d_s'
    for cp in range(0, NDC, 2):
        ld_g = {}
        for j in (0, 1):
            ch = cp + j
            b.dma("sync", dsl(10 + j, 0, TT),
                  AP(sp_xi, ch * 128 * TT, [[TT, 128], [1, TT]]),
                  wait=[(ST, st_xi)], inc=LD)
            b.dma("sync", dsl(12 + j, 0, TT),
                  AP(sp_silz, ch * 128 * TT, [[TT, 128], [1, TT]]),
                  wait=[(ST, st_silz)], inc=LD)
            ld_g[j] = LD.n
        for j in (0, 1):
            ch = cp + j
            gtw = [(GT, S["last_gtt"])] if cp == 0 else []
            b.ts(dsl(14 + j, 0, TT), dsl(10 + j, 0, TT),
                 pp["dvec" + sfx][:, ch:ch + 1], None, ALU.mult,
                 wait=[(LD, ld_g[j]), (VE, scan_ve[(ch, NS - 1)])] + gtw)
        for j in (0, 1):
            ch = cp + j
            b.tt(dsl(10 + j, 0, TT), csl(ch, 0, TT), dsl(14 + j, 0, TT), ALU.add)
        for j in (0, 1):
            ch = cp + j
            b.tt(csl(ch, 0, TT), dsl(10 + j, 0, TT), dsl(12 + j, 0, TT),
                 ALU.mult, inc=VE)
            scan_ve[("g", ch)] = VE.n

    if d == "f" and tl == 0:
        tap("tap_y3_f0", ap3(C, 0, CW, [[TT, NDC], [1, TT]]),
            wait=[(VE, scan_ve[("g", NDC - 1)])])
    b.phase = b.phase.split(':')[0] + ':out_proj_yst_b_are'
    wname = "w_out" + sfx
    wslots = [(5632, None), (7680, None)]
    ve_yst = {}
    for po in range(8):
        eoff, busy = wslots[po % 2]
        wwait = [(TE, TE.n)] if busy is None else [(TE, busy)]
        b.dma("gpsimd", ap3(E, eoff, EW, [[128, 16], [1, 128]]),
              AP(W[wname].tensor, po * 128,
                 [[D, 128], [128 * D, 16], [1, 128]]),
              wait=wwait, inc=GW)
        gw_w = GW.n
        for f in range(2):
            bk = (po % 2) * 2 + f
            pairs = [(AP(E, eoff + pin * 128, [[EW, 128], [1, 128]]),
                      csl(pin, f * 512, 512)) for pin in range(NDC)]
            mm_bank(bk, PS[bk][:, 0:512], pairs,
                    wait=[(GW, gw_w), (VE, scan_ve[("g", NDC - 1)])], inc=TE)
            te_g = TE.n
            if d == "f":
                b.vcopy(bsl(po, f * 512, 512), PS[bk][:, 0:512],
                        wait=[(TE, te_g)], inc=VE)
            else:
                b.vcopy(AP(B, po * TT + (TT - 1 - f * 512), [[BW, 128], [-1, 512]]),
                        PS[bk][:, 0:512], wait=[(TE, te_g)], inc=VE)
            ve_yst[(po, f)] = VE.n
            bank_release(bk, VE)
        wslots[po % 2] = (eoff, TE.n)
    if d == "f":
        if tl == 0:
            tap("tap_yf_t0", ap3(B, 0, BW, [[TT, 8], [1, TT]]),
                wait=[(VE, ve_yst[(7, 1)])])
        for po in range(8):
            b.dma("sync", AP(sp_yf, tl * TT + po * 128 * T, [[T, 128], [1, TT]]),
                  bsl(po, 0, TT), wait=[(VE, ve_yst[(po, 1)])], inc=ST)
        S["st_yf_%d" % tl] = ST.n
        return

    if d == "b" and tl == 1:
        tap("tap_yb_o0", ap3(B, 0, BW, [[TT, 8], [1, TT]]),
            wait=[(VE, ve_yst[(7, 1)])])
    b.phase = b.phase.split(':')[0] + ':merge_bwd_only_s_l'
    otl = NTT - 1 - tl
    for po in range(8):
        b.dma("sync", AP(B, 8 * TT + po * TT, [[BW, 128], [1, TT]]),
              AP(sp_yf, otl * TT + po * 128 * T, [[T, 128], [1, TT]]),
              wait=[(ST, S["st_yf_%d" % otl])], inc=LD)
    ld_yfr = LD.n
    wslots = [(5632, None), (7680, None)]
    for po in range(8):
        eoff, busy = wslots[po % 2]
        wwait = [(TE, TE.n)] if busy is None else [(TE, busy)]
        b.dma("gpsimd", ap3(E, eoff, EW, [[128, 16], [1, 128]]),
              AP(W["w_mrg"].tensor, po * 128, [[D, 128], [128 * D, 16], [1, 128]]),
              wait=wwait, inc=GW)
        gw_w = GW.n
        b.dma("sync", dsl(15, 0, TT),
              AP(sp_xn, otl * TT + po * 128 * T, [[T, 128], [1, TT]]), inc=LD)
        ld_xn = LD.n
        for f in range(2):
            bk = (po % 2) * 2 + f
            pairs = [(AP(E, eoff + pin * 128, [[EW, 128], [1, 128]]),
                      AP(B, 8 * TT + pin * TT + f * 512, [[BW, 128], [1, 512]]))
                     for pin in range(8)]
            pairs += [(AP(E, eoff + (8 + pin) * 128, [[EW, 128], [1, 128]]),
                       bsl(pin, f * 512, 512)) for pin in range(8)]
            mm_bank(bk, PS[bk][:, 0:512], pairs,
                    wait=[(GW, gw_w), (LD, ld_yfr), (VE, ve_yst[(7, 1)])], inc=TE)
            te_g = TE.n
            b.ts(dsl(14, f * 512, 512), PS[bk][:, 0:512],
                 pp["mrgb"][:, po:po + 1], None, ALU.add,
                 wait=[(TE, te_g), (LD, ld_xn)])
            b.tt(dsl(14, f * 512, 512), dsl(14, f * 512, 512),
                 dsl(15, f * 512, 512), ALU.add, inc=VE)
            bank_release(bk, VE)
        b.dma("sync", AP(sp_s, otl * TT + po * 128 * T, [[T, 128], [1, TT]]),
              dsl(14, 0, TT), wait=[(VE, VE.n)], inc=ST)
    S["st_s_%d" % otl] = ST.n
    if ("tap_s_t%d" % otl) in S["taps"]:
        b.dma("sync", ap3(B, 8 * TT, BW, [[TT, 8], [1, TT]]),
              AP(sp_s, otl * TT, [[T, 128], [128 * T, 8], [1, TT]]),
              wait=[(ST, ST.n)], inc=LD)
        tap("tap_s_t%d" % otl, ap3(B, 8 * TT, BW, [[TT, 8], [1, TT]]),
            wait=[(LD, LD.n)])




def _scalar_chain_quant(b, S, t_ar_ap, cols, sym=False):
    t_sc = S["t_sc"]
    c_inv, c_lo, c_hi, c_s = cols
    r = lambda c: t_sc[0:1, c:c + 1]
    b.tt(r(c_s), t_ar_ap[0:1, 0:1], t_ar_ap[0:1, 1:2], ALU.add)
    b.ts(r(c_s), r(c_s), 1.0 / 255.0, 1e-8, ALU.mult, ALU.max)
    b.vrecip(r(c_inv), r(c_s))
    b.tt(r(c_lo), t_ar_ap[0:1, 0:1], r(c_inv), ALU.mult)
    b.ts(r(c_lo), r(c_lo), MAGIC, MAGIC, ALU.add, ALU.subtract)
    b.ts(r(c_lo), r(c_lo), -128.0, -128.0, ALU.add, ALU.max)
    b.ts(r(c_lo), r(c_lo), 127.0, None, ALU.min)
    b.ts(r(c_hi), r(c_lo), -1.0, 127.0, ALU.mult, ALU.add)
    b.ts(r(c_lo), r(c_lo), -1.0, -128.0, ALU.mult, ALU.add)


def _finalize_col_stat(b, S, part_ap, op, c_dst, wait):
    AP = bass.AP
    bounce, t_red, t_sc = _v(S, "bounce", "t_red", "t_sc")
    LD, ST, VE = _v(S, "LD", "ST", "VE")
    b.vred(S["t_wst"][:, 63:64], part_ap, op, wait=wait, inc=VE)
    b.dma("sync", AP(bounce, 3 * 128, [[1, 1], [1, 128]]),
          S["t_wst"][:, 63:64], wait=[(VE, VE.n)], inc=ST)
    b.dma("sync", t_red[0:1, :], bounce.ap()[3:4, :], wait=[(ST, ST.n)], inc=LD)
    b.vred(t_sc[0:1, c_dst:c_dst + 1], t_red[0:1, :], op, wait=[(LD, LD.n)], inc=VE)


def _rep_cols(b, S, c0, c1):
    PS, t_sone, t_sc, t_srep = _v(S, "PS", "t_sone", "t_sc", "t_srep")
    TE, VE = _v(S, "TE", "VE")
    S["mm_bank"](4, PS[4][:, 0:c1 - c0], [(t_sone[0:1, :], t_sc[0:1, c0:c1])],
                 wait=[(VE, VE.n)], inc=TE, no_ldw=True)
    b.vcopy(t_srep[:, c0:c1], PS[4][:, 0:c1 - c0], wait=[(TE, TE.n)], inc=VE)
    S["bank_release"](4, VE)


def phase_ffn(S):
    b = S["b"]; AP = bass.AP
    A, B, Dm, E, Fp = _v(S, "A", "B", "Dm", "E", "Fp")
    PS, pp, t_sc, t_srep, t_hm, t_wst = _v(S, "PS", "pp", "t_sc", "t_srep",
                                           "t_hm", "t_wst")
    W, sp_xn, sp_h, sp_wq2, sp_g = _v(S, "W", "sp_xn", "sp_h", "sp_wq2", "sp_g")
    t_ar, ar_i, ar_o = _v(S, "t_ar", "ar_i", "ar_o")
    LD, ST, GW, VE, SE, TE, CC = _v(S, "LD", "ST", "GW", "VE", "SE", "TE", "CC")
    mm_bank, bank_release = S["mm_bank"], S["bank_release"]
    AW = NDC * (TT + 4)
    b.barrier([LD, ST, GW, VE, SE, TE])

    _finalize_col_stat(b, S, S["t_wst"][:, 0:16], ALU.max, 11, [(VE, S["ve_wst"])])
    b.ts(t_sc[0:1, 11:12], t_sc[0:1, 11:12], 1.0 / 127.0, 1e-8, ALU.mult, ALU.max)
    b.vrecip(t_sc[0:1, 10:11], t_sc[0:1, 11:12])
    _finalize_col_stat(b, S, S["t_wst"][:, 16:32], ALU.max, 13, [(VE, S["ve_wst"])])
    b.ts(t_sc[0:1, 13:14], t_sc[0:1, 13:14], 1.0 / 127.0, 1e-8, ALU.mult, ALU.max)
    b.vrecip(t_sc[0:1, 12:13], t_sc[0:1, 13:14])
    b.dma("gpsimd", t_ar[0:1, :], ar_o[0].ap()[0:1, :],
          wait=[(CC, S["cc_ar1"])], inc=GW)
    b.wait("vector", GW, GW.n)
    _scalar_chain_quant(b, S, t_ar, (0, 1, 2, 16))
    b.tt(t_sc[0:1, 3:4], t_sc[0:1, 16:17], t_sc[0:1, 11:12], ALU.mult, inc=VE)
    _rep_cols(b, S, 0, 4)
    _rep_cols(b, S, 10, 14)
    b.ts(t_srep[:, 14:15], t_srep[:, 1:2], -1.0, -MAGIC, ALU.mult, ALU.add,
         inc=VE)
    ve_bias1 = VE.n

    b.dma("sync", ap3(A, 0, AW, [[T, 8], [1, T]]),
          AP(sp_xn, 0, [[T, 128], [128 * T, 8], [1, T]]), inc=LD)
    ld_q = LD.n
    sr = lambda c: t_srep[:, c:c + 1]
    ve_pq = None
    for ch in range(8):
        qs = lambda lo, n: AP(A, ch * T + lo, [[AW, 128], [1, n]])
        b.act(Fp[:, 0:2048], qs(0, 2048), AF.Relu, scale=sr(0), bias=MAGIC,
              wait=[(LD, ld_q), (VE, VE.n)] if ch == 0 else ())
        b.act(Fp[:, 2048:4096], Fp[:, 0:2048], AF.Relu, bias=t_srep[:, 14:15],
              wait=([(VE, ve_bias1)] if ch == 0 else
                    [(VE, ve_pq)]), inc=SE)
        b.ts(qs(0, 2048), Fp[:, 2048:4096], sr(1), sr(2), ALU.add, ALU.min,
             wait=[(SE, SE.n)], inc=VE)
        ve_pq = VE.n
    ve_q1 = VE.n

    wslots = [(0, None), (TT, None)]
    stage_busy = {}
    for po in range(32):
        eoff, busy = wslots[po % 2]
        fo = (po % 2) * 1024
        b.dma("gpsimd", ap3(Fp, 2048 + fo, 5120, [[128, 8], [1, 128]]),
              AP(W["w1"].tensor, po * 128, [[H, 128], [128 * H, 8], [1, 128]]),
              wait=[(VE, ve_q1)] if busy is None else [(TE, busy)], inc=GW)
        gw_w = GW.n
        wq_w = [(GW, gw_w)]
        if po == 0:
            wq_w.append((VE, VE.n))
        if busy is not None:
            wq_w.append((TE, busy))
        b.act(Fp[:, 0:1024], Fp[:, 2048 + fo:2048 + fo + 1024], AF.Relu,
              scale=sr(10), bias=MAGIC, wait=wq_w)
        b.act(Fp[:, 1024:2048], Fp[:, 0:1024], AF.Relu, bias=-MAGIC + 127.0,
              wait=([] if ve_pq is None else [(VE, ve_pq)]), inc=SE)
        b.ts(AP(E, eoff, [[NDC * TT, 128], [1, 1024]]),
             Fp[:, 1024:2048], -127.0, 127.0, ALU.add, ALU.min,
             wait=[(SE, SE.n)], inc=VE)
        ve_pq = ve_wq = VE.n
        for f in range(4):
            bk = (po % 2) * 2 + (f % 2)
            pairs = [(AP(E, eoff + pin * 128, [[NDC * TT, 128], [1, 128]]),
                      AP(A, pin * T + f * 512, [[AW, 128], [1, 512]]))
                     for pin in range(8)]
            mm_bank(bk, PS[bk][:, 0:512], pairs,
                    wait=[(VE, max(ve_wq, ve_q1))], inc=TE)
            te_g = TE.n
            sg = (f % 2)
            stg = AP(Dm, (14 + sg) * TT + (f // 2) * 512, [[NDC * TT, 128], [1, 512]])
            sw = [(TE, te_g)]
            if (po, sg, f // 2) != (0, 0, 0) and ((14 + sg), f // 2) in stage_busy:
                sw.append((ST, stage_busy[((14 + sg), f // 2)]))
            b.act(stg, PS[bk][:, 0:512], AF.Gelu, scale=sr(3),
                  bias=pp["b1c"][:, po:po + 1], wait=sw, inc=SE)
            bank_release(bk, SE)
            b.dma("sync", AP(sp_h, po * 128 * T + f * 512, [[T, 128], [1, 512]]),
                  stg, wait=[(SE, SE.n)], inc=ST)
            stage_busy[((14 + sg), f // 2)] = ST.n
            hj = po * 4 + f
            b.vred(Fp[:, 4096 + hj:4097 + hj], stg, ALU.min, wait=[(SE, SE.n)])
            b.vred(Fp[:, 4224 + hj:4225 + hj], stg, ALU.max, inc=VE)
        wslots[po % 2] = (eoff, TE.n)
    b.vred(t_hm[:, 0:1], Fp[:, 4096:4224], ALU.min)
    b.vred(t_hm[:, 1:2], Fp[:, 4224:4352], ALU.max, inc=VE)

    bounce, t_red = _v(S, "bounce", "t_red")
    b.dma("sync", AP(bounce, 0, [[1, 1], [1, 128]]), t_hm[:, 0:1],
          wait=[(VE, VE.n)], inc=ST)
    b.dma("sync", AP(bounce, 128, [[1, 1], [1, 128]]), t_hm[:, 1:2],
          wait=[(VE, VE.n)], inc=ST)
    b.dma("sync", t_red[0:1, :], bounce.ap()[0:1, :], wait=[(ST, ST.n)], inc=LD)
    b.vred(t_ar[0:1, 0:1], t_red[0:1, :], ALU.min, wait=[(LD, LD.n)], inc=VE)
    b.ts(t_ar[0:1, 0:1], t_ar[0:1, 0:1], -1.0, inc=VE)
    b.dma("sync", t_red[0:1, :], bounce.ap()[1:2, :], wait=[(ST, ST.n)], inc=LD)
    b.vred(t_ar[0:1, 1:2], t_red[0:1, :], ALU.max, wait=[(LD, LD.n)], inc=VE)
    b.dma("sync", ar_i[1].ap()[0:1, :], t_ar[0:1, :], wait=[(VE, VE.n)], inc=ST)
    b.wait("gpsimd", ST, ST.n)
    b.coll("AllReduce", ALU.max, [ar_i[1].ap().opt()], [ar_o[1].ap().opt()], inc=CC)
    b.dma("gpsimd", t_ar[0:1, :], ar_o[1].ap()[0:1, :], wait=[(CC, CC.n)], inc=GW)
    b.wait("vector", GW, GW.n)
    _scalar_chain_quant(b, S, t_ar, (4, 5, 6, 17))
    b.tt(t_sc[0:1, 7:8], t_sc[0:1, 17:18], t_sc[0:1, 13:14], ALU.mult, inc=VE)
    _rep_cols(b, S, 4, 8)
    b.ts(t_srep[:, 15:16], t_srep[:, 5:6], -1.0, -MAGIC, ALU.mult, ALU.add,
         inc=VE)
    ve_bias2 = VE.n

    for rch in range(32):
        fo = (rch % 2) * 1024
        b.dma("gpsimd", Fp[:, 2048 + fo:2048 + fo + 1024],
              AP(W["w2"].tensor, rch * 128 * D, [[D, 128], [1, D]]),
              wait=[(VE, VE.n)], inc=GW)
        b.act(Fp[:, 0:1024], Fp[:, 2048 + fo:2048 + fo + 1024], AF.Relu,
              scale=sr(12), bias=MAGIC, wait=[(GW, GW.n)])
        b.act(Fp[:, 1024:2048], Fp[:, 0:1024], AF.Relu, bias=-MAGIC + 127.0,
              wait=[(VE, ve_pq)], inc=SE)
        eoff = (rch % 2) * 1024
        b.ts(AP(E, eoff, [[NDC * TT, 128], [1, 1024]]), Fp[:, 1024:2048],
             -127.0, 127.0, ALU.add, ALU.min, wait=[(SE, SE.n)], inc=VE)
        ve_pq = VE.n
        b.dma("sync", AP(sp_wq2, rch * 128 * D, [[D, 128], [1, D]]),
              AP(E, eoff, [[NDC * TT, 128], [1, 1024]]),
              wait=[(VE, VE.n)], inc=ST)
    st_wq2 = ST.n

    for t4 in range(4):
        b.dma("sync", ap3(B, 0, NDC * TT, [[512, 32], [1, 512]]),
              AP(sp_h, t4 * 512, [[T, 128], [128 * T, 32], [1, 512]]),
              wait=[(TE, TE.n)], inc=LD)
        ld_h = LD.n
        for g8 in range(8):
            qs = lambda lo, n: AP(B, g8 * 2048 + lo, [[NDC * TT, 128], [1, n]])
            b.act(Fp[:, 0:2048], qs(0, 2048), AF.Relu, scale=sr(4), bias=MAGIC,
                  wait=[(LD, ld_h), (VE, VE.n), (GW, GW.n)] if g8 == 0 else ())
            b.act(Fp[:, 2048:4096], Fp[:, 0:2048], AF.Relu,
                  bias=t_srep[:, 15:16],
                  wait=[(VE, ve_bias2 if (t4 == 0 and g8 == 0) else ve_pq)],
                  inc=SE)
            b.ts(qs(0, 2048), Fp[:, 2048:4096], sr(5), sr(6), ALU.add, ALU.min,
                 wait=[(SE, SE.n)], inc=VE)
            ve_pq = VE.n
        ve_q2 = VE.n
        wping = [(0, None), (1024, None)]
        for pin in range(32):
            eoff, busy = wping[pin % 2]
            b.dma("sync", AP(E, eoff, [[NDC * TT, 128], [1, D]]),
                  AP(sp_wq2, pin * 128 * D, [[D, 128], [1, D]]),
                  wait=[(ST, st_wq2)] + ([] if busy is None else [(TE, busy)]),
                  inc=LD)
            ld_w = LD.n
            for po in range(8):
                mm_bank(po, PS[po][:, 0:512],
                        [(AP(E, eoff + po * 128, [[NDC * TT, 128], [1, 128]]),
                          AP(B, pin * 512, [[NDC * TT, 128], [1, 512]]))],
                        wait=[(LD, ld_w), (VE, ve_q2)],
                        start=(pin == 0), stop=(pin == 31),
                        inc=(TE if po == 7 else None))
            wping[pin % 2] = (eoff, TE.n)
        te_f2 = TE.n
        for po in range(8):
            sg = po % 2
            stg = AP(Dm, (14 + sg) * TT, [[NDC * TT, 128], [1, 512]])
            sw = [(TE, te_f2)]
            if ("f2", sg) in stage_busy:
                sw.append((ST, stage_busy[("f2", sg)]))
            b.act(stg, PS[po][:, 0:512], AF.Silu, scale=sr(7),
                  bias=pp["b2c"][:, po:po + 1], wait=sw, inc=SE)
            bank_release(po, SE)
            b.dma("sync", AP(sp_g, po * 128 * T + t4 * 512, [[T, 128], [1, 512]]),
                  stg, wait=[(SE, SE.n)], inc=ST)
            stage_busy[("f2", sg)] = ST.n
    S["st_g"] = ST.n


def phase_x1_ag(S):
    b = S["b"]; AP = bass.AP
    A, B, C = _v(S, "A", "B", "C")
    X, sp_s, sp_g, ag_i, ag_o = _v(S, "X", "sp_s", "sp_g", "ag_i", "ag_o")
    LD, ST, GW, VE, SE, TE, CC = _v(S, "LD", "ST", "GW", "VE", "SE", "TE", "CC")
    AW = NDC * (TT + 4)
    b.barrier([LD, ST, GW, VE, SE, TE])
    for tl in range(NTT):
        b.dma("gpsimd", ap3(C, 0, NDC * TT, [[TT, 8], [1, TT]]),
              AP(X.tensor, tl * TT, [[T, 128], [128 * T, 8], [1, TT]]), inc=GW)
        b.dma("sync", ap3(C, 8 * TT, NDC * TT, [[TT, 8], [1, TT]]),
              AP(sp_s, tl * TT, [[T, 128], [128 * T, 8], [1, TT]]),
              wait=[(ST, S["st_s_%d" % tl])], inc=LD)
        b.dma("sync", ap3(B, 0, NDC * TT, [[TT, 8], [1, TT]]),
              AP(sp_g, tl * TT, [[T, 128], [128 * T, 8], [1, TT]]),
              wait=[(ST, S["st_g"])], inc=LD)
        b.tt(ap3(B, 8 * TT, NDC * TT, [[TT, 8], [1, TT]]),
             ap3(C, 8 * TT, NDC * TT, [[TT, 8], [1, TT]]),
             ap3(B, 0, NDC * TT, [[TT, 8], [1, TT]]), ALU.mult,
             wait=[(LD, LD.n), (GW, GW.n)])
        b.tt(ap3(A, 0, AW, [[TT, 8], [1, TT]]),
             ap3(C, 0, NDC * TT, [[TT, 8], [1, TT]]),
             ap3(B, 8 * TT, NDC * TT, [[TT, 8], [1, TT]]), ALU.add, inc=VE)
        b.dma("sync", AP(ag_i, tl * TT, [[T, 128], [128 * T, 8], [1, TT]]),
              ap3(A, 0, AW, [[TT, 8], [1, TT]]), wait=[(VE, VE.n)], inc=ST)
        if tl == 0:
            S["tap"]("tap_g_t0", ap3(B, 0, NDC * TT, [[TT, 8], [1, TT]]),
                     wait=[(VE, VE.n)])
            S["tap"]("tap_x1_t0", ap3(A, 0, AW, [[TT, 8], [1, TT]]),
                     wait=[(VE, VE.n)])
    b.wait("gpsimd", ST, ST.n)
    b.coll("AllGather", ALU.bypass, [ag_i.ap().opt()], [ag_o.ap().opt()], inc=CC)
    S["cc_ag"] = CC.n


def phase_l1(S):
    b = S["b"]; AP = bass.AP
    A, B, C, Dm, E, Fp = _v(S, "A", "B", "C", "Dm", "E", "Fp")
    PS, pp, c_one, c_bd2 = _v(S, "PS", "pp", "c_one", "c_bd2")
    t_sc, t_srep, t_wst, t_scr = _v(S, "t_sc", "t_srep", "t_wst", "t_scr")
    t_ar = S["t_ar"]
    rep_ap, rep_bc = _v(S, "rep_ap", "rep_bc")
    OFF_M, OFF_A, OFF_C = _v(S, "OFF_M", "OFF_A", "OFF_C")
    W, ag_o, sp_g2, sp_outs, OUT = _v(S, "W", "ag_o", "sp_g2", "sp_outs", "OUT")
    ar_i, ar_o = _v(S, "ar_i", "ar_o")
    LD, ST, GW, VE, SE, TE, CC = _v(S, "LD", "ST", "GW", "VE", "SE", "TE", "CC")
    mm_bank, bank_release, tap = S["mm_bank"], S["bank_release"], S["tap"]
    AW = NDC * (TT + 4)
    XW = NDC * TT
    sp_x2 = S["sp_x2"]
    b.barrier([LD, ST, GW, VE, SE, TE])

    cc_ag = S["cc_ag"]
    def _gather(e, cc_h=CC.h, gw_h=GW.h):
        e.wait_ge(cc_h, cc_ag)
        rk = e.cc_rank([list(range(NCORES))])
        for i in e.Switch(rk, NCORES):
            for rr in range(NCORES):
                e.dma_start(AP(sp_x2, rr * 256, [[T, 128], [128 * T, 8], [1, 256]]),
                            AP(ag_o, rr * D * T + i * 256,
                               [[T, 128], [128 * T, 8], [1, 256]])).then_inc(gw_h, 16)
    b.on("gpsimd", _gather)
    GW.n += NCORES * 16
    b.dma("sync", ap3(A, 0, AW, [[2048, 8], [1, 2048]]),
          AP(sp_x2, 0, [[T, 128], [128 * T, 8], [1, T]]),
          wait=[(GW, GW.n)], inc=LD)
    ld_x2 = LD.n

    x2s = lambda ch, lo, n: AP(A, ch * 2048 + lo, [[AW, 128], [1, n]])
    xn2s = lambda ch, lo, n: AP(Dm, ch * 2048 + lo, [[XW, 128], [1, n]])
    sq = lambda ch, lo, n: AP(B, ch * 1024 + lo, [[XW, 128], [1, n]])
    for hl in range(2):
        b.act(ap3(B, 0, XW, [[1024, 8], [1, 1024]]),
              ap3(A, hl * 1024, AW, [[2048, 8], [1, 1024]]), AF.Square,
              wait=[(LD, ld_x2), (VE, VE.n)], inc=SE)
        se_sq = SE.n
        px = lambda f: [(c_one[:, 0:128], x2s(ch, hl * 1024 + f * 512, 512))
                        for ch in range(8)]
        psq = lambda f: [(c_one[:, 0:128], sq(ch, f * 512, 512))
                         for ch in range(8)]
        ln_stats(S, px, psq, True, wx=[(LD, ld_x2)], wsq=[(SE, se_sq)])
        b.wait("vector", TE, TE.n)
        b.tt(ap3(B, 8 * TT, XW, [[1024, 8], [1, 1024]]),
             ap3(A, hl * 1024, AW, [[2048, 8], [1, 1024]]),
             rep_bc(OFF_M), ALU.subtract)
        b.tt(ap3(Dm, hl * 1024, XW, [[2048, 8], [1, 1024]]),
             ap3(B, 8 * TT, XW, [[1024, 8], [1, 1024]]), rep_bc(OFF_A), ALU.mult)
        b.tt(ap3(B, 0, XW, [[1024, 8], [1, 1024]]),
             ap3(B, 8 * TT, XW, [[1024, 8], [1, 1024]]), rep_bc(OFF_C), ALU.mult)
        for ch in range(8):
            b.ts(AP(A, ch * 2048 + hl * 1024, [[AW, 128], [1, 1024]]),
                 AP(B, ch * 1024, [[XW, 128], [1, 1024]]),
                 pp["gam"][:, ch:ch + 1], pp["bet"][:, ch:ch + 1],
                 ALU.mult, ALU.add, inc=(VE if ch == 7 else None))
    ve_xf = VE.n
    tap("tap_xn2", ap3(Dm, 0, XW, [[2048, 8], [1, 2048]]), wait=[(VE, ve_xf)])

    b.vred(t_wst[:, 62:63], AP(A, 0, [[AW, 128], [1, 8 * 2048]]),
           ALU.max, absval=True, wait=[(VE, ve_xf)], inc=VE)
    bounce, t_red = _v(S, "bounce", "t_red")
    b.dma("sync", AP(bounce, 2 * 128, [[1, 1], [1, 128]]), t_wst[:, 62:63],
          wait=[(VE, VE.n)], inc=ST)
    b.dma("sync", t_red[0:1, :], bounce.ap()[2:3, :], wait=[(ST, ST.n)], inc=LD)
    b.vred(t_ar[0:1, 0:1], t_red[0:1, :], ALU.max, wait=[(LD, LD.n)])
    b.vcopy(t_ar[0:1, 1:2], t_ar[0:1, 0:1], inc=VE)
    b.dma("sync", ar_i[2].ap()[0:1, :], t_ar[0:1, :], wait=[(VE, VE.n)], inc=ST)
    b.wait("gpsimd", ST, ST.n)
    b.coll("AllReduce", ALU.max, [ar_i[2].ap().opt()], [ar_o[2].ap().opt()], inc=CC)
    b.dma("gpsimd", t_ar[0:1, :], ar_o[2].ap()[0:1, :], wait=[(CC, CC.n)], inc=GW)

    _finalize_col_stat(b, S, t_wst[:, 32:36], ALU.add, 14, [(VE, S["ve_wst"])])
    b.ts(t_sc[0:1, 14:15], t_sc[0:1, 14:15], 0.7 / (D * D), None, ALU.mult, inc=VE)
    _rep_cols(b, S, 14, 15)
    es = lambda i, lo=0, n=2048: AP(E, i * 2048 + lo, [[XW, 128], [1, n]])
    wq1 = lambda ch, lo, n: AP(E, ch * 1024 + lo, [[XW, 128], [1, n]])
    for ch in range(8):
        fo = (ch % 2) * 1024
        b.dma("gpsimd", Fp[:, 2048 + fo:2048 + fo + 1024],
              AP(W["wl1"].tensor, ch * 128 * D, [[D, 128], [1, D]]),
              wait=[(VE, VE.n)], inc=GW)
        gw_w = GW.n
        b.wait("scalar", GW, gw_w)
        b.act(es(4, 0, 1024), Fp[:, 2048 + fo:2048 + fo + 1024], AF.Abs, inc=SE)
        b.act(es(4, 1024, 1024), Fp[:, 2048 + fo:2048 + fo + 1024], AF.Sign, inc=SE)
        se_aw = SE.n
        b.ts(es(5, 0, 1024), es(4, 0, 1024), t_srep[:, 14:15], None, ALU.is_gt,
             wait=[(SE, se_aw)])
        b.vred(t_wst[:, ch:ch + 1], es(5, 0, 1024), ALU.add)
        b.tt(es(5, 1024, 1024), es(4, 0, 1024), es(5, 0, 1024), ALU.mult)
        b.vred(t_wst[:, 8 + ch:9 + ch], es(5, 1024, 1024), ALU.add)
        b.tt(wq1(ch, 0, 1024), es(4, 1024, 1024), es(5, 0, 1024), ALU.mult, inc=VE)
    ve_wq1 = VE.n
    _finalize_col_stat(b, S, t_wst[:, 0:8], ALU.add, 20, [(VE, ve_wq1)])
    _finalize_col_stat(b, S, t_wst[:, 8:16], ALU.add, 21, [(VE, ve_wq1)])
    b.ts(t_sc[0:1, 20:21], t_sc[0:1, 20:21], 1.0, None, ALU.max)
    b.vrecip(t_sc[0:1, 22:23], t_sc[0:1, 20:21])
    b.tt(t_sc[0:1, 23:24], t_sc[0:1, 21:22], t_sc[0:1, 22:23], ALU.mult)
    b.wait("vector", GW, GW.n)
    b.ts(t_sc[0:1, 18:19], t_ar[0:1, 0:1], 1.0 / 127.0, 1e-8, ALU.mult, ALU.max)
    b.vrecip(t_sc[0:1, 8:9], t_sc[0:1, 18:19])
    b.tt(t_sc[0:1, 9:10], t_sc[0:1, 18:19], t_sc[0:1, 23:24], ALU.mult, inc=VE)
    _rep_cols(b, S, 8, 10)
    sr = lambda c: t_srep[:, c:c + 1]

    tap("tap_xf", ap3(A, 0, AW, [[2048, 8], [1, 2048]]), wait=[(VE, ve_xf)])
    tap("tap_sc", S["t_sc"][0:1, :], wait=[(VE, VE.n)])
    ve_pq = None
    for ch in range(8):
        xfs = lambda lo, n: AP(A, ch * 2048 + lo, [[AW, 128], [1, n]])
        b.act(Fp[:, 0:2048], xfs(0, 2048), AF.Relu, scale=sr(8), bias=MAGIC,
              wait=[(VE, VE.n), (GW, GW.n)] if ch == 0 else ())
        b.act(Fp[:, 2048:4096], Fp[:, 0:2048], AF.Relu, bias=-MAGIC + 127.0,
              wait=([] if ve_pq is None else [(VE, ve_pq)]), inc=SE)
        b.ts(xfs(0, 2048), Fp[:, 2048:4096], -127.0, 127.0, ALU.add, ALU.min,
             wait=[(SE, SE.n)], inc=VE)
        ve_pq = VE.n
    ve_q3 = VE.n
    tap("tap_q3", ap3(A, 0, AW, [[2048, 8], [1, 2048]]), wait=[(VE, ve_q3)])
    tap("tap_wl1q", ap3(E, 0, XW, [[1024, 8], [1, 1024]]), wait=[(VE, ve_q3)])
    stage_busy = {}
    for f in range(4):
        for pin in range(8):
            for po in range(8):
                mm_bank(po, PS[po][:, 0:512],
                        [(AP(E, pin * 1024 + po * 128, [[XW, 128], [1, 128]]),
                          AP(A, pin * 2048 + f * 512, [[AW, 128], [1, 512]]))],
                        wait=[(VE, ve_q3)], start=(pin == 0), stop=(pin == 7),
                        inc=(TE if pin == 7 else None))
        te_g = TE.n
        for po in range(8):
            sg = po % 2
            stg = t_scr[:, sg * 512:(sg + 1) * 512]
            sw = [(TE, te_g)]
            if ("g2", sg) in stage_busy:
                sw.append((ST, stage_busy[("g2", sg)]))
            b.act(stg, PS[po][:, 0:512], AF.Silu, scale=sr(9), wait=sw, inc=SE)
            bank_release(po, SE)
            b.dma("sync", AP(sp_g2, po * 128 * T + f * 512, [[T, 128], [1, 512]]),
                  stg, wait=[(SE, SE.n)], inc=ST)
            stage_busy[("g2", sg)] = ST.n
    st_g2 = ST.n

    te_ffn = TE.n
    wping = [(0, None), (1024, None)]
    ve_qkv = {}
    se_beta = {}
    for po in range(25):
        eoff, busy = wping[po % 2]
        ncols = 128 if po < 24 else 16
        b.dma("gpsimd", AP(E, eoff, [[XW, 128], [128, 8], [1, ncols]]),
              AP(W["wqkv"].tensor, po * 128,
                 [[3088, 128], [128 * 3088, 8], [1, ncols]]),
              wait=([(TE, te_ffn)] if busy is None else [(TE, busy)]), inc=GW)
        gw_w = GW.n
        for f in range(4):
            bk = (po % 2) * 2 + (f % 2)
            pairs = [(AP(E, eoff + pin * 128, [[XW, 128], [1, ncols]]),
                      AP(Dm, pin * 2048 + f * 512, [[XW, 128], [1, 512]]))
                     for pin in range(8)]
            mm_bank(bk, PS[bk][0:ncols, 0:512], pairs,
                    wait=[(GW, gw_w), (VE, ve_xf)], inc=TE)
            te_g = TE.n
            if po < 8:
                b.vcopy(AP(A, po * 2048 + f * 512, [[AW, 128], [1, 512]]),
                        PS[bk][:, 0:512], wait=[(TE, te_g), (TE, te_ffn)], inc=VE)
                ve_qkv[("q", po, f)] = VE.n
                bank_release(bk, VE)
            elif po < 16:
                b.vcopy(AP(B, (po - 8) * 2048 + f * 512, [[XW, 128], [1, 512]]),
                        PS[bk][:, 0:512], wait=[(TE, te_g)], inc=VE)
                ve_qkv[("k", po - 8, f)] = VE.n
                bank_release(bk, VE)
            elif po < 24:
                b.vcopy(AP(C, (po - 16) * 2048 + f * 512, [[XW, 128], [1, 512]]),
                        PS[bk][:, 0:512], wait=[(TE, te_g)], inc=VE)
                ve_qkv[("v", po - 16, f)] = VE.n
                bank_release(bk, VE)
            else:
                b.act(AP(E, 7 * 2048 + f * 512, [[XW, 16], [1, 512]]),
                      PS[bk][0:16, 0:512], AF.Sigmoid,
                      wait=[(TE, te_g)], inc=SE)
                se_beta[f] = SE.n
                bank_release(bk, SE)
        wping[po % 2] = (eoff, TE.n)

    es2 = lambda lo, n: AP(E, 2 * 2048 + lo, [[XW, 128], [1, n]])
    es3 = lambda lo, n: AP(E, 4 * 2048 + lo, [[XW, 128], [1, n]])
    bd2cols = AP(c_bd2, 0, [[128, 128], [64, 2]])
    ve_norm = {}
    for which, Ar in (("q", A), ("k", B)):
        Wd = AW if which == "q" else XW
        for ch in range(8):
            sl = lambda lo, n: AP(Ar, ch * 2048 + lo, [[Wd, 128], [1, n]])
            wv = [(VE, ve_qkv[(which, ch, 3)])]
            b.ts(es2(0, 2048), sl(0, 2048), 0.0, None, ALU.min, wait=wv, inc=VE)
            b.wait("scalar", VE, VE.n)
            b.act(es3(0, 2048), es2(0, 2048), AF.Exp, inc=SE)
            b.stt(es2(0, 2048), sl(0, 2048), 0.0, es3(0, 2048), ALU.max, ALU.add,
                  wait=[(SE, SE.n)], inc=VE)
            ve_e = VE.n
            for f in range(4):
                mm_bank(6, PS[6][0:2, 0:512],
                        [(bd2cols, es2(f * 512, 512))],
                        wait=[(VE, ve_e)], inc=TE)
                mw = [(TE, TE.n)]
                bf = S["bank_free"]
                for bx in (4, 5):
                    if bf[bx] is not None:
                        mw.append(bf[bx]); bf[bx] = None
                b.vrecip(PS[5][0:2, 0:512], PS[6][0:2, 0:512], wait=mw)
                if which == "q":
                    b.ts(PS[4][0:2, 0:512], PS[5][0:2, 0:512], 0.125, None,
                         ALU.mult)
                    b.vcopy(AP(E, 9728, [[XW, 2], [1, 512]]), PS[4][0:2, 0:512],
                            inc=VE)
                else:
                    b.vcopy(AP(E, 9728, [[XW, 2], [1, 512]]), PS[5][0:2, 0:512],
                            inc=VE)
                bank_release(6, VE)
                bank_release(5, VE)
                bank_release(4, VE)
                mm_bank(7, PS[7][:, 0:512],
                        [(c_bd2[0:2, :], AP(E, 9728, [[XW, 2], [1, 512]]))],
                        wait=[(VE, VE.n)], inc=TE)
                b.tt(sl(f * 512, 512), es2(f * 512, 512), PS[7][:, 0:512],
                     ALU.mult, wait=[(TE, TE.n)], inc=VE)
                bank_release(7, VE)
            ve_norm[(which, ch)] = VE.n

    el = lambda i, lo, n: AP(E, i * 2048 + lo, [[XW, 128], [1, n]])
    st_outs = {}
    for ht in range(8):
        qs = lambda lo, n: AP(A, ht * 2048 + lo, [[AW, 128], [1, n]])
        ks = lambda lo, n: AP(B, ht * 2048 + lo, [[XW, 128], [1, n]])
        vs = lambda lo, n: AP(C, ht * 2048 + lo, [[XW, 128], [1, n]])
        kbc = lambda u, nrep: AP(B, ht * 2048 + u * 256,
                                 [[XW, 128], [0, nrep], [1, 256]])
        dbc = lambda u, nrep: AP(E, 0 * 2048 + u * 256,
                                 [[XW, 128], [0, nrep], [1, 256]])
        for f in range(4):
            mm_bank(5, PS[5][:, 0:512],
                    [(S["c_selh"][0:16, ht * 128:(ht + 1) * 128],
                      AP(E, 7 * 2048 + f * 512, [[XW, 16], [1, 512]]))],
                    wait=[(SE, se_beta[f]), (VE, VE.n)], inc=TE)
            b.vcopy(el(2, f * 512, 512), PS[5][:, 0:512], wait=[(TE, TE.n)], inc=VE)
            bank_release(5, VE)
        b.tt(el(1, 0, 2048), vs(0, 2048), el(2, 0, 2048), ALU.mult,
             wait=[(VE, ve_qkv[("v", ht, 3)])])
        b.vcopy(el(0, 0, 256), el(1, 0, 256))
        prev_mm = None
        for u in range(7):
            span = (7 - u) * 256
            base = (u + 1) * 256
            for c0 in range(0, span, 512):
                n = min(512, span - c0)
                pw = [(VE, ve_norm[("k", ht)])] if (u == 0 and c0 == 0) else []
                if prev_mm is not None:
                    pw.append((TE, prev_mm))
                b.tt(t_scr[:, 0:n], ks(base + c0, n), kbc(u, n // 256), ALU.mult,
                     wait=pw, inc=VE)
                mm_bank(4, PS[4][:, 0:n], [(c_bd2[:, :], t_scr[:, 0:n])],
                        wait=[(VE, VE.n)], inc=TE)
                prev_mm = TE.n
                b.tt(t_scr[:, 512:512 + n], dbc(u, n // 256), PS[4][:, 0:n],
                     ALU.mult, wait=[(TE, prev_mm)], inc=VE)
                bank_release(4, VE)
                if u == 0:
                    b.vcopy(el(3, base + c0, n), t_scr[:, 512:512 + n])
                else:
                    b.tt(el(3 + (u % 2), base + c0, n),
                         el(3 + ((u + 1) % 2), base + c0, n),
                         t_scr[:, 512:512 + n], ALU.add)
            acur = 3 + (u % 2) if u > 0 else 3
            b.tt(t_scr[:, 0:256], el(acur, base, 256), el(2, base, 256), ALU.mult)
            b.tt(el(0, base, 256), el(1, base, 256), t_scr[:, 0:256],
                 ALU.subtract, inc=VE)
        for u in range(8):
            span = (8 - u) * 256
            base = u * 256
            for c0 in range(0, span, 512):
                n = min(512, span - c0)
                pw = [(VE, ve_norm[("q", ht)])] if (u == 0 and c0 == 0) else []
                if prev_mm is not None:
                    pw.append((TE, prev_mm))
                b.tt(t_scr[:, 0:n], qs(base + c0, n), kbc(u, n // 256), ALU.mult,
                     wait=pw, inc=VE)
                mm_bank(4, PS[4][:, 0:n], [(c_bd2[:, :], t_scr[:, 0:n])],
                        wait=[(VE, VE.n)], inc=TE)
                prev_mm = TE.n
                b.tt(t_scr[:, 512:512 + n], dbc(u, n // 256), PS[4][:, 0:n],
                     ALU.mult, wait=[(TE, prev_mm)], inc=VE)
                bank_release(4, VE)
                if u == 0:
                    b.vcopy(el(5, base + c0, n), t_scr[:, 512:512 + n])
                else:
                    b.tt(el(5 + (u % 2), base + c0, n),
                         el(5 + ((u + 1) % 2), base + c0, n),
                         t_scr[:, 512:512 + n], ALU.add)
            if u > 0:
                b.vcopy(el(5 + (u % 2), 0, base), el(5 + ((u + 1) % 2), 0, base),
                        inc=(VE if u == 7 else None))
        b.dma("sync", AP(sp_outs, ht * 128 * T, [[T, 128], [1, T]]),
              el(6, 0, 2048), wait=[(VE, VE.n)], inc=ST)
        st_outs[ht] = ST.n

    if "tap_outs" in S["taps"]:
        b.dma("sync", ap3(B, 0, XW, [[2048, 8], [1, 2048]]),
              AP(sp_outs, 0, [[T, 128], [128 * T, 8], [1, T]]),
              wait=[(ST, st_outs[7])], inc=LD)
        tap("tap_outs", ap3(B, 0, XW, [[2048, 8], [1, 2048]]), wait=[(LD, LD.n)])
    wping = [(0, None), (1024, None)]
    ve_x3 = {}
    for f in range(4):
        b.dma("sync", ap3(B, 0, XW, [[512, 8], [1, 512]]),
              AP(sp_outs, f * 512, [[T, 128], [128 * T, 8], [1, 512]]),
              wait=[(ST, st_outs[7]), (TE, TE.n)], inc=LD)
        ld_o = LD.n
        for pin in range(8):
            eoff, busy = wping[pin % 2]
            b.dma("gpsimd", AP(E, eoff, [[XW, 128], [1, D]]),
              AP(W["wo"].tensor, pin * 128 * D, [[D, 128], [1, D]]),
              wait=[(TE, TE.n)] if busy is None else [(TE, busy)], inc=GW)
            gw_w = GW.n
            for po in range(8):
                mm_bank(po, PS[po][:, 0:512],
                        [(AP(E, eoff + po * 128, [[XW, 128], [1, 128]]),
                          AP(B, pin * 512, [[XW, 128], [1, 512]]))],
                        wait=[(GW, gw_w), (LD, ld_o)],
                        start=(pin == 0), stop=(pin == 7),
                        inc=(TE if po == 7 else None))
            wping[pin % 2] = (eoff, TE.n)
        te_g = TE.n
        for po in range(8):
            b.tt(AP(C, po * 2048 + f * 512, [[XW, 128], [1, 512]]),
                 PS[po][:, 0:512],
                 AP(Dm, po * 2048 + f * 512, [[XW, 128], [1, 512]]), ALU.add,
                 wait=[(TE, te_g)], inc=VE)
            bank_release(po, VE)
        ve_x3[f] = VE.n

    tap("tap_x3", ap3(C, 0, XW, [[2048, 8], [1, 2048]]),
        wait=[(VE, ve_x3[3])])
    for hl in range(2):
        b.act(ap3(B, 0, XW, [[1024, 8], [1, 1024]]),
              ap3(C, hl * 1024, XW, [[2048, 8], [1, 1024]]), AF.Square,
              wait=[(VE, ve_x3[3]), (VE, VE.n)], inc=SE)
        se_sq = SE.n
        px = lambda f: [(c_one[:, 0:128],
                         AP(C, ch * 2048 + hl * 1024 + f * 512, [[XW, 128], [1, 512]]))
                        for ch in range(8)]
        psq = lambda f: [(c_one[:, 0:128],
                          AP(B, ch * 1024 + f * 512, [[XW, 128], [1, 512]]))
                         for ch in range(8)]
        ln_stats(S, px, psq, False, wx=[(VE, ve_x3[3])], wsq=[(SE, se_sq)])
        b.wait("vector", TE, TE.n)
        cen2 = ap3(B, 8 * TT, XW, [[1024, 8], [1, 1024]])
        b.tt(cen2, ap3(C, hl * 1024, XW, [[2048, 8], [1, 1024]]),
             rep_bc(OFF_M), ALU.subtract)
        b.tt(ap3(B, 0, XW, [[1024, 8], [1, 1024]]), cen2, rep_bc(OFF_A), ALU.mult)
        for ch in range(8):
            b.ts(AP(B, 8 * TT + ch * 1024, [[XW, 128], [1, 1024]]),
                 AP(B, ch * 1024, [[XW, 128], [1, 1024]]),
                 pp["lng3"][:, ch:ch + 1], pp["lnb3"][:, ch:ch + 1],
                 ALU.mult, ALU.add, inc=(VE if ch == 7 else None))
        if hl == 0:
            b.vcopy(ap3(Dm, 8 * TT, XW, [[1024, 8], [1, 1024]]),
                    ap3(B, 8 * TT, XW, [[1024, 8], [1, 1024]]), inc=VE)
    ve_s2 = VE.n
    tap("tap_s2a", ap3(Dm, 8 * TT, XW, [[1024, 8], [1, 1024]]), wait=[(VE, ve_s2)])
    tap("tap_s2b", ap3(B, 8 * TT, XW, [[1024, 8], [1, 1024]]), wait=[(VE, ve_s2)])

    b.dma("sync", ap3(A, 0, AW, [[2048, 8], [1, 2048]]),
          AP(sp_x2, 0, [[T, 128], [128 * T, 8], [1, T]]),
          wait=[(VE, ve_s2)], inc=LD)
    b.dma("sync", ap3(C, 0, XW, [[2048, 8], [1, 2048]]),
          AP(sp_g2, 0, [[T, 128], [128 * T, 8], [1, T]]),
          wait=[(ST, st_g2), (VE, ve_s2)], inc=LD)
    ld_fin = LD.n
    tap("tap_g2", ap3(C, 0, XW, [[2048, 8], [1, 2048]]), wait=[(LD, ld_fin)])
    stage_busy2 = {}
    for ch in range(8):
        fo = (ch % 2) * 2048
        sw = [(LD, ld_fin)]
        if ("fin", ch % 2) in stage_busy2:
            sw.append((ST, stage_busy2[("fin", ch % 2)]))
        for hl in range(2):
            s2t, s2base = (Dm, 8 * TT) if hl == 0 else (B, 8 * TT)
            b.tt(t_scr[:, 0:1024],
                 AP(s2t, s2base + ch * 1024, [[XW, 128], [1, 1024]]),
                 AP(C, ch * 2048 + hl * 1024, [[XW, 128], [1, 1024]]), ALU.mult,
                 wait=sw if hl == 0 else ())
            b.tt(Fp[:, fo + hl * 1024:fo + (hl + 1) * 1024],
                 AP(A, ch * 2048 + hl * 1024, [[AW, 128], [1, 1024]]),
                 t_scr[:, 0:1024], ALU.add, inc=VE)
        b.dma("sync", AP(S["OUT"].tensor, ch * 128 * T, [[T, 128], [1, T]]),
              Fp[:, fo:fo + 2048], wait=[(VE, VE.n)], inc=ST)
        stage_busy2[("fin", ch % 2)] = ST.n


# version 09
# speedup vs baseline: 1.0832x; 1.0832x over previous
import os
import sys
import numpy as np
from contextlib import ExitStack
import concourse.bass as bass
import concourse.mybir as mybir
from concourse.bass_utils import run_bass_kernel_spmd

F32 = mybir.dt.float32
F16 = mybir.dt.float16
BF16 = mybir.dt.bfloat16
I32 = mybir.dt.int32
ALU = mybir.AluOpType
AF = mybir.ActivationFunctionType
AX = mybir.AxisListType

NCORES = 8
D = 1024
T = 2048
TT = 1024
NTT = T // TT
DI = 2048
NDC = DI // 128
NS = 16
DTR = 64
NH = 16
DH = 64
H = 4096
EPS = 1e-5
MAGIC = 3.0 * 2.0 ** 22

ENGINES = ("sync", "vector", "scalar", "tensor", "gpsimd")

SIM_DRAINS = (os.environ.get("BASS_SIM_FP64") == "1"
              or os.environ.get("KERNEL_SIM_DRAINS") == "1")

_DEBUG_TAPS = []


class Sem:
    __slots__ = ("name", "h", "n")

    def __init__(self, name, h):
        self.name = name
        self.h = h
        self.n = 0


class Build:

    def __init__(self):
        self.nc = bass.Bass("TRN2", target_bir_lowering=False)
        self.st = ExitStack()
        self.q = {e: [] for e in ENGINES}
        self.dirty = {e: ([], []) for e in ENGINES}
        self.opct = {e: 0 for e in ENGINES}
        self.phase = "init"
        self.drainct = {}
        self.dma_touch = {}
        self.sems = []
        self.ntap = 0

    def sem(self, name):
        s = Sem(name, self.st.enter_context(self.nc.semaphore(name)))
        self.sems.append(s)
        return s

    def sbuf(self, name, shape, dt=F32):
        return self.st.enter_context(self.nc.sbuf_tensor(name, shape, dt))

    def psum(self, name, shape=(128, 512)):
        return self.st.enter_context(self.nc.psum_tensor(name, list(shape), F32))

    def on(self, eng, fn):
        self.q[eng].append(fn)

    @staticmethod
    def _ivs(aps):
        out = []
        for a in aps:
            t = getattr(a, "tensor", None)
            if t is None:
                continue
            try:
                if str(a.space) not in ("MemorySpace.SBUF", "MemorySpace.PSUM"):
                    continue
            except Exception:
                pass
            dims = a.ap
            width = dims[0][0] if dims else 1
            off = a.offset % max(1, width) if width > 0 else a.offset
            ext = 1
            for st_, ct in dims[1:]:
                ext += abs(st_) * (ct - 1)
            out.append((t.name, off - (ext if any(st_ < 0 for st_, _ in dims[1:]) else 0),
                        off + ext))
        return out

    def _overlap(self, a, bset, eng=None):
        for cur in a:
            (n1, l1, h1) = cur[:3]
            for rec in bset:
                (n2, l2, h2) = rec[:3]
                if eng is not None and not SIM_DRAINS:
                    if self.opct[eng] - rec[3] > 1:
                        continue
                if n1 == n2 and l1 < h2 and l2 < h1:
                    return True
        return False

    def _pre(self, eng, reads, writes, wait, track=True):
        need_drain = False
        wait = list(wait)
        if track:
            best = {}
            for (nm, lo, hi) in self._ivs(list(reads) + list(writes)):
                for (l2, h2, sm, tg) in self.dma_touch.get(nm, ()):
                    if lo < h2 and l2 < hi:
                        if best.get(id(sm), (None, -1))[1] < tg:
                            best[id(sm)] = (sm, tg)
            for sm, tg in best.values():
                wait.append((sm, tg))
        if track:
            self.opct[eng] += 1
            ct = self.opct[eng]
            R = [iv + (ct,) for iv in self._ivs(reads)]
            Wv = [iv + (ct,) for iv in self._ivs(writes)]
            tr, tw = self.dirty[eng]
            need_drain = (self._overlap(R, tw, eng) or self._overlap(Wv, tw, eng)
                          or self._overlap(Wv, tr, eng))
            if need_drain:
                k = (self.phase, eng)
                if os.environ.get("KERNEL_DRAIN_REPORT") == "2":
                    f = sys._getframe(2)
                    k = (self.phase, eng, f.f_lineno, sys._getframe(3).f_lineno)
                self.drainct[k] = self.drainct.get(k, 0) + 1
                tr.clear()
                tw.clear()
                tr, tw = self.dirty[eng]
            tr.extend(R)
            tw.extend(Wv)
        wl = [(s.h, v) for (s, v) in wait]
        if wait:
            for s, v in wait:
                assert v <= s.n, f"wait on {s.name}>={v} but planned only {s.n}"

        def pre(e):
            for h, v in wl:
                e.wait_ge(h, v)
            if need_drain:
                e.drain()
        return pre

    def _fin(self, inst, inc):
        if inc is not None:
            inst.then_inc(inc.h, 1)

    def _bump(self, inc):
        if inc is not None:
            inc.n += 1

    def tt(self, out, a, b, op, wait=(), inc=None):
        pre = self._pre("vector", [a, b], [out], wait)
        self._bump(inc)
        def f(e):
            pre(e)
            self._fin(e.tensor_tensor(out, a, b, op), inc)
        self.on("vector", f)

    def ts(self, out, a, s1, s2=None, op0=ALU.mult, op1=None, wait=(), inc=None):
        rd = [a] + [x for x in (s1, s2) if hasattr(x, "tensor")]
        pre = self._pre("vector", rd, [out], wait)
        self._bump(inc)
        if op1 is None:
            def f(e):
                pre(e)
                self._fin(e.tensor_scalar(out, a, s1, None, op0), inc)
        else:
            def f(e):
                pre(e)
                self._fin(e.tensor_scalar(out, a, s1, s2, op0, op1), inc)
        self.on("vector", f)

    def stt(self, out, in0, scalar, in1, op0, op1, wait=(), inc=None):
        rd = [in0, in1] + ([scalar] if hasattr(scalar, "tensor") else [])
        pre = self._pre("vector", rd, [out], wait)
        self._bump(inc)
        def f(e):
            pre(e)
            self._fin(e.scalar_tensor_tensor(out, in0, scalar, in1, op0, op1), inc)
        self.on("vector", f)

    def scan(self, out, a, b, init, wait=(), inc=None):
        pre = self._pre("vector", [a, b, init], [out], wait)
        self._bump(inc)
        def f(e):
            pre(e)
            self._fin(e.tensor_tensor_scan(out, a, b, init, ALU.mult, ALU.add), inc)
        self.on("vector", f)

    def ttr(self, out, in0, in1, accum, wait=(), inc=None):
        pre = self._pre("vector", [in0, in1], [out, accum], wait)
        self._bump(inc)
        def f(e):
            pre(e)
            self._fin(e.tensor_tensor_reduce(out, in0, in1, 1.0, 0.0,
                                             ALU.mult, ALU.add, accum), inc)
        self.on("vector", f)

    def vcopy(self, out, src, wait=(), inc=None):
        pre = self._pre("vector", [src], [out], wait)
        self._bump(inc)
        def f(e):
            pre(e)
            self._fin(e.tensor_copy(out, src), inc)
        self.on("vector", f)

    def vred(self, out, src, op, absval=None, wait=(), inc=None):
        pre = self._pre("vector", [src], [out], wait)
        self._bump(inc)
        def f(e):
            pre(e)
            self._fin(e.tensor_reduce(out, src, AX.X, op, apply_absolute_value=absval), inc)
        self.on("vector", f)

    def vrecip(self, out, src, wait=(), inc=None):
        pre = self._pre("vector", [src], [out], wait)
        self._bump(inc)
        def f(e):
            pre(e)
            self._fin(e.reciprocal(out, src), inc)
        self.on("vector", f)

    def memset(self, out, val, eng="vector", wait=(), inc=None):
        pre = self._pre(eng, [], [out], wait)
        self._bump(inc)
        def f(e):
            pre(e)
            self._fin(e.memset(out, val), inc)
        self.on(eng, f)

    def act(self, out, src, func, bias=None, scale=None, wait=(), inc=None):
        rd = [src] + [x for x in (bias, scale) if hasattr(x, "tensor")]
        pre = self._pre("scalar", rd, [out], wait)
        self._bump(inc)
        kw = {}
        if bias is not None:
            kw["bias"] = bias
        if scale is not None:
            kw["scale"] = scale
        def f(e):
            pre(e)
            self._fin(e.activation(out, src, func, **kw), inc)
        self.on("scalar", f)

    def scopy(self, out, src, wait=(), inc=None):
        pre = self._pre("scalar", [src], [out], wait)
        self._bump(inc)
        def f(e):
            pre(e)
            self._fin(e.activation(out, src, AF.Copy), inc)
        self.on("scalar", f)

    def gtt(self, out, a, b, op, wait=(), inc=None):
        pre = self._pre("gpsimd", [a, b], [out], wait)
        self._bump(inc)
        def f(e):
            pre(e)
            self._fin(e.tensor_tensor(out, a, b, op), inc)
        self.on("gpsimd", f)

    def dma(self, eng, out, src, wait=(), inc=None, guard=True):
        w = list(wait)
        if guard and getattr(self, "guard_sems", None):
            have = {}
            for sm, v in w:
                have[id(sm)] = max(have.get(id(sm), 0), v)
            for sm in self.guard_sems:
                if sm.n > 0 and have.get(id(sm), -1) < sm.n:
                    w.append((sm, sm.n))
        pre = self._pre(eng, [src], [out], w, track=False)
        if inc is not None:
            inc.n += 16
            for iv in self._ivs([src, out]):
                self.dma_touch.setdefault(iv[0], []).append(
                    (iv[1], iv[2], inc, inc.n))
        def f(e):
            pre(e)
            d = e.dma_start(out, src)
            if inc is not None:
                d.then_inc(inc.h, 16)
        self.on(eng, f)

    def coll(self, kind, op, ins, outs, wait=(), inc=None):
        pre = self._pre("gpsimd", [], [], wait)
        self._bump(inc)
        def f(e):
            pre(e)
            c = e.collective_compute(kind, op, replica_groups=[list(range(NCORES))],
                                     ins=ins, outs=outs)
            if inc is not None:
                c.then_inc(inc.h, 1)
        self.on("gpsimd", f)

    def mm(self, out, pairs, wait=(), inc=None, ldw_once=False):
        rd = [p for pr in pairs for p in pr]
        pre = self._pre("tensor", rd, [out], wait)
        self._bump(inc)
        n = len(pairs)
        def f(e):
            pre(e)
            last = None
            for i, (l, r) in enumerate(pairs):
                if not (ldw_once and i > 0):
                    e.ldweights(l)
                last = e.matmul(out, l, r, start=(i == 0), stop=(i == n - 1))
            self._fin(last, inc)
        self.on("tensor", f)

    def barrier(self, sems):
        targets = [(s, s.n) for s in sems]
        for eng in ENGINES:
            for s, v in targets:
                self.wait(eng, s, v)

    def wait(self, eng, sem, val):
        assert val <= sem.n, f"{eng} wait {sem.name}>={val}, planned {sem.n}"
        def f(e):
            e.wait_ge(sem.h, val)
        self.on(eng, f)

    def finalize(self):
        blk = self.st.enter_context(self.nc.Block())
        qs = self.q

        @blk.sync
        def _(e):
            for f in qs["sync"]:
                f(e)

        @blk.vector
        def _(e):
            for f in qs["vector"]:
                f(e)

        @blk.scalar
        def _(e):
            for f in qs["scalar"]:
                f(e)

        @blk.tensor
        def _(e):
            for f in qs["tensor"]:
                f(e)

        @blk.gpsimd
        def _(e):
            for f in qs["gpsimd"]:
                f(e)


def _chunked(vec, p=128):
    n = vec.shape[0]
    return np.ascontiguousarray(vec.reshape(n // p, p).T)


def _qkvb_perm():
    perm = []
    for c0 in (0, DH, 2 * DH):
        for h in range(NH):
            for i in range(DH):
                perm.append(h * (3 * DH + 1) + c0 + i)
    for h in range(NH):
        perm.append(h * (3 * DH + 1) + 3 * DH)
    return np.array(perm)


def stage_inputs(inputs):
    f32 = np.float32
    g = {}
    g["w_in_f"] = np.ascontiguousarray(inputs["L0_attn_fwd_in_w"].T)
    g["w_in_b"] = np.ascontiguousarray(inputs["L0_attn_bwd_in_w"].T)
    g["w_xp_f"] = np.ascontiguousarray(inputs["L0_attn_fwd_xproj_w"].T)
    g["w_xp_b"] = np.ascontiguousarray(inputs["L0_attn_bwd_xproj_w"].T)
    g["w_dt_f"] = np.ascontiguousarray(inputs["L0_attn_fwd_dt_w"].T)
    g["w_dt_b"] = np.ascontiguousarray(inputs["L0_attn_bwd_dt_w"].T)
    g["w_out_f"] = np.ascontiguousarray(inputs["L0_attn_fwd_out_w"].T)
    g["w_out_b"] = np.ascontiguousarray(inputs["L0_attn_bwd_out_w"].T)
    g["w_mrg"] = np.ascontiguousarray(inputs["L0_attn_merge_w"].T)
    g["w1"] = np.ascontiguousarray(inputs["L0_ffn_W1"].T)
    g["w2"] = np.ascontiguousarray(inputs["L0_ffn_W2"].T)
    g["wqkv"] = np.ascontiguousarray(inputs["L1_attn_W_qkvb"][:, _qkvb_perm()])
    g["wo"] = np.ascontiguousarray(inputs["L1_attn_W_o"])
    g["wl1"] = np.ascontiguousarray(inputs["L1_ffn_weight"].T)
    for nm, key in [("convb_f", "L0_attn_fwd_conv_b"), ("convb_b", "L0_attn_bwd_conv_b"),
                    ("dtb_f", "L0_attn_fwd_dt_b"), ("dtb_b", "L0_attn_bwd_dt_b"),
                    ("dvec_f", "L0_attn_fwd_D"), ("dvec_b", "L0_attn_bwd_D"),
                    ("b1c", "L0_ffn_b1"), ("b2c", "L0_ffn_b2"),
                    ("lng1", "L0_attn_ln_g"), ("lnb1", "L0_attn_ln_b"),
                    ("mrgb", "L0_attn_merge_b"),
                    ("lng3", "L1_attn_ln_g"), ("lnb3", "L1_attn_ln_b"),
                    ("gam", "L1_ffn_gamma"), ("bet", "L1_ffn_beta")]:
        g[nm] = _chunked(np.asarray(inputs[key], f32))
    for nm, key in [("convw_f", "L0_attn_fwd_conv_w"), ("convw_b", "L0_attn_bwd_conv_w")]:
        w = np.asarray(inputs[key], f32)
        g[nm] = np.ascontiguousarray(w.reshape(NDC, 128, 4).transpose(1, 0, 2).reshape(128, NDC * 4))
    for nm, key in [("alog_f", "L0_attn_fwd_A_log"), ("alog_b", "L0_attn_bwd_A_log")]:
        a = np.asarray(inputs[key], f32)
        g[nm] = np.ascontiguousarray(a.reshape(NDC, 128, NS).transpose(1, 0, 2).reshape(128, NDC * NS))
    g["ones16"] = np.ones((128, 128), np.float16)
    g["cvec"] = np.tile(np.array([[EPS, 1.0, MAGIC, -MAGIC + 127.0, 0.0, 0.0, 0.0, 0.0]], np.float32), (128, 1))
    bd2 = np.zeros((128, 128), np.float16)
    bd2[:64, :64] = 1.0
    bd2[64:, 64:] = 1.0
    g["bd2"] = bd2
    selb = np.zeros((128, NS * 128), np.float16)
    for s_ in range(NS):
        selb[s_, s_ * 128:(s_ + 1) * 128] = 1.0
    g["selb"] = selb
    selh = np.zeros((128, 8 * 128), np.float16)
    for ht in range(8):
        selh[2 * ht, ht * 128:ht * 128 + 64] = 1.0
        selh[2 * ht + 1, ht * 128 + 64:(ht + 1) * 128] = 1.0
    g["selh"] = selh
    return g


def kernel(**inputs):
    g = stage_inputs(inputs)
    x = np.asarray(inputs["L0_attn_x"], np.float32)
    in_maps = []
    for c in range(NCORES):
        m = dict(g)
        m["x_dm"] = np.ascontiguousarray(x[c].T)
        in_maps.append(m)
    nc = build_program()
    res = run_bass_kernel_spmd(nc, in_maps, core_ids=list(range(NCORES)))
    out = np.empty((NCORES, T, D), np.float32)
    for c in range(NCORES):
        oc = np.asarray(res.results[c]["out_dm"])
        out_c = oc.reshape(D, NCORES, T // NCORES).transpose(1, 2, 0)
        out[:, c * (T // NCORES):(c + 1) * (T // NCORES), :] = out_c
    return out


def build_program():
    b = Build()
    nc = b.nc
    emit_all(b)
    b.finalize()
    if os.environ.get("KERNEL_DRAIN_REPORT") == "1":
        tot = 0
        for k, v in sorted(b.drainct.items(), key=lambda x: -x[1]):
            print("DRAIN %-32s %-8s %5d" % (k[0], k[1], v))
            tot += v
        print("DRAIN TOTAL", tot)
    return nc



def emit_all(b: Build):
    nc = b.nc
    AP = bass.AP

    def din(name, shape, dt=F32):
        return nc.dram_tensor(name, list(shape), dt, kind="ExternalInput").ap()

    X = din("x_dm", [D, T])
    OUT = nc.dram_tensor("out_dm", [D, T], F32, kind="ExternalOutput").ap()
    W = {}
    for nm, shape in [("w_in_f", (D, 2 * DI)), ("w_in_b", (D, 2 * DI)),
                      ("w_xp_f", (DI, 96)), ("w_xp_b", (DI, 96)),
                      ("w_dt_f", (DTR, DI)), ("w_dt_b", (DTR, DI)),
                      ("w_out_f", (DI, D)), ("w_out_b", (DI, D)),
                      ("w_mrg", (DI, D)), ("w1", (D, H)), ("w2", (H, D)),
                      ("wqkv", (D, 3088)), ("wo", (D, D)), ("wl1", (D, D))]:
        W[nm] = din(nm, shape)
    pp_d = {}
    for nm, w in [("convw_f", NDC * 4), ("convw_b", NDC * 4), ("convb_f", NDC),
                  ("convb_b", NDC), ("dtb_f", NDC), ("dtb_b", NDC),
                  ("dvec_f", NDC), ("dvec_b", NDC), ("alog_f", NDC * NS),
                  ("alog_b", NDC * NS), ("b1c", 32), ("b2c", 8), ("lng1", 8),
                  ("lnb1", 8), ("mrgb", 8), ("lng3", 8), ("lnb3", 8),
                  ("gam", 8), ("bet", 8)]:
        pp_d[nm] = din(nm, (128, w))
    ONES = din("ones16", (128, 128), F16)
    BD2 = din("bd2", (128, 128), F16)
    CVEC = din("cvec", (128, 8), F32)
    SELB = din("selb", (128, NS * 128), F16)
    SELH = din("selh", (128, 8 * 128), F16)

    dr = lambda nm, shape, dt=F16: nc.dram_tensor(nm, list(shape), dt)
    sp_xn = dr("sp_xn", [D, T]); sp_xm = dr("sp_xm", [D, T])
    sp_xi = dr("sp_xi", [DI, TT]); sp_silz = dr("sp_silz", [DI, TT])
    sp_yf = dr("sp_yf", [D, T]); sp_s = dr("sp_s", [D, T])
    sp_h = dr("sp_h", [H, T]); sp_wq2 = dr("sp_wq2", [H, D])
    sp_g = dr("sp_g", [D, T]); sp_outs = dr("sp_outs", [D, T])
    sp_g2 = dr("sp_g2", [D, T])
    sp_x2 = dr("sp_x2", [D, T])
    bounce = dr("bounce", [4, 128], F32)
    ar_i = [dr("ar%d_i" % i, [1, 2], F32) for i in range(3)]
    ar_o = [dr("ar%d_o" % i, [1, 2], F32) for i in range(3)]
    ag_i = dr("ag_i", [D * T], F16)
    ag_o = dr("ag_o", [NCORES * D * T], F16)

    A = b.sbuf("A", [128, NDC * (TT + 4)], F16)
    B = b.sbuf("B", [128, NDC * TT], F16)
    C = b.sbuf("C", [128, NDC * TT], F16)
    Dm = b.sbuf("Dm", [128, NDC * TT], F16)
    E = b.sbuf("E", [128, NDC * TT], F16)
    Fp = b.sbuf("Fp", [128, 5120], F32)
    def dslot(i, n=1):
        return Dm[:, i * TT:(i + n) * TT]
    def eslot(i, n=1):
        return E[:, i * 2048:(i + n) * 2048]
    XWc = NDC * TT
    OFF_M, OFF_A, OFF_C, OFF_R16 = 10240, 11264, 12288, 13312
    OFF_WDT = 3584
    def rep_ap(off, lo, n):
        return bass.AP(E, off + lo, [[XWc, 128], [1, n]])
    def rep_bc(off):
        return bass.AP(E, off, [[XWc, 128], [0, 8], [1, TT]])
    def r16row(j, lo, n, np_=1):
        return bass.AP(E, j * XWc + OFF_R16 + lo, [[XWc, np_], [1, n]])

    pp = {nm: b.sbuf("p_" + nm, [128, pp_d[nm].shape[1]], F32) for nm in pp_d}
    pAn = {d: b.sbuf("pAn" + d, [128, NDC * NS], F32) for d in "fb"}
    c_one = b.sbuf("c_one", [128, 128], F16)
    c_bd2 = b.sbuf("c_bd2", [128, 128], F16)
    c_selb = b.sbuf("c_selb", [128, NS * 128], F16)
    c_selh = b.sbuf("c_selh", [128, 8 * 128], F16)
    t_mm1 = b.sbuf("t_mm1", [128, 4], F32)
    t_hm = b.sbuf("t_hm", [128, 4], F32)
    t_sc = b.sbuf("t_sc", [1, 32], F32)
    t_srep = b.sbuf("t_srep", [128, 16], F32)
    t_sone = b.sbuf("t_sone", [1, 128], F32)
    t_st = b.sbuf("t_st", [128, NDC * NS], F32)
    t_halo = b.sbuf("t_halo", [128, NDC * 4], F16)
    t_ar = b.sbuf("t_ar", [1, 2], F32)
    t_cst = b.sbuf("t_cst", [128, 8], F32)
    t_wst = b.sbuf("t_wst", [128, 64], F32)
    t_scr = b.sbuf("t_scr", [128, 2304], F16)
    t_red = b.sbuf("t_red", [1, 128], F32)

    PS = [b.psum("ps%d" % i) for i in range(8)]

    LD = b.sem("LD"); ST = b.sem("ST"); GW = b.sem("GW")
    VE = b.sem("VE"); SE = b.sem("SE"); TE = b.sem("TE"); CC = b.sem("CC")
    GT = b.sem("GT")
    b.guard_sems = [LD, ST, GW, VE, SE, TE]
    bank_free = {i: None for i in range(8)}

    def mm_bank(i, out, pairs, wait=(), inc=None, ldw_once=False,
                start=None, stop=None, no_ldw=False):
        w = list(wait)
        if start is None or start:
            if bank_free[i] is not None:
                w.append(bank_free[i])
                bank_free[i] = None
        rd = [p for pr in pairs for p in pr]
        pre = b._pre("tensor", rd, [out], w)
        b._bump(inc)
        n = len(pairs)
        st_ = True if start is None else start
        sp_ = True if stop is None else stop
        def f(e):
            pre(e)
            last = None
            for j, (l, r) in enumerate(pairs):
                if not no_ldw and not (ldw_once and j > 0):
                    e.ldweights(l)
                last = e.matmul(out, l, r,
                                start=(st_ and j == 0),
                                stop=(sp_ and j == n - 1))
            if inc is not None:
                last.then_inc(inc.h, 1)
        b.on("tensor", f)

    def bank_release(i, sem):
        bank_free[i] = (sem, sem.n)

    def bfree(tile, off, n_rep, inner):
        return AP(tile.tensor if hasattr(tile, "tensor") else tile,
                  (tile.offset if hasattr(tile, "offset") else 0) + off,
                  [[_pstep(tile), 128], [0, n_rep], [1, inner]])

    def _pstep(t):
        if hasattr(t, "ap"):
            return t.ap[0][0]
        return t.shape[1]

    taps = {}
    S_taps_marker = True
    for (nm, shape, dt) in _DEBUG_TAPS:
        taps[nm] = nc.dram_tensor(nm, list(shape), dt, kind="ExternalOutput").ap()

    def tap(nm, src_ap, wait=()):
        if nm in taps:
            b.dma("sync", taps[nm], src_ap, wait=wait, inc=ST)

    b.dma("sync", c_one[:, :], ONES, inc=LD)
    b.dma("sync", c_bd2[:, :], BD2, inc=LD)
    b.dma("sync", t_cst[:, :], CVEC, inc=LD)
    b.dma("sync", c_selb[:, :], SELB, inc=LD)
    b.dma("sync", c_selh[:, :], SELH, inc=LD)
    nc.const_aps.register(F32, EPS, t_cst[:, 0:1])
    nc.const_aps.register(F32, 1.0, t_cst[:, 1:2])
    nc.const_aps.register(F32, MAGIC, t_cst[:, 2:3])
    nc.const_aps.register(F32, -MAGIC + 127.0, t_cst[:, 3:4])
    for nm in pp:
        b.dma("sync", pp[nm][:, :], pp_d[nm], inc=LD)
    consts_ld = LD.n
    b.memset(t_sone[:, :], 1.0, eng="vector", inc=VE)
    b.memset(t_st[:, :], 0.0, eng="vector", inc=VE)
    b.memset(t_sc[:, :], 0.0, eng="vector", inc=VE)
    b.wait("scalar", LD, consts_ld)
    for d in "fb":
        b.act(Fp[:, 0:NDC * NS], pp["alog_" + d][:, :], AF.Exp,
              wait=[(VE, VE.n)], inc=SE)
        b.ts(pAn[d][:, :], Fp[:, 0:NDC * NS], -1.0, wait=[(SE, SE.n)], inc=VE)

    S = dict(locals())
    S["b"] = b
    b.phase = "ln1"; phase_ln1(S)
    b.phase = "wstats"; phase_wstats(S)
    for d_ in "fb":
        for tl_ in range(NTT):
            b.phase = "mamba"; phase_mamba(S, d_, tl_)
    phase_merge_check = None
    b.phase = "ffn"; phase_ffn(S)
    b.phase = "x1ag"; phase_x1_ag(S)
    b.phase = "l1"; phase_l1(S)

    b.wait("sync", ST, ST.n)
    b.wait("sync", LD, LD.n)
    b.wait("gpsimd", GW, GW.n)
    b.wait("gpsimd", GT, GT.n)
    b.wait("vector", VE, VE.n)
    b.wait("scalar", SE, SE.n)
    b.wait("tensor", TE, TE.n)
    b.wait("gpsimd", CC, CC.n)


def _v(S, *names):
    return [S[n] for n in names]


def ap3(th, off, pstep, dims):
    return bass.AP(th, off, [[pstep, 128]] + dims)


def ln_stats(S, pairs_x, pairs_sq, with_rc, wx=(), wsq=()):
    b = S["b"]
    PS = S["PS"]; VE = S["VE"]; SE = S["SE"]; TE = S["TE"]
    Fp = S["Fp"]
    mm_bank, bank_release = S["mm_bank"], S["bank_release"]
    rep_ap = S["rep_ap"]
    OFF_M, OFF_A, OFF_C = S["OFF_M"], S["OFF_A"], S["OFF_C"]
    for f in range(2):
        sb_, qb, b2 = 6 + f, 4 + f, 0 + f
        sA = Fp[:, 0:512]
        sB = Fp[:, 512:1024]
        mm_bank(sb_, PS[sb_][:, 0:512], pairs_x(f), wait=list(wx), inc=TE,
                ldw_once=True)
        mm_bank(qb, PS[qb][:, 0:512], pairs_sq(f), wait=list(wsq), inc=TE,
                ldw_once=True)
        te_s = TE.n
        mmw = [(TE, te_s)]
        bf = S["bank_free"]
        if bf[b2] is not None:
            mmw.append(bf[b2]); bf[b2] = None
        b.ts(sA, PS[sb_][:, 0:512], 1.0 / D, wait=mmw)
        b.vcopy(rep_ap(OFF_M, f * 512, 512), sA)
        b.tt(PS[b2][:, 0:512], sA, sA, ALU.mult)
        b.ts(sB, PS[qb][:, 0:512], 1.0 / D)
        b.tt(PS[qb][:, 0:512], sB, PS[b2][:, 0:512], ALU.subtract, inc=VE)
        b.act(PS[b2][:, 0:512], PS[qb][:, 0:512], AF.Sqrt, bias=EPS,
              wait=[(VE, VE.n)], inc=SE)
        b.vrecip(sA, PS[b2][:, 0:512], wait=[(SE, SE.n)])
        b.vcopy(rep_ap(OFF_A, f * 512, 512), sA, inc=VE)
        if with_rc:
            b.tt(sB, sA, sA, ALU.mult)
            b.tt(PS[b2][:, 0:512], PS[qb][:, 0:512], sB, ALU.mult, inc=VE)
            b.act(PS[qb][:, 0:512], PS[b2][:, 0:512], AF.Sqrt, bias=EPS,
                  wait=[(VE, VE.n)], inc=SE)
            b.vrecip(sB, PS[qb][:, 0:512], wait=[(SE, SE.n)])
            b.tt(PS[b2][:, 0:512], sA, sB, ALU.mult)
            b.vcopy(rep_ap(OFF_C, f * 512, 512), PS[b2][:, 0:512], inc=VE)
        for i in (sb_, qb, b2):
            bank_release(i, VE)


def phase_ln1(S):
    b = S["b"]; AP = bass.AP
    A, B, C, Dm, Fp = _v(S, "A", "B", "C", "Dm", "Fp")
    X, sp_xn, sp_xm = _v(S, "X", "sp_xn", "sp_xm")
    PS, c_one, E = _v(S, "PS", "c_one", "E")
    t_mm1 = S["t_mm1"]
    rep_ap, rep_bc = _v(S, "rep_ap", "rep_bc")
    OFF_M, OFF_A, OFF_C = _v(S, "OFF_M", "OFF_A", "OFF_C")
    pp, t_ar, t_red, bounce = _v(S, "pp", "t_ar", "t_red", "bounce")
    LD, ST, GW, VE, SE, TE, CC = _v(S, "LD", "ST", "GW", "VE", "SE", "TE", "CC")
    mm_bank, bank_release = S["mm_bank"], S["bank_release"]
    ar_i, ar_o = S["ar_i"], S["ar_o"]
    tap = S["tap"]
    AW = NDC * (TT + 4)

    for tl in range(NTT):
        x16 = ap3(A, 0, AW, [[TT, 8], [1, TT]])
        sq = ap3(A, 8 * TT, AW, [[TT, 8], [1, TT]])
        cen = ap3(B, 0, NDC * TT, [[TT, 8], [1, TT]])
        xnb = ap3(B, 8 * TT, NDC * TT, [[TT, 8], [1, TT]])
        xm0 = ap3(C, 0, NDC * TT, [[TT, 8], [1, TT]])
        xmb = ap3(C, 8 * TT, NDC * TT, [[TT, 8], [1, TT]])
        b.dma("gpsimd", x16, AP(X.tensor, tl * TT, [[T, 128], [128 * T, 8], [1, TT]]),
              inc=GW)
        gw_x = GW.n
        b.wait("scalar", GW, gw_x)
        b.act(sq, x16, AF.Square, inc=SE)
        se_sq = SE.n
        px = lambda f: [(c_one[:, 0:128],
                         AP(A, ch * TT + f * 512, [[AW, 128], [1, 512]]))
                        for ch in range(8)]
        psq = lambda f: [(c_one[:, 0:128],
                          AP(A, 8 * TT + ch * TT + f * 512, [[AW, 128], [1, 512]]))
                         for ch in range(8)]
        ln_stats(S, px, psq, True,
                 wx=[(GW, gw_x), (LD, S["consts_ld"])], wsq=[(SE, se_sq)])
        b.wait("vector", TE, TE.n)
        b.tt(cen, x16, rep_bc(OFF_M), ALU.subtract)
        b.tt(xnb, cen, rep_bc(OFF_A), ALU.mult, inc=VE)
        b.dma("sync", AP(sp_xn, tl * TT, [[T, 128], [128 * T, 8], [1, TT]]),
              xnb, wait=[(VE, VE.n)], inc=ST)
        xnb_flat = AP(B, 8 * TT, [[NDC * TT, 128], [1, 8 * TT]])
        b.vred(t_mm1[:, 2 * tl:2 * tl + 1], xnb_flat, ALU.min)
        b.vred(t_mm1[:, 2 * tl + 1:2 * tl + 2], xnb_flat, ALU.max)
        b.tt(xm0, cen, rep_bc(OFF_C), ALU.mult)
        for ch in range(8):
            b.ts(AP(C, 8 * TT + ch * TT, [[NDC * TT, 128], [1, TT]]),
                 AP(C, ch * TT, [[NDC * TT, 128], [1, TT]]),
                 pp["lng1"][:, ch:ch + 1], pp["lnb1"][:, ch:ch + 1],
                 ALU.mult, ALU.add, inc=(VE if ch == 7 else None))
        b.dma("sync", AP(sp_xm, tl * TT, [[T, 128], [128 * T, 8], [1, TT]]),
              xmb, wait=[(VE, VE.n)], inc=ST)
        tap("tap_xn_t%d" % tl, xnb, wait=[(VE, VE.n)])
        tap("tap_xm_t%d" % tl, xmb, wait=[(VE, VE.n)])
    b.tt(t_mm1[:, 0:1], t_mm1[:, 0:1], t_mm1[:, 2:3], ALU.min)
    b.tt(t_mm1[:, 1:2], t_mm1[:, 1:2], t_mm1[:, 3:4], ALU.max, inc=VE)
    b.dma("sync", AP(bounce, 0, [[1, 1], [1, 128]]), t_mm1[:, 0:1],
          wait=[(VE, VE.n)], inc=ST)
    b.dma("sync", AP(bounce, 128, [[1, 1], [1, 128]]), t_mm1[:, 1:2],
          wait=[(VE, VE.n)], inc=ST)
    b.dma("sync", t_red[0:1, :], bounce.ap()[0:1, :], wait=[(ST, ST.n)], inc=LD)
    b.vred(t_ar[0:1, 0:1], t_red[0:1, :], ALU.min, wait=[(LD, LD.n)], inc=VE)
    b.ts(t_ar[0:1, 0:1], t_ar[0:1, 0:1], -1.0, inc=VE)
    b.dma("sync", t_red[0:1, :], bounce.ap()[1:2, :], wait=[(ST, ST.n)], inc=LD)
    b.vred(t_ar[0:1, 1:2], t_red[0:1, :], ALU.max, wait=[(LD, LD.n)], inc=VE)
    b.dma("sync", ar_i[0].ap()[0:1, :], t_ar[0:1, :], wait=[(VE, VE.n)], inc=ST)
    st_ar = ST.n
    b.wait("gpsimd", ST, st_ar)
    b.coll("AllReduce", ALU.max, [ar_i[0].ap().opt()], [ar_o[0].ap().opt()], inc=CC)
    S["cc_ar1"] = CC.n


def phase_wstats(S):
    b = S["b"]; AP = bass.AP
    Fp, W = _v(S, "Fp", "W")
    GW, VE, SE, ST = _v(S, "GW", "VE", "SE", "ST")
    t_wst = S["t_wst"]
    slots = []
    k = 0
    ve_slot = {}
    kq = []
    for nm, rows, cols, red in [("w1", D, H, ALU.max), ("w2", H, D, ALU.max),
                                ("wl1", D, D, ALU.add)]:
        ctile = min(cols, 1024)
        per_row = cols // ctile
        for i in range(rows // 128):
            for j in range(per_row):
                kq.append((nm, cols, ctile, i, j, red))
    for q, (nm, cols, ctile, i, j, red) in enumerate(kq):
        sl = q % 4
        buf = Fp[:, 1024 + sl * 1024:1024 + sl * 1024 + ctile]
        wv = [(VE, ve_slot[sl])] if sl in ve_slot else []
        b.dma("gpsimd" if sl % 2 == 0 else "sync", buf,
              AP(W[nm].tensor, i * 128 * cols + j * ctile,
                 [[cols, 128], [1, ctile]]),
              wait=wv, inc=(GW if sl % 2 == 0 else ST))
        gw = (GW if sl % 2 == 0 else ST).n
        if q % 2 == 0:
            b.vred(t_wst[:, k:k + 1], buf, red, absval=True,
                   wait=[(GW if sl % 2 == 0 else ST, gw)], inc=VE)
        else:
            b.vred(t_wst[:, 60:61], buf, red, absval=True,
                   wait=[(GW if sl % 2 == 0 else ST, gw)])
            b.tt(t_wst[:, k:k + 1], t_wst[:, k:k + 1], t_wst[:, 60:61],
                 red, inc=VE)
            k += 1
        ve_slot[sl] = VE.n
    for nm in ("w1", "w2", "wl1"):
        slots.append((nm, k))
    S["wst_slots"] = slots
    S["ve_wst"] = VE.n


def phase_mamba(S, d, tl):
    b = S["b"]; AP = bass.AP
    A, B, C, Dm, E, Fp = _v(S, "A", "B", "C", "Dm", "E", "Fp")
    PS, c_one, c_selb, pp, pAn = _v(S, "PS", "c_one", "c_selb", "pp", "pAn")
    t_st, t_halo = _v(S, "t_st", "t_halo")
    OFF_WDT = S["OFF_WDT"]
    W, sp_xm, sp_xi, sp_silz, sp_yf, sp_s = _v(S, "W", "sp_xm", "sp_xi",
                                               "sp_silz", "sp_yf", "sp_s")
    sp_xn = S["sp_xn"]
    LD, ST, GW, VE, SE, TE = _v(S, "LD", "ST", "GW", "VE", "SE", "TE")
    GT = S["GT"]
    mm_bank, bank_release, tap = S["mm_bank"], S["bank_release"], S["tap"]
    AW = NDC * (TT + 4)
    BW = CW = DW = EW = NDC * TT
    sfx = "_" + d
    b.barrier([LD, ST, GW, VE, SE, TE])
    axi = lambda ch, lo, n: AP(A, ch * (TT + 4) + lo, [[AW, 128], [1, n]])
    adl = lambda ch, lo, n: AP(A, ch * TT + lo, [[AW, 128], [1, n]])
    bsl = lambda ch, lo, n: AP(B, ch * TT + lo, [[BW, 128], [1, n]])
    csl = lambda ch, lo, n: AP(C, ch * TT + lo, [[CW, 128], [1, n]])
    dsl = lambda i, lo, n: AP(Dm, i * TT + lo, [[DW, 128], [1, n]])
    esl = lambda i, lo, n: AP(E, i * TT + lo, [[EW, 128], [1, n]])

    b.phase = b.phase.split(':')[0] + ':load_mamba_input_t'
    src_tl = tl if d == "f" else (NTT - 1 - tl)
    xs_raw = ap3(C, 0, CW, [[TT, 8], [1, TT]])
    b.dma("sync", xs_raw,
          AP(sp_xm, src_tl * TT, [[T, 128], [128 * T, 8], [1, TT]]), inc=LD)
    ld_xs = LD.n
    if d == "b":
        b.vcopy(ap3(C, 8 * TT, CW, [[TT, 8], [1, TT]]),
                AP(C, TT - 1, [[CW, 128], [TT, 8], [-1, TT]]),
                wait=[(LD, ld_xs)], inc=VE)
        xs_off = 8 * TT
        ve_xs = VE.n
        if tl == 0:
            tap("tap_xsrev_b0", ap3(C, 8 * TT, CW, [[TT, 8], [1, TT]]),
                wait=[(VE, ve_xs)])
    else:
        xs_off = 0
        ve_xs = None

    b.phase = b.phase.split(':')[0] + ':in_proj_xz_in_w_xs'
    wname = "w_in" + sfx
    wslots = [(0, None), (TT, None)]
    ve_xi = {}
    se_z = {}
    for po in range(32):
        eoff, busy = wslots[po % 2]
        wwait = [] if busy is None else [(TE, busy)]
        b.dma("gpsimd", ap3(E, eoff, EW, [[128, 8], [1, 128]]),
              AP(W[wname].tensor, po * 128,
                 [[2 * DI, 128], [128 * 2 * DI, 8], [1, 128]]),
              wait=wwait, inc=GW)
        gw_w = GW.n
        for f in range(2):
            bk = (po % 2) * 2 + f
            xw = [(GW, gw_w), (LD, ld_xs)]
            if ve_xs is not None:
                xw.append((VE, ve_xs))
            pairs = [(AP(E, eoff + pin * 128, [[EW, 128], [1, 128]]),
                      AP(C, xs_off + pin * TT + f * 512, [[CW, 128], [1, 512]]))
                     for pin in range(8)]
            mm_bank(bk, PS[bk][:, 0:512], pairs, wait=xw, inc=TE)
            te_g = TE.n
            if po < 16:
                b.vcopy(axi(po, 4 + f * 512, 512), PS[bk][:, 0:512],
                        wait=[(TE, te_g)], inc=VE)
                ve_xi[(po, f)] = VE.n
                bank_release(bk, VE)
            else:
                b.act(bsl(po - 16, f * 512, 512), PS[bk][:, 0:512], AF.Silu,
                      wait=[(TE, te_g)], inc=SE)
                se_z[(po - 16, f)] = SE.n
                bank_release(bk, SE)
        wslots[po % 2] = (eoff, TE.n)
    te_inp = TE.n
    for ch in range(NDC):
        b.dma("sync", AP(sp_silz, ch * 128 * TT, [[TT, 128], [1, TT]]),
              bsl(ch, 0, TT), wait=[(SE, se_z[(ch, 1)])], inc=ST)
    st_silz = ST.n

    b.phase = b.phase.split(':')[0] + ':conv_4_tap_causal'
    se_act = {}
    for ch in range(NDC):
        if tl == 0:
            b.memset(axi(ch, 0, 4), 0.0)
        else:
            b.vcopy(axi(ch, 0, 4), t_halo[:, ch * 4:(ch + 1) * 4])
        cw = pp["convw" + sfx]
        wx = [(VE, ve_xi[(ch, 1)]), (TE, te_inp)]
        if ch > 0:
            wx.append((SE, se_act[ch - 1]))
        b.ts(dsl(10, 0, TT), axi(ch, 1, TT), cw[:, ch * 4 + 0:ch * 4 + 1],
             wait=wx, op0=ALU.mult)
        b.ts(dsl(11, 0, TT), axi(ch, 2, TT), cw[:, ch * 4 + 1:ch * 4 + 2])
        b.ts(dsl(12, 0, TT), axi(ch, 3, TT), cw[:, ch * 4 + 2:ch * 4 + 3])
        b.ts(dsl(13, 0, TT), axi(ch, 4, TT), cw[:, ch * 4 + 3:ch * 4 + 4])
        b.tt(dsl(14, 0, TT), dsl(10, 0, TT), dsl(11, 0, TT), ALU.add)
        b.tt(dsl(15, 0, TT), dsl(12, 0, TT), dsl(13, 0, TT), ALU.add)
        b.vcopy(t_halo[:, ch * 4:(ch + 1) * 4], axi(ch, TT + 0, 4))
        b.tt(dsl(10, 0, TT), dsl(14, 0, TT), dsl(15, 0, TT), ALU.add, inc=VE)
        b.act(csl(ch, 0, TT), dsl(10, 0, TT), AF.Silu,
              bias=pp["convb" + sfx][:, ch:ch + 1], wait=[(VE, VE.n)], inc=SE)
        se_act[ch] = SE.n

    if d == "f" and tl == 0:
        tap("tap_xi_f0", ap3(C, 0, CW, [[TT, NDC], [1, TT]]),
            wait=[(SE, SE.n)])
    b.phase = b.phase.split(':')[0] + ':xproj_dt_64_b_16_c'
    b.dma("gpsimd", ap3(E, 2 * TT, EW, [[96, 16], [1, 96]]),
          AP(W["w_xp" + sfx].tensor, 0, [[96, 128], [128 * 96, 16], [1, 96]]),
          inc=GW)
    gw_xp = GW.n
    ve_xp = {}
    for f in range(2):
        pairs = [(AP(E, 2 * TT + pin * 96, [[EW, 128], [1, 64]]),
                  csl(pin, f * 512, 512)) for pin in range(NDC)]
        mm_bank(4, PS[4][0:64, 0:512], pairs,
                wait=[(GW, gw_xp), (SE, se_act[NDC - 1])], inc=TE)
        b.vcopy(bass.AP(Dm, 8 * TT + f * 512, [[DW, 64], [1, 512]]),
                PS[4][0:64, 0:512], wait=[(TE, TE.n)], inc=VE)
        bank_release(4, VE)
        pairs = [(AP(E, 2 * TT + pin * 96 + 64, [[EW, 128], [1, 16]]),
                  csl(pin, f * 512, 512)) for pin in range(NDC)]
        mm_bank(5, PS[5][0:16, 0:512], pairs, wait=[(GW, gw_xp)], inc=TE)
        b.vcopy(bass.AP(Dm, 9 * TT + f * 512, [[DW, 16], [1, 512]]),
                PS[5][0:16, 0:512], wait=[(TE, TE.n)], inc=VE)
        bank_release(5, VE)
        pairs = [(AP(E, 2 * TT + pin * 96 + 80, [[EW, 128], [1, 16]]),
                  csl(pin, f * 512, 512)) for pin in range(NDC)]
        mm_bank(5, PS[5][0:16, 0:512], pairs, wait=[(GW, gw_xp)], inc=TE)
        b.vcopy(bass.AP(Dm, 15 * TT + f * 512, [[DW, 16], [1, 512]]),
                PS[5][0:16, 0:512], wait=[(TE, TE.n)], inc=VE)
        bank_release(5, VE)
        ve_xp[f] = VE.n

    b.phase = b.phase.split(':')[0] + ':dt_proj_softplus_a'
    if tl == 0:
        b.dma("gpsimd", AP(E, OFF_WDT, [[EW, 64], [1, DI]]),
              AP(W["w_dt" + sfx].tensor, 0, [[DI, 64], [1, DI]]),
              wait=[(TE, TE.n)], inc=GW)
        S["gw_wdt_" + d] = GW.n
    gw_dt = S["gw_wdt_" + d]
    se_dl = {}
    for po in range(NDC):
        for f in range(2):
            bk = (po % 2) * 2 + f
            pairs = [(AP(E, OFF_WDT + po * 128, [[EW, 64], [1, 128]]),
                      bass.AP(Dm, 8 * TT + f * 512, [[DW, 64], [1, 512]]))]
            mm_bank(bk, PS[bk][:, 0:512], pairs,
                    wait=[(GW, gw_dt), (VE, ve_xp[f])], inc=TE)
            te_g = TE.n
            fpo = (po % 2) * 512
            b.act(Fp[:, fpo:fpo + 512], PS[bk][:, 0:512], AF.Exp,
                  bias=pp["dtb" + sfx][:, po:po + 1], wait=[(TE, te_g)])
            b.act(adl(po, f * 512, 512), Fp[:, fpo:fpo + 512], AF.Ln,
                  bias=1.0, inc=SE)
            se_dl[(po, f)] = SE.n
            bank_release(bk, SE)

    if d == "f" and tl == 0:
        tap("tap_dl_f0", ap3(A, 0, AW, [[TT, NDC], [1, TT]]), wait=[(SE, SE.n)])
        tap("tap_Brow_f0", bass.AP(Dm, 9 * TT, [[DW, 16], [1, TT]]),
            wait=[(VE, VE.n)])
        tap("tap_Crow_f0", bass.AP(Dm, 15 * TT, [[DW, 16], [1, TT]]),
            wait=[(VE, VE.n)])
    b.phase = b.phase.split(':')[0] + ':deltax_delta_xi_ac'
    ve_dx = {}
    for ch in range(NDC):
        b.tt(bsl(ch, 0, TT), adl(ch, 0, TT), csl(ch, 0, TT), ALU.mult,
             wait=[(SE, se_dl[(ch, 1)]), (ST, st_silz)], inc=VE)
        ve_dx[ch] = VE.n

    b.phase = b.phase.split(':')[0] + ':spill_xi_act_then'
    b.dma("sync", AP(sp_xi, 0, [[TT, 128], [128 * TT, NDC], [1, TT]]),
          ap3(C, 0, CW, [[TT, NDC], [1, TT]]),
          wait=[(SE, se_act[NDC - 1])], inc=ST)
    st_xi = ST.n
    if tl == 0:
        b.memset(t_st[:, :], 0.0)
    scan_ve = {}
    sa_se = {}
    for s in range(NS):
        for f in range(2):
            mm_bank(6, PS[6][:, 0:512],
                    [(c_selb[0:16, s * 128:(s + 1) * 128],
                      bass.AP(Dm, 9 * TT + f * 512, [[DW, 16], [1, 512]]))],
                    wait=[(VE, ve_xp[f])], inc=TE)
            b.vcopy(dsl(6, f * 512, 512), PS[6][:, 0:512], wait=[(TE, TE.n)], inc=VE)
            bank_release(6, VE)
            mm_bank(7, PS[7][:, 0:512],
                    [(c_selb[0:16, s * 128:(s + 1) * 128],
                      bass.AP(Dm, 15 * TT + f * 512, [[DW, 16], [1, 512]]))],
                    wait=[(VE, ve_xp[f])], inc=TE)
            b.vcopy(dsl(7, f * 512, 512), PS[7][:, 0:512],
                    wait=[(TE, TE.n)], inc=VE)
            bank_release(7, VE)
        sc_ve = {}
        for i in range(NDC + 3):
            if i < NDC:
                dc = i
                pr = dc % 2
                aw = [(VE, sc_ve[(dc - 2, s)] if dc >= 2 else VE.n)]
                b.act(dsl(0 + pr, 0, TT), adl(dc, 0, TT), AF.Exp,
                      scale=pAn[d][:, dc * NS + s:dc * NS + s + 1],
                      wait=aw + [(SE, se_dl[(dc, 1)])], inc=SE)
                sa_se[(dc, s)] = SE.n
                b.tt(dsl(2 + pr, 0, TT), bsl(dc, 0, TT), dsl(6, 0, TT), ALU.mult,
                     wait=[(VE, ve_dx[dc])] if s == 0 else ())
            if 1 <= i <= NDC:
                dc = i - 1
                pr = dc % 2
                col = dc * NS + s
                b.scan(dsl(4 + pr, 0, TT), dsl(0 + pr, 0, TT), dsl(2 + pr, 0, TT),
                       t_st[:, col:col + 1], wait=[(SE, sa_se[(dc, s)])], inc=VE)
                sc_ve[(dc, s)] = VE.n
            if 2 <= i <= NDC + 1:
                dc = i - 2
                pr = dc % 2
                col = dc * NS + s
                tslot = dsl(8, 0, TT) if pr == 0 else dsl(14, 0, TT)
                b.tt(tslot, dsl(4 + pr, 0, TT), dsl(7, 0, TT), ALU.mult)
                b.vcopy(t_st[:, col:col + 1], dsl(4 + pr, TT - 1, 1))
            if 3 <= i <= NDC + 2:
                dc = i - 3
                pr = dc % 2
                tslot = dsl(8, 0, TT) if pr == 0 else dsl(14, 0, TT)
                if s == 0:
                    b.vcopy(csl(dc, 0, TT), tslot,
                            wait=[(ST, st_xi)] if dc == 0 else (), inc=VE)
                else:
                    b.tt(csl(dc, 0, TT), csl(dc, 0, TT), tslot, ALU.add, inc=VE)
                scan_ve[(dc, s)] = VE.n
    b.phase = b.phase.split(':')[0] + ':gating_y3_y_xi_d_s'
    for cp in range(0, NDC, 2):
        ld_g = {}
        for j in (0, 1):
            ch = cp + j
            b.dma("sync", dsl(10 + j, 0, TT),
                  AP(sp_xi, ch * 128 * TT, [[TT, 128], [1, TT]]),
                  wait=[(ST, st_xi)], inc=LD)
            b.dma("sync", dsl(12 + j, 0, TT),
                  AP(sp_silz, ch * 128 * TT, [[TT, 128], [1, TT]]),
                  wait=[(ST, st_silz)], inc=LD)
            ld_g[j] = LD.n
        for j in (0, 1):
            ch = cp + j
            b.ts(dsl(14 + j, 0, TT), dsl(10 + j, 0, TT),
                 pp["dvec" + sfx][:, ch:ch + 1], None, ALU.mult,
                 wait=[(LD, ld_g[j]), (VE, scan_ve[(ch, NS - 1)])])
        for j in (0, 1):
            ch = cp + j
            b.tt(dsl(10 + j, 0, TT), csl(ch, 0, TT), dsl(14 + j, 0, TT), ALU.add)
        for j in (0, 1):
            ch = cp + j
            b.tt(csl(ch, 0, TT), dsl(10 + j, 0, TT), dsl(12 + j, 0, TT),
                 ALU.mult, inc=VE)
            scan_ve[("g", ch)] = VE.n

    if d == "f" and tl == 0:
        tap("tap_y3_f0", ap3(C, 0, CW, [[TT, NDC], [1, TT]]),
            wait=[(VE, scan_ve[("g", NDC - 1)])])
    b.phase = b.phase.split(':')[0] + ':out_proj_yst_b_are'
    wname = "w_out" + sfx
    wslots = [(5632, None), (7680, None)]
    ve_yst = {}
    for po in range(8):
        eoff, busy = wslots[po % 2]
        wwait = [(TE, TE.n)] if busy is None else [(TE, busy)]
        b.dma("gpsimd", ap3(E, eoff, EW, [[128, 16], [1, 128]]),
              AP(W[wname].tensor, po * 128,
                 [[D, 128], [128 * D, 16], [1, 128]]),
              wait=wwait, inc=GW)
        gw_w = GW.n
        for f in range(2):
            bk = (po % 2) * 2 + f
            pairs = [(AP(E, eoff + pin * 128, [[EW, 128], [1, 128]]),
                      csl(pin, f * 512, 512)) for pin in range(NDC)]
            mm_bank(bk, PS[bk][:, 0:512], pairs,
                    wait=[(GW, gw_w), (VE, scan_ve[("g", NDC - 1)])], inc=TE)
            te_g = TE.n
            if d == "f":
                b.vcopy(bsl(po, f * 512, 512), PS[bk][:, 0:512],
                        wait=[(TE, te_g)], inc=VE)
            else:
                b.vcopy(AP(B, po * TT + (TT - 1 - f * 512), [[BW, 128], [-1, 512]]),
                        PS[bk][:, 0:512], wait=[(TE, te_g)], inc=VE)
            ve_yst[(po, f)] = VE.n
            bank_release(bk, VE)
        wslots[po % 2] = (eoff, TE.n)
    if d == "f":
        if tl == 0:
            tap("tap_yf_t0", ap3(B, 0, BW, [[TT, 8], [1, TT]]),
                wait=[(VE, ve_yst[(7, 1)])])
        for po in range(8):
            b.dma("sync", AP(sp_yf, tl * TT + po * 128 * T, [[T, 128], [1, TT]]),
                  bsl(po, 0, TT), wait=[(VE, ve_yst[(po, 1)])], inc=ST)
        S["st_yf_%d" % tl] = ST.n
        return

    if d == "b" and tl == 1:
        tap("tap_yb_o0", ap3(B, 0, BW, [[TT, 8], [1, TT]]),
            wait=[(VE, ve_yst[(7, 1)])])
    b.phase = b.phase.split(':')[0] + ':merge_bwd_only_s_l'
    otl = NTT - 1 - tl
    for po in range(8):
        b.dma("sync", AP(B, 8 * TT + po * TT, [[BW, 128], [1, TT]]),
              AP(sp_yf, otl * TT + po * 128 * T, [[T, 128], [1, TT]]),
              wait=[(ST, S["st_yf_%d" % otl])], inc=LD)
    ld_yfr = LD.n
    wslots = [(5632, None), (7680, None)]
    for po in range(8):
        eoff, busy = wslots[po % 2]
        wwait = [(TE, TE.n)] if busy is None else [(TE, busy)]
        b.dma("gpsimd", ap3(E, eoff, EW, [[128, 16], [1, 128]]),
              AP(W["w_mrg"].tensor, po * 128, [[D, 128], [128 * D, 16], [1, 128]]),
              wait=wwait, inc=GW)
        gw_w = GW.n
        b.dma("sync", dsl(15, 0, TT),
              AP(sp_xn, otl * TT + po * 128 * T, [[T, 128], [1, TT]]), inc=LD)
        ld_xn = LD.n
        for f in range(2):
            bk = (po % 2) * 2 + f
            pairs = [(AP(E, eoff + pin * 128, [[EW, 128], [1, 128]]),
                      AP(B, 8 * TT + pin * TT + f * 512, [[BW, 128], [1, 512]]))
                     for pin in range(8)]
            pairs += [(AP(E, eoff + (8 + pin) * 128, [[EW, 128], [1, 128]]),
                       bsl(pin, f * 512, 512)) for pin in range(8)]
            mm_bank(bk, PS[bk][:, 0:512], pairs,
                    wait=[(GW, gw_w), (LD, ld_yfr), (VE, ve_yst[(7, 1)])], inc=TE)
            te_g = TE.n
            b.ts(dsl(14, f * 512, 512), PS[bk][:, 0:512],
                 pp["mrgb"][:, po:po + 1], None, ALU.add,
                 wait=[(TE, te_g), (LD, ld_xn)])
            b.tt(dsl(14, f * 512, 512), dsl(14, f * 512, 512),
                 dsl(15, f * 512, 512), ALU.add, inc=VE)
            bank_release(bk, VE)
        b.dma("sync", AP(sp_s, otl * TT + po * 128 * T, [[T, 128], [1, TT]]),
              dsl(14, 0, TT), wait=[(VE, VE.n)], inc=ST)
    S["st_s_%d" % otl] = ST.n
    if ("tap_s_t%d" % otl) in S["taps"]:
        b.dma("sync", ap3(B, 8 * TT, BW, [[TT, 8], [1, TT]]),
              AP(sp_s, otl * TT, [[T, 128], [128 * T, 8], [1, TT]]),
              wait=[(ST, ST.n)], inc=LD)
        tap("tap_s_t%d" % otl, ap3(B, 8 * TT, BW, [[TT, 8], [1, TT]]),
            wait=[(LD, LD.n)])




def _scalar_chain_quant(b, S, t_ar_ap, cols, sym=False):
    t_sc = S["t_sc"]
    c_inv, c_lo, c_hi, c_s = cols
    r = lambda c: t_sc[0:1, c:c + 1]
    b.tt(r(c_s), t_ar_ap[0:1, 0:1], t_ar_ap[0:1, 1:2], ALU.add)
    b.ts(r(c_s), r(c_s), 1.0 / 255.0, 1e-8, ALU.mult, ALU.max)
    b.vrecip(r(c_inv), r(c_s))
    b.tt(r(c_lo), t_ar_ap[0:1, 0:1], r(c_inv), ALU.mult)
    b.ts(r(c_lo), r(c_lo), MAGIC, MAGIC, ALU.add, ALU.subtract)
    b.ts(r(c_lo), r(c_lo), -128.0, -128.0, ALU.add, ALU.max)
    b.ts(r(c_lo), r(c_lo), 127.0, None, ALU.min)
    b.ts(r(c_hi), r(c_lo), -1.0, 127.0, ALU.mult, ALU.add)
    b.ts(r(c_lo), r(c_lo), -1.0, -128.0, ALU.mult, ALU.add)


def _finalize_col_stat(b, S, part_ap, op, c_dst, wait):
    AP = bass.AP
    bounce, t_red, t_sc = _v(S, "bounce", "t_red", "t_sc")
    LD, ST, VE = _v(S, "LD", "ST", "VE")
    b.vred(S["t_wst"][:, 63:64], part_ap, op, wait=wait, inc=VE)
    b.dma("sync", AP(bounce, 3 * 128, [[1, 1], [1, 128]]),
          S["t_wst"][:, 63:64], wait=[(VE, VE.n)], inc=ST)
    b.dma("sync", t_red[0:1, :], bounce.ap()[3:4, :], wait=[(ST, ST.n)], inc=LD)
    b.vred(t_sc[0:1, c_dst:c_dst + 1], t_red[0:1, :], op, wait=[(LD, LD.n)], inc=VE)


def _rep_cols(b, S, c0, c1):
    PS, t_sone, t_sc, t_srep = _v(S, "PS", "t_sone", "t_sc", "t_srep")
    TE, VE = _v(S, "TE", "VE")
    S["mm_bank"](4, PS[4][:, 0:c1 - c0], [(t_sone[0:1, :], t_sc[0:1, c0:c1])],
                 wait=[(VE, VE.n)], inc=TE, no_ldw=True)
    b.vcopy(t_srep[:, c0:c1], PS[4][:, 0:c1 - c0], wait=[(TE, TE.n)], inc=VE)
    S["bank_release"](4, VE)


def phase_ffn(S):
    b = S["b"]; AP = bass.AP
    A, B, Dm, E, Fp = _v(S, "A", "B", "Dm", "E", "Fp")
    PS, pp, t_sc, t_srep, t_hm, t_wst = _v(S, "PS", "pp", "t_sc", "t_srep",
                                           "t_hm", "t_wst")
    W, sp_xn, sp_h, sp_wq2, sp_g = _v(S, "W", "sp_xn", "sp_h", "sp_wq2", "sp_g")
    t_ar, ar_i, ar_o = _v(S, "t_ar", "ar_i", "ar_o")
    LD, ST, GW, VE, SE, TE, CC = _v(S, "LD", "ST", "GW", "VE", "SE", "TE", "CC")
    mm_bank, bank_release = S["mm_bank"], S["bank_release"]
    AW = NDC * (TT + 4)
    b.barrier([LD, ST, GW, VE, SE, TE])

    _finalize_col_stat(b, S, S["t_wst"][:, 0:16], ALU.max, 11, [(VE, S["ve_wst"])])
    b.ts(t_sc[0:1, 11:12], t_sc[0:1, 11:12], 1.0 / 127.0, 1e-8, ALU.mult, ALU.max)
    b.vrecip(t_sc[0:1, 10:11], t_sc[0:1, 11:12])
    _finalize_col_stat(b, S, S["t_wst"][:, 16:32], ALU.max, 13, [(VE, S["ve_wst"])])
    b.ts(t_sc[0:1, 13:14], t_sc[0:1, 13:14], 1.0 / 127.0, 1e-8, ALU.mult, ALU.max)
    b.vrecip(t_sc[0:1, 12:13], t_sc[0:1, 13:14])
    b.dma("gpsimd", t_ar[0:1, :], ar_o[0].ap()[0:1, :],
          wait=[(CC, S["cc_ar1"])], inc=GW)
    b.wait("vector", GW, GW.n)
    _scalar_chain_quant(b, S, t_ar, (0, 1, 2, 16))
    b.tt(t_sc[0:1, 3:4], t_sc[0:1, 16:17], t_sc[0:1, 11:12], ALU.mult, inc=VE)
    _rep_cols(b, S, 0, 4)
    _rep_cols(b, S, 10, 14)
    b.ts(t_srep[:, 14:15], t_srep[:, 1:2], -1.0, -MAGIC, ALU.mult, ALU.add,
         inc=VE)
    ve_bias1 = VE.n

    b.dma("sync", ap3(A, 0, AW, [[T, 8], [1, T]]),
          AP(sp_xn, 0, [[T, 128], [128 * T, 8], [1, T]]), inc=LD)
    ld_q = LD.n
    sr = lambda c: t_srep[:, c:c + 1]
    ve_pq = None
    for ch in range(8):
        qs = lambda lo, n: AP(A, ch * T + lo, [[AW, 128], [1, n]])
        b.act(Fp[:, 0:2048], qs(0, 2048), AF.Relu, scale=sr(0), bias=MAGIC,
              wait=[(LD, ld_q), (VE, VE.n)] if ch == 0 else ())
        b.act(Fp[:, 2048:4096], Fp[:, 0:2048], AF.Relu, bias=t_srep[:, 14:15],
              wait=([(VE, ve_bias1)] if ch == 0 else
                    [(VE, ve_pq)]), inc=SE)
        b.ts(qs(0, 2048), Fp[:, 2048:4096], sr(1), sr(2), ALU.add, ALU.min,
             wait=[(SE, SE.n)], inc=VE)
        ve_pq = VE.n
    ve_q1 = VE.n

    wslots = [(0, None), (TT, None)]
    stage_busy = {}
    for po in range(32):
        eoff, busy = wslots[po % 2]
        fo = (po % 2) * 1024
        b.dma("gpsimd", ap3(Fp, 2048 + fo, 5120, [[128, 8], [1, 128]]),
              AP(W["w1"].tensor, po * 128, [[H, 128], [128 * H, 8], [1, 128]]),
              wait=[(VE, ve_q1)] if busy is None else [(TE, busy)], inc=GW)
        gw_w = GW.n
        wq_w = [(GW, gw_w)]
        if po == 0:
            wq_w.append((VE, VE.n))
        if busy is not None:
            wq_w.append((TE, busy))
        b.act(Fp[:, 0:1024], Fp[:, 2048 + fo:2048 + fo + 1024], AF.Relu,
              scale=sr(10), bias=MAGIC, wait=wq_w)
        b.act(Fp[:, 1024:2048], Fp[:, 0:1024], AF.Relu, bias=-MAGIC + 127.0,
              wait=([] if ve_pq is None else [(VE, ve_pq)]), inc=SE)
        b.ts(AP(E, eoff, [[NDC * TT, 128], [1, 1024]]),
             Fp[:, 1024:2048], -127.0, 127.0, ALU.add, ALU.min,
             wait=[(SE, SE.n)], inc=VE)
        ve_pq = ve_wq = VE.n
        for f in range(4):
            bk = (po % 2) * 2 + (f % 2)
            pairs = [(AP(E, eoff + pin * 128, [[NDC * TT, 128], [1, 128]]),
                      AP(A, pin * T + f * 512, [[AW, 128], [1, 512]]))
                     for pin in range(8)]
            mm_bank(bk, PS[bk][:, 0:512], pairs,
                    wait=[(VE, max(ve_wq, ve_q1))], inc=TE)
            te_g = TE.n
            sg = (f % 2)
            stg = AP(Dm, (14 + sg) * TT + (f // 2) * 512, [[NDC * TT, 128], [1, 512]])
            sw = [(TE, te_g)]
            if (po, sg, f // 2) != (0, 0, 0) and ((14 + sg), f // 2) in stage_busy:
                sw.append((ST, stage_busy[((14 + sg), f // 2)]))
            b.act(stg, PS[bk][:, 0:512], AF.Gelu, scale=sr(3),
                  bias=pp["b1c"][:, po:po + 1], wait=sw, inc=SE)
            bank_release(bk, SE)
            b.dma("sync", AP(sp_h, po * 128 * T + f * 512, [[T, 128], [1, 512]]),
                  stg, wait=[(SE, SE.n)], inc=ST)
            stage_busy[((14 + sg), f // 2)] = ST.n
            hj = po * 4 + f
            b.vred(Fp[:, 4096 + hj:4097 + hj], stg, ALU.min, wait=[(SE, SE.n)])
            b.vred(Fp[:, 4224 + hj:4225 + hj], stg, ALU.max, inc=VE)
        wslots[po % 2] = (eoff, TE.n)
    b.vred(t_hm[:, 0:1], Fp[:, 4096:4224], ALU.min)
    b.vred(t_hm[:, 1:2], Fp[:, 4224:4352], ALU.max, inc=VE)

    bounce, t_red = _v(S, "bounce", "t_red")
    b.dma("sync", AP(bounce, 0, [[1, 1], [1, 128]]), t_hm[:, 0:1],
          wait=[(VE, VE.n)], inc=ST)
    b.dma("sync", AP(bounce, 128, [[1, 1], [1, 128]]), t_hm[:, 1:2],
          wait=[(VE, VE.n)], inc=ST)
    b.dma("sync", t_red[0:1, :], bounce.ap()[0:1, :], wait=[(ST, ST.n)], inc=LD)
    b.vred(t_ar[0:1, 0:1], t_red[0:1, :], ALU.min, wait=[(LD, LD.n)], inc=VE)
    b.ts(t_ar[0:1, 0:1], t_ar[0:1, 0:1], -1.0, inc=VE)
    b.dma("sync", t_red[0:1, :], bounce.ap()[1:2, :], wait=[(ST, ST.n)], inc=LD)
    b.vred(t_ar[0:1, 1:2], t_red[0:1, :], ALU.max, wait=[(LD, LD.n)], inc=VE)
    b.dma("sync", ar_i[1].ap()[0:1, :], t_ar[0:1, :], wait=[(VE, VE.n)], inc=ST)
    b.wait("gpsimd", ST, ST.n)
    b.coll("AllReduce", ALU.max, [ar_i[1].ap().opt()], [ar_o[1].ap().opt()], inc=CC)
    b.dma("gpsimd", t_ar[0:1, :], ar_o[1].ap()[0:1, :], wait=[(CC, CC.n)], inc=GW)
    b.wait("vector", GW, GW.n)
    _scalar_chain_quant(b, S, t_ar, (4, 5, 6, 17))
    b.tt(t_sc[0:1, 7:8], t_sc[0:1, 17:18], t_sc[0:1, 13:14], ALU.mult, inc=VE)
    _rep_cols(b, S, 4, 8)
    b.ts(t_srep[:, 15:16], t_srep[:, 5:6], -1.0, -MAGIC, ALU.mult, ALU.add,
         inc=VE)
    ve_bias2 = VE.n

    for rch in range(32):
        fo = (rch % 2) * 1024
        b.dma("gpsimd", Fp[:, 2048 + fo:2048 + fo + 1024],
              AP(W["w2"].tensor, rch * 128 * D, [[D, 128], [1, D]]),
              wait=[(VE, VE.n)], inc=GW)
        b.act(Fp[:, 0:1024], Fp[:, 2048 + fo:2048 + fo + 1024], AF.Relu,
              scale=sr(12), bias=MAGIC, wait=[(GW, GW.n)])
        b.act(Fp[:, 1024:2048], Fp[:, 0:1024], AF.Relu, bias=-MAGIC + 127.0,
              wait=[(VE, ve_pq)], inc=SE)
        eoff = (rch % 2) * 1024
        b.ts(AP(E, eoff, [[NDC * TT, 128], [1, 1024]]), Fp[:, 1024:2048],
             -127.0, 127.0, ALU.add, ALU.min, wait=[(SE, SE.n)], inc=VE)
        ve_pq = VE.n
        b.dma("sync", AP(sp_wq2, rch * 128 * D, [[D, 128], [1, D]]),
              AP(E, eoff, [[NDC * TT, 128], [1, 1024]]),
              wait=[(VE, VE.n)], inc=ST)
    st_wq2 = ST.n

    for t4 in range(4):
        b.dma("sync", ap3(B, 0, NDC * TT, [[512, 32], [1, 512]]),
              AP(sp_h, t4 * 512, [[T, 128], [128 * T, 32], [1, 512]]),
              wait=[(TE, TE.n)], inc=LD)
        ld_h = LD.n
        for g8 in range(8):
            qs = lambda lo, n: AP(B, g8 * 2048 + lo, [[NDC * TT, 128], [1, n]])
            b.act(Fp[:, 0:2048], qs(0, 2048), AF.Relu, scale=sr(4), bias=MAGIC,
                  wait=[(LD, ld_h), (VE, VE.n), (GW, GW.n)] if g8 == 0 else ())
            b.act(Fp[:, 2048:4096], Fp[:, 0:2048], AF.Relu,
                  bias=t_srep[:, 15:16],
                  wait=[(VE, ve_bias2 if (t4 == 0 and g8 == 0) else ve_pq)],
                  inc=SE)
            b.ts(qs(0, 2048), Fp[:, 2048:4096], sr(5), sr(6), ALU.add, ALU.min,
                 wait=[(SE, SE.n)], inc=VE)
            ve_pq = VE.n
        ve_q2 = VE.n
        wping = [(0, None), (1024, None)]
        for pin in range(32):
            eoff, busy = wping[pin % 2]
            b.dma("sync", AP(E, eoff, [[NDC * TT, 128], [1, D]]),
                  AP(sp_wq2, pin * 128 * D, [[D, 128], [1, D]]),
                  wait=[(ST, st_wq2)] + ([] if busy is None else [(TE, busy)]),
                  inc=LD)
            ld_w = LD.n
            for po in range(8):
                mm_bank(po, PS[po][:, 0:512],
                        [(AP(E, eoff + po * 128, [[NDC * TT, 128], [1, 128]]),
                          AP(B, pin * 512, [[NDC * TT, 128], [1, 512]]))],
                        wait=[(LD, ld_w), (VE, ve_q2)],
                        start=(pin == 0), stop=(pin == 31),
                        inc=(TE if po == 7 else None))
            wping[pin % 2] = (eoff, TE.n)
        te_f2 = TE.n
        for po in range(8):
            sg = po % 2
            stg = AP(Dm, (14 + sg) * TT, [[NDC * TT, 128], [1, 512]])
            sw = [(TE, te_f2)]
            if ("f2", sg) in stage_busy:
                sw.append((ST, stage_busy[("f2", sg)]))
            b.act(stg, PS[po][:, 0:512], AF.Silu, scale=sr(7),
                  bias=pp["b2c"][:, po:po + 1], wait=sw, inc=SE)
            bank_release(po, SE)
            b.dma("sync", AP(sp_g, po * 128 * T + t4 * 512, [[T, 128], [1, 512]]),
                  stg, wait=[(SE, SE.n)], inc=ST)
            stage_busy[("f2", sg)] = ST.n
    S["st_g"] = ST.n


def phase_x1_ag(S):
    b = S["b"]; AP = bass.AP
    A, B, C = _v(S, "A", "B", "C")
    X, sp_s, sp_g, ag_i, ag_o = _v(S, "X", "sp_s", "sp_g", "ag_i", "ag_o")
    LD, ST, GW, VE, SE, TE, CC = _v(S, "LD", "ST", "GW", "VE", "SE", "TE", "CC")
    AW = NDC * (TT + 4)
    b.barrier([LD, ST, GW, VE, SE, TE])
    for tl in range(NTT):
        b.dma("gpsimd", ap3(C, 0, NDC * TT, [[TT, 8], [1, TT]]),
              AP(X.tensor, tl * TT, [[T, 128], [128 * T, 8], [1, TT]]), inc=GW)
        b.dma("sync", ap3(C, 8 * TT, NDC * TT, [[TT, 8], [1, TT]]),
              AP(sp_s, tl * TT, [[T, 128], [128 * T, 8], [1, TT]]),
              wait=[(ST, S["st_s_%d" % tl])], inc=LD)
        b.dma("sync", ap3(B, 0, NDC * TT, [[TT, 8], [1, TT]]),
              AP(sp_g, tl * TT, [[T, 128], [128 * T, 8], [1, TT]]),
              wait=[(ST, S["st_g"])], inc=LD)
        b.tt(ap3(B, 8 * TT, NDC * TT, [[TT, 8], [1, TT]]),
             ap3(C, 8 * TT, NDC * TT, [[TT, 8], [1, TT]]),
             ap3(B, 0, NDC * TT, [[TT, 8], [1, TT]]), ALU.mult,
             wait=[(LD, LD.n), (GW, GW.n)])
        b.tt(ap3(A, 0, AW, [[TT, 8], [1, TT]]),
             ap3(C, 0, NDC * TT, [[TT, 8], [1, TT]]),
             ap3(B, 8 * TT, NDC * TT, [[TT, 8], [1, TT]]), ALU.add, inc=VE)
        b.dma("sync", AP(ag_i, tl * TT, [[T, 128], [128 * T, 8], [1, TT]]),
              ap3(A, 0, AW, [[TT, 8], [1, TT]]), wait=[(VE, VE.n)], inc=ST)
        if tl == 0:
            S["tap"]("tap_g_t0", ap3(B, 0, NDC * TT, [[TT, 8], [1, TT]]),
                     wait=[(VE, VE.n)])
            S["tap"]("tap_x1_t0", ap3(A, 0, AW, [[TT, 8], [1, TT]]),
                     wait=[(VE, VE.n)])
    b.wait("gpsimd", ST, ST.n)
    b.coll("AllGather", ALU.bypass, [ag_i.ap().opt()], [ag_o.ap().opt()], inc=CC)
    S["cc_ag"] = CC.n


def phase_l1(S):
    b = S["b"]; AP = bass.AP
    A, B, C, Dm, E, Fp = _v(S, "A", "B", "C", "Dm", "E", "Fp")
    PS, pp, c_one, c_bd2 = _v(S, "PS", "pp", "c_one", "c_bd2")
    t_sc, t_srep, t_wst, t_scr = _v(S, "t_sc", "t_srep", "t_wst", "t_scr")
    t_ar = S["t_ar"]
    rep_ap, rep_bc = _v(S, "rep_ap", "rep_bc")
    OFF_M, OFF_A, OFF_C = _v(S, "OFF_M", "OFF_A", "OFF_C")
    W, ag_o, sp_g2, sp_outs, OUT = _v(S, "W", "ag_o", "sp_g2", "sp_outs", "OUT")
    ar_i, ar_o = _v(S, "ar_i", "ar_o")
    LD, ST, GW, VE, SE, TE, CC = _v(S, "LD", "ST", "GW", "VE", "SE", "TE", "CC")
    mm_bank, bank_release, tap = S["mm_bank"], S["bank_release"], S["tap"]
    AW = NDC * (TT + 4)
    XW = NDC * TT
    sp_x2 = S["sp_x2"]
    b.barrier([LD, ST, GW, VE, SE, TE])

    cc_ag = S["cc_ag"]
    def _gather(e, cc_h=CC.h, gw_h=GW.h):
        e.wait_ge(cc_h, cc_ag)
        rk = e.cc_rank([list(range(NCORES))])
        for i in e.Switch(rk, NCORES):
            for rr in range(NCORES):
                e.dma_start(AP(sp_x2, rr * 256, [[T, 128], [128 * T, 8], [1, 256]]),
                            AP(ag_o, rr * D * T + i * 256,
                               [[T, 128], [128 * T, 8], [1, 256]])).then_inc(gw_h, 16)
    b.on("gpsimd", _gather)
    GW.n += NCORES * 16
    b.dma("sync", ap3(A, 0, AW, [[2048, 8], [1, 2048]]),
          AP(sp_x2, 0, [[T, 128], [128 * T, 8], [1, T]]),
          wait=[(GW, GW.n)], inc=LD)
    ld_x2 = LD.n

    x2s = lambda ch, lo, n: AP(A, ch * 2048 + lo, [[AW, 128], [1, n]])
    xn2s = lambda ch, lo, n: AP(Dm, ch * 2048 + lo, [[XW, 128], [1, n]])
    sq = lambda ch, lo, n: AP(B, ch * 1024 + lo, [[XW, 128], [1, n]])
    for hl in range(2):
        b.act(ap3(B, 0, XW, [[1024, 8], [1, 1024]]),
              ap3(A, hl * 1024, AW, [[2048, 8], [1, 1024]]), AF.Square,
              wait=[(LD, ld_x2), (VE, VE.n)], inc=SE)
        se_sq = SE.n
        px = lambda f: [(c_one[:, 0:128], x2s(ch, hl * 1024 + f * 512, 512))
                        for ch in range(8)]
        psq = lambda f: [(c_one[:, 0:128], sq(ch, f * 512, 512))
                         for ch in range(8)]
        ln_stats(S, px, psq, True, wx=[(LD, ld_x2)], wsq=[(SE, se_sq)])
        b.wait("vector", TE, TE.n)
        b.tt(ap3(B, 8 * TT, XW, [[1024, 8], [1, 1024]]),
             ap3(A, hl * 1024, AW, [[2048, 8], [1, 1024]]),
             rep_bc(OFF_M), ALU.subtract)
        b.tt(ap3(Dm, hl * 1024, XW, [[2048, 8], [1, 1024]]),
             ap3(B, 8 * TT, XW, [[1024, 8], [1, 1024]]), rep_bc(OFF_A), ALU.mult)
        b.tt(ap3(B, 0, XW, [[1024, 8], [1, 1024]]),
             ap3(B, 8 * TT, XW, [[1024, 8], [1, 1024]]), rep_bc(OFF_C), ALU.mult)
        for ch in range(8):
            b.ts(AP(A, ch * 2048 + hl * 1024, [[AW, 128], [1, 1024]]),
                 AP(B, ch * 1024, [[XW, 128], [1, 1024]]),
                 pp["gam"][:, ch:ch + 1], pp["bet"][:, ch:ch + 1],
                 ALU.mult, ALU.add, inc=(VE if ch == 7 else None))
    ve_xf = VE.n
    tap("tap_xn2", ap3(Dm, 0, XW, [[2048, 8], [1, 2048]]), wait=[(VE, ve_xf)])

    b.vred(t_wst[:, 62:63], AP(A, 0, [[AW, 128], [1, 8 * 2048]]),
           ALU.max, absval=True, wait=[(VE, ve_xf)], inc=VE)
    bounce, t_red = _v(S, "bounce", "t_red")
    b.dma("sync", AP(bounce, 2 * 128, [[1, 1], [1, 128]]), t_wst[:, 62:63],
          wait=[(VE, VE.n)], inc=ST)
    b.dma("sync", t_red[0:1, :], bounce.ap()[2:3, :], wait=[(ST, ST.n)], inc=LD)
    b.vred(t_ar[0:1, 0:1], t_red[0:1, :], ALU.max, wait=[(LD, LD.n)])
    b.vcopy(t_ar[0:1, 1:2], t_ar[0:1, 0:1], inc=VE)
    b.dma("sync", ar_i[2].ap()[0:1, :], t_ar[0:1, :], wait=[(VE, VE.n)], inc=ST)
    b.wait("gpsimd", ST, ST.n)
    b.coll("AllReduce", ALU.max, [ar_i[2].ap().opt()], [ar_o[2].ap().opt()], inc=CC)
    b.dma("gpsimd", t_ar[0:1, :], ar_o[2].ap()[0:1, :], wait=[(CC, CC.n)], inc=GW)

    _finalize_col_stat(b, S, t_wst[:, 32:36], ALU.add, 14, [(VE, S["ve_wst"])])
    b.ts(t_sc[0:1, 14:15], t_sc[0:1, 14:15], 0.7 / (D * D), None, ALU.mult, inc=VE)
    _rep_cols(b, S, 14, 15)
    es = lambda i, lo=0, n=2048: AP(E, i * 2048 + lo, [[XW, 128], [1, n]])
    wq1 = lambda ch, lo, n: AP(E, ch * 1024 + lo, [[XW, 128], [1, n]])
    for ch in range(8):
        fo = (ch % 2) * 1024
        b.dma("gpsimd", Fp[:, 2048 + fo:2048 + fo + 1024],
              AP(W["wl1"].tensor, ch * 128 * D, [[D, 128], [1, D]]),
              wait=[(VE, VE.n)], inc=GW)
        gw_w = GW.n
        b.wait("scalar", GW, gw_w)
        b.act(es(4, 0, 1024), Fp[:, 2048 + fo:2048 + fo + 1024], AF.Abs, inc=SE)
        b.act(es(4, 1024, 1024), Fp[:, 2048 + fo:2048 + fo + 1024], AF.Sign, inc=SE)
        se_aw = SE.n
        b.ts(es(5, 0, 1024), es(4, 0, 1024), t_srep[:, 14:15], None, ALU.is_gt,
             wait=[(SE, se_aw)])
        b.vred(t_wst[:, ch:ch + 1], es(5, 0, 1024), ALU.add)
        b.tt(es(5, 1024, 1024), es(4, 0, 1024), es(5, 0, 1024), ALU.mult)
        b.vred(t_wst[:, 8 + ch:9 + ch], es(5, 1024, 1024), ALU.add)
        b.tt(wq1(ch, 0, 1024), es(4, 1024, 1024), es(5, 0, 1024), ALU.mult, inc=VE)
    ve_wq1 = VE.n
    _finalize_col_stat(b, S, t_wst[:, 0:8], ALU.add, 20, [(VE, ve_wq1)])
    _finalize_col_stat(b, S, t_wst[:, 8:16], ALU.add, 21, [(VE, ve_wq1)])
    b.ts(t_sc[0:1, 20:21], t_sc[0:1, 20:21], 1.0, None, ALU.max)
    b.vrecip(t_sc[0:1, 22:23], t_sc[0:1, 20:21])
    b.tt(t_sc[0:1, 23:24], t_sc[0:1, 21:22], t_sc[0:1, 22:23], ALU.mult)
    b.wait("vector", GW, GW.n)
    b.ts(t_sc[0:1, 18:19], t_ar[0:1, 0:1], 1.0 / 127.0, 1e-8, ALU.mult, ALU.max)
    b.vrecip(t_sc[0:1, 8:9], t_sc[0:1, 18:19])
    b.tt(t_sc[0:1, 9:10], t_sc[0:1, 18:19], t_sc[0:1, 23:24], ALU.mult, inc=VE)
    _rep_cols(b, S, 8, 10)
    sr = lambda c: t_srep[:, c:c + 1]

    tap("tap_xf", ap3(A, 0, AW, [[2048, 8], [1, 2048]]), wait=[(VE, ve_xf)])
    tap("tap_sc", S["t_sc"][0:1, :], wait=[(VE, VE.n)])
    ve_pq = None
    for ch in range(8):
        xfs = lambda lo, n: AP(A, ch * 2048 + lo, [[AW, 128], [1, n]])
        b.act(Fp[:, 0:2048], xfs(0, 2048), AF.Relu, scale=sr(8), bias=MAGIC,
              wait=[(VE, VE.n), (GW, GW.n)] if ch == 0 else ())
        b.act(Fp[:, 2048:4096], Fp[:, 0:2048], AF.Relu, bias=-MAGIC + 127.0,
              wait=([] if ve_pq is None else [(VE, ve_pq)]), inc=SE)
        b.ts(xfs(0, 2048), Fp[:, 2048:4096], -127.0, 127.0, ALU.add, ALU.min,
             wait=[(SE, SE.n)], inc=VE)
        ve_pq = VE.n
    ve_q3 = VE.n
    tap("tap_q3", ap3(A, 0, AW, [[2048, 8], [1, 2048]]), wait=[(VE, ve_q3)])
    tap("tap_wl1q", ap3(E, 0, XW, [[1024, 8], [1, 1024]]), wait=[(VE, ve_q3)])
    stage_busy = {}
    for f in range(4):
        for pin in range(8):
            for po in range(8):
                mm_bank(po, PS[po][:, 0:512],
                        [(AP(E, pin * 1024 + po * 128, [[XW, 128], [1, 128]]),
                          AP(A, pin * 2048 + f * 512, [[AW, 128], [1, 512]]))],
                        wait=[(VE, ve_q3)], start=(pin == 0), stop=(pin == 7),
                        inc=(TE if pin == 7 else None))
        te_g = TE.n
        for po in range(8):
            sg = po % 2
            stg = t_scr[:, sg * 512:(sg + 1) * 512]
            sw = [(TE, te_g)]
            if ("g2", sg) in stage_busy:
                sw.append((ST, stage_busy[("g2", sg)]))
            b.act(stg, PS[po][:, 0:512], AF.Silu, scale=sr(9), wait=sw, inc=SE)
            bank_release(po, SE)
            b.dma("sync", AP(sp_g2, po * 128 * T + f * 512, [[T, 128], [1, 512]]),
                  stg, wait=[(SE, SE.n)], inc=ST)
            stage_busy[("g2", sg)] = ST.n
    st_g2 = ST.n

    te_ffn = TE.n
    wping = [(0, None), (1024, None)]
    ve_qkv = {}
    se_beta = {}
    for po in range(25):
        eoff, busy = wping[po % 2]
        ncols = 128 if po < 24 else 16
        b.dma("gpsimd", AP(E, eoff, [[XW, 128], [128, 8], [1, ncols]]),
              AP(W["wqkv"].tensor, po * 128,
                 [[3088, 128], [128 * 3088, 8], [1, ncols]]),
              wait=([(TE, te_ffn)] if busy is None else [(TE, busy)]), inc=GW)
        gw_w = GW.n
        for f in range(4):
            bk = (po % 2) * 2 + (f % 2)
            pairs = [(AP(E, eoff + pin * 128, [[XW, 128], [1, ncols]]),
                      AP(Dm, pin * 2048 + f * 512, [[XW, 128], [1, 512]]))
                     for pin in range(8)]
            mm_bank(bk, PS[bk][0:ncols, 0:512], pairs,
                    wait=[(GW, gw_w), (VE, ve_xf)], inc=TE)
            te_g = TE.n
            if po < 8:
                b.vcopy(AP(A, po * 2048 + f * 512, [[AW, 128], [1, 512]]),
                        PS[bk][:, 0:512], wait=[(TE, te_g), (TE, te_ffn)], inc=VE)
                ve_qkv[("q", po, f)] = VE.n
                bank_release(bk, VE)
            elif po < 16:
                b.vcopy(AP(B, (po - 8) * 2048 + f * 512, [[XW, 128], [1, 512]]),
                        PS[bk][:, 0:512], wait=[(TE, te_g)], inc=VE)
                ve_qkv[("k", po - 8, f)] = VE.n
                bank_release(bk, VE)
            elif po < 24:
                b.vcopy(AP(C, (po - 16) * 2048 + f * 512, [[XW, 128], [1, 512]]),
                        PS[bk][:, 0:512], wait=[(TE, te_g)], inc=VE)
                ve_qkv[("v", po - 16, f)] = VE.n
                bank_release(bk, VE)
            else:
                b.act(AP(E, 7 * 2048 + f * 512, [[XW, 16], [1, 512]]),
                      PS[bk][0:16, 0:512], AF.Sigmoid,
                      wait=[(TE, te_g)], inc=SE)
                se_beta[f] = SE.n
                bank_release(bk, SE)
        wping[po % 2] = (eoff, TE.n)

    es2 = lambda lo, n: AP(E, 2 * 2048 + lo, [[XW, 128], [1, n]])
    es3 = lambda lo, n: AP(E, 4 * 2048 + lo, [[XW, 128], [1, n]])
    bd2cols = AP(c_bd2, 0, [[128, 128], [64, 2]])
    ve_norm = {}
    for which, Ar in (("q", A), ("k", B)):
        Wd = AW if which == "q" else XW
        for ch in range(8):
            sl = lambda lo, n: AP(Ar, ch * 2048 + lo, [[Wd, 128], [1, n]])
            wv = [(VE, ve_qkv[(which, ch, 3)])]
            b.ts(es2(0, 2048), sl(0, 2048), 0.0, None, ALU.min, wait=wv, inc=VE)
            b.wait("scalar", VE, VE.n)
            b.act(es3(0, 2048), es2(0, 2048), AF.Exp, inc=SE)
            b.stt(es2(0, 2048), sl(0, 2048), 0.0, es3(0, 2048), ALU.max, ALU.add,
                  wait=[(SE, SE.n)], inc=VE)
            ve_e = VE.n
            for f in range(4):
                mm_bank(6, PS[6][0:2, 0:512],
                        [(bd2cols, es2(f * 512, 512))],
                        wait=[(VE, ve_e)], inc=TE)
                mw = [(TE, TE.n)]
                bf = S["bank_free"]
                for bx in (4, 5):
                    if bf[bx] is not None:
                        mw.append(bf[bx]); bf[bx] = None
                b.vrecip(PS[5][0:2, 0:512], PS[6][0:2, 0:512], wait=mw)
                if which == "q":
                    b.ts(PS[4][0:2, 0:512], PS[5][0:2, 0:512], 0.125, None,
                         ALU.mult)
                    b.vcopy(AP(E, 9728, [[XW, 2], [1, 512]]), PS[4][0:2, 0:512],
                            inc=VE)
                else:
                    b.vcopy(AP(E, 9728, [[XW, 2], [1, 512]]), PS[5][0:2, 0:512],
                            inc=VE)
                bank_release(6, VE)
                bank_release(5, VE)
                bank_release(4, VE)
                mm_bank(7, PS[7][:, 0:512],
                        [(c_bd2[0:2, :], AP(E, 9728, [[XW, 2], [1, 512]]))],
                        wait=[(VE, VE.n)], inc=TE)
                b.tt(sl(f * 512, 512), es2(f * 512, 512), PS[7][:, 0:512],
                     ALU.mult, wait=[(TE, TE.n)], inc=VE)
                bank_release(7, VE)
            ve_norm[(which, ch)] = VE.n

    el = lambda i, lo, n: AP(E, i * 2048 + lo, [[XW, 128], [1, n]])
    st_outs = {}
    for ht in range(8):
        qs = lambda lo, n: AP(A, ht * 2048 + lo, [[AW, 128], [1, n]])
        ks = lambda lo, n: AP(B, ht * 2048 + lo, [[XW, 128], [1, n]])
        vs = lambda lo, n: AP(C, ht * 2048 + lo, [[XW, 128], [1, n]])
        kbc = lambda u, nrep: AP(B, ht * 2048 + u * 256,
                                 [[XW, 128], [0, nrep], [1, 256]])
        dbc = lambda u, nrep: AP(E, 0 * 2048 + u * 256,
                                 [[XW, 128], [0, nrep], [1, 256]])
        for f in range(4):
            mm_bank(5, PS[5][:, 0:512],
                    [(S["c_selh"][0:16, ht * 128:(ht + 1) * 128],
                      AP(E, 7 * 2048 + f * 512, [[XW, 16], [1, 512]]))],
                    wait=[(SE, se_beta[f]), (VE, VE.n)], inc=TE)
            b.vcopy(el(2, f * 512, 512), PS[5][:, 0:512], wait=[(TE, TE.n)], inc=VE)
            bank_release(5, VE)
        b.tt(el(1, 0, 2048), vs(0, 2048), el(2, 0, 2048), ALU.mult,
             wait=[(VE, ve_qkv[("v", ht, 3)])])

        b.vcopy(el(0, 0, 256), el(1, 0, 256))
        prev_mm = None
        for u in range(7):
            span = (7 - u) * 256
            base = (u + 1) * 256
            for c0 in range(0, span, 512):
                n = min(512, span - c0)
                pw = [(VE, ve_norm[("k", ht)])] if (u == 0 and c0 == 0) else []
                if prev_mm is not None:
                    pw.append((TE, prev_mm))
                b.tt(t_scr[:, 0:n], ks(base + c0, n), kbc(u, n // 256), ALU.mult,
                     wait=pw, inc=VE)
                mm_bank(4, PS[4][:, 0:n], [(c_bd2[:, :], t_scr[:, 0:n])],
                        wait=[(VE, VE.n)], inc=TE)
                prev_mm = TE.n
                b.tt(t_scr[:, 512:512 + n], dbc(u, n // 256), PS[4][:, 0:n],
                     ALU.mult, wait=[(TE, prev_mm)], inc=VE)
                bank_release(4, VE)
                if u == 0:
                    b.vcopy(el(3, base + c0, n), t_scr[:, 512:512 + n])
                else:
                    b.tt(el(3 + (u % 2), base + c0, n),
                         el(3 + ((u + 1) % 2), base + c0, n),
                         t_scr[:, 512:512 + n], ALU.add)
            acur = 3 + (u % 2) if u > 0 else 3
            b.tt(t_scr[:, 0:256], el(acur, base, 256), el(2, base, 256), ALU.mult)
            b.tt(el(0, base, 256), el(1, base, 256), t_scr[:, 0:256],
                 ALU.subtract, inc=VE)
        for u in range(8):
            span = (8 - u) * 256
            base = u * 256
            for c0 in range(0, span, 512):
                n = min(512, span - c0)
                pw = [(VE, ve_norm[("q", ht)])] if (u == 0 and c0 == 0) else []
                if prev_mm is not None:
                    pw.append((TE, prev_mm))
                b.tt(t_scr[:, 0:n], qs(base + c0, n), kbc(u, n // 256), ALU.mult,
                     wait=pw, inc=VE)
                mm_bank(4, PS[4][:, 0:n], [(c_bd2[:, :], t_scr[:, 0:n])],
                        wait=[(VE, VE.n)], inc=TE)
                prev_mm = TE.n
                b.tt(t_scr[:, 512:512 + n], dbc(u, n // 256), PS[4][:, 0:n],
                     ALU.mult, wait=[(TE, prev_mm)], inc=VE)
                bank_release(4, VE)
                if u == 0:
                    b.vcopy(el(5, base + c0, n), t_scr[:, 512:512 + n])
                else:
                    b.tt(el(5 + (u % 2), base + c0, n),
                         el(5 + ((u + 1) % 2), base + c0, n),
                         t_scr[:, 512:512 + n], ALU.add)
            if u > 0:
                b.vcopy(el(5 + (u % 2), 0, base), el(5 + ((u + 1) % 2), 0, base),
                        inc=(VE if u == 7 else None))
        b.dma("sync", AP(sp_outs, ht * 128 * T, [[T, 128], [1, T]]),
              el(6, 0, 2048), wait=[(VE, VE.n)], inc=ST)
        st_outs[ht] = ST.n

    if "tap_outs" in S["taps"]:
        b.dma("sync", ap3(B, 0, XW, [[2048, 8], [1, 2048]]),
              AP(sp_outs, 0, [[T, 128], [128 * T, 8], [1, T]]),
              wait=[(ST, st_outs[7])], inc=LD)
        tap("tap_outs", ap3(B, 0, XW, [[2048, 8], [1, 2048]]), wait=[(LD, LD.n)])
    wping = [(0, None), (1024, None)]
    ve_x3 = {}
    for f in range(4):
        b.dma("sync", ap3(B, 0, XW, [[512, 8], [1, 512]]),
              AP(sp_outs, f * 512, [[T, 128], [128 * T, 8], [1, 512]]),
              wait=[(ST, st_outs[7]), (TE, TE.n)], inc=LD)
        ld_o = LD.n
        for pin in range(8):
            eoff, busy = wping[pin % 2]
            b.dma("gpsimd", AP(E, eoff, [[XW, 128], [1, D]]),
              AP(W["wo"].tensor, pin * 128 * D, [[D, 128], [1, D]]),
              wait=[(TE, TE.n)] if busy is None else [(TE, busy)], inc=GW)
            gw_w = GW.n
            for po in range(8):
                mm_bank(po, PS[po][:, 0:512],
                        [(AP(E, eoff + po * 128, [[XW, 128], [1, 128]]),
                          AP(B, pin * 512, [[XW, 128], [1, 512]]))],
                        wait=[(GW, gw_w), (LD, ld_o)],
                        start=(pin == 0), stop=(pin == 7),
                        inc=(TE if po == 7 else None))
            wping[pin % 2] = (eoff, TE.n)
        te_g = TE.n
        for po in range(8):
            b.tt(AP(C, po * 2048 + f * 512, [[XW, 128], [1, 512]]),
                 PS[po][:, 0:512],
                 AP(Dm, po * 2048 + f * 512, [[XW, 128], [1, 512]]), ALU.add,
                 wait=[(TE, te_g)], inc=VE)
            bank_release(po, VE)
        ve_x3[f] = VE.n

    tap("tap_x3", ap3(C, 0, XW, [[2048, 8], [1, 2048]]),
        wait=[(VE, ve_x3[3])])
    for hl in range(2):
        b.act(ap3(B, 0, XW, [[1024, 8], [1, 1024]]),
              ap3(C, hl * 1024, XW, [[2048, 8], [1, 1024]]), AF.Square,
              wait=[(VE, ve_x3[3]), (VE, VE.n)], inc=SE)
        se_sq = SE.n
        px = lambda f: [(c_one[:, 0:128],
                         AP(C, ch * 2048 + hl * 1024 + f * 512, [[XW, 128], [1, 512]]))
                        for ch in range(8)]
        psq = lambda f: [(c_one[:, 0:128],
                          AP(B, ch * 1024 + f * 512, [[XW, 128], [1, 512]]))
                         for ch in range(8)]
        ln_stats(S, px, psq, False, wx=[(VE, ve_x3[3])], wsq=[(SE, se_sq)])
        b.wait("vector", TE, TE.n)
        cen2 = ap3(B, 8 * TT, XW, [[1024, 8], [1, 1024]])
        b.tt(cen2, ap3(C, hl * 1024, XW, [[2048, 8], [1, 1024]]),
             rep_bc(OFF_M), ALU.subtract)
        b.tt(ap3(B, 0, XW, [[1024, 8], [1, 1024]]), cen2, rep_bc(OFF_A), ALU.mult)
        for ch in range(8):
            b.ts(AP(B, 8 * TT + ch * 1024, [[XW, 128], [1, 1024]]),
                 AP(B, ch * 1024, [[XW, 128], [1, 1024]]),
                 pp["lng3"][:, ch:ch + 1], pp["lnb3"][:, ch:ch + 1],
                 ALU.mult, ALU.add, inc=(VE if ch == 7 else None))
        if hl == 0:
            b.vcopy(ap3(Dm, 8 * TT, XW, [[1024, 8], [1, 1024]]),
                    ap3(B, 8 * TT, XW, [[1024, 8], [1, 1024]]), inc=VE)
    ve_s2 = VE.n
    tap("tap_s2a", ap3(Dm, 8 * TT, XW, [[1024, 8], [1, 1024]]), wait=[(VE, ve_s2)])
    tap("tap_s2b", ap3(B, 8 * TT, XW, [[1024, 8], [1, 1024]]), wait=[(VE, ve_s2)])

    b.dma("sync", ap3(A, 0, AW, [[2048, 8], [1, 2048]]),
          AP(sp_x2, 0, [[T, 128], [128 * T, 8], [1, T]]),
          wait=[(VE, ve_s2)], inc=LD)
    b.dma("sync", ap3(C, 0, XW, [[2048, 8], [1, 2048]]),
          AP(sp_g2, 0, [[T, 128], [128 * T, 8], [1, T]]),
          wait=[(ST, st_g2), (VE, ve_s2)], inc=LD)
    ld_fin = LD.n
    tap("tap_g2", ap3(C, 0, XW, [[2048, 8], [1, 2048]]), wait=[(LD, ld_fin)])
    stage_busy2 = {}
    for ch in range(8):
        fo = (ch % 2) * 2048
        sw = [(LD, ld_fin)]
        if ("fin", ch % 2) in stage_busy2:
            sw.append((ST, stage_busy2[("fin", ch % 2)]))
        for hl in range(2):
            s2t, s2base = (Dm, 8 * TT) if hl == 0 else (B, 8 * TT)
            b.tt(t_scr[:, 0:1024],
                 AP(s2t, s2base + ch * 1024, [[XW, 128], [1, 1024]]),
                 AP(C, ch * 2048 + hl * 1024, [[XW, 128], [1, 1024]]), ALU.mult,
                 wait=sw if hl == 0 else ())
            b.tt(Fp[:, fo + hl * 1024:fo + (hl + 1) * 1024],
                 AP(A, ch * 2048 + hl * 1024, [[AW, 128], [1, 1024]]),
                 t_scr[:, 0:1024], ALU.add, inc=VE)
        b.dma("sync", AP(S["OUT"].tensor, ch * 128 * T, [[T, 128], [1, T]]),
              Fp[:, fo:fo + 2048], wait=[(VE, VE.n)], inc=ST)
        stage_busy2[("fin", ch % 2)] = ST.n


# version 10
# speedup vs baseline: 1.1622x; 1.0730x over previous
import os
import sys
import numpy as np
from contextlib import ExitStack
import concourse.bass as bass
import concourse.mybir as mybir
from concourse.bass_utils import run_bass_kernel_spmd

F32 = mybir.dt.float32
F16 = mybir.dt.float16
BF16 = mybir.dt.bfloat16
I32 = mybir.dt.int32
ALU = mybir.AluOpType
AF = mybir.ActivationFunctionType
AX = mybir.AxisListType

NCORES = 8
D = 1024
T = 2048
TT = 1024
NTT = T // TT
DI = 2048
NDC = DI // 128
NS = 16
DTR = 64
NH = 16
DH = 64
H = 4096
EPS = 1e-5
MAGIC = 3.0 * 2.0 ** 22

ENGINES = ("sync", "vector", "scalar", "tensor", "gpsimd")

SIM_DRAINS = (os.environ.get("BASS_SIM_FP64") == "1"
              or os.environ.get("KERNEL_SIM_DRAINS") == "1")

_DEBUG_TAPS = []


class Sem:
    __slots__ = ("name", "h", "n")

    def __init__(self, name, h):
        self.name = name
        self.h = h
        self.n = 0


class Build:

    def __init__(self):
        self.nc = bass.Bass("TRN2", target_bir_lowering=False)
        self.st = ExitStack()
        self.q = {e: [] for e in ENGINES}
        self.dirty = {e: ([], []) for e in ENGINES}
        self.opct = {e: 0 for e in ENGINES}
        self.waited = {e: {} for e in ENGINES}
        self.phase = "init"
        self.drainct = {}
        self.dma_touch = {}
        self.sems = []
        self.ntap = 0

    def sem(self, name):
        s = Sem(name, self.st.enter_context(self.nc.semaphore(name)))
        self.sems.append(s)
        return s

    def sbuf(self, name, shape, dt=F32):
        return self.st.enter_context(self.nc.sbuf_tensor(name, shape, dt))

    def psum(self, name, shape=(128, 512)):
        return self.st.enter_context(self.nc.psum_tensor(name, list(shape), F32))

    def on(self, eng, fn):
        self.q[eng].append(fn)

    @staticmethod
    def _ivs(aps):
        out = []
        for a in aps:
            t = getattr(a, "tensor", None)
            if t is None:
                continue
            try:
                if str(a.space) not in ("MemorySpace.SBUF", "MemorySpace.PSUM"):
                    continue
            except Exception:
                pass
            dims = a.ap
            width = dims[0][0] if dims else 1
            off = a.offset % max(1, width) if width > 0 else a.offset
            ext = 1
            for st_, ct in dims[1:]:
                ext += abs(st_) * (ct - 1)
            out.append((t.name, off - (ext if any(st_ < 0 for st_, _ in dims[1:]) else 0),
                        off + ext))
        return out

    def _overlap(self, a, bset, eng=None):
        for cur in a:
            (n1, l1, h1) = cur[:3]
            for rec in bset:
                (n2, l2, h2) = rec[:3]
                if eng is not None and not SIM_DRAINS:
                    if self.opct[eng] - rec[3] > 1:
                        continue
                if n1 == n2 and l1 < h2 and l2 < h1:
                    return True
        return False

    def _pre(self, eng, reads, writes, wait, track=True):
        need_drain = False
        wait = list(wait)
        if track:
            best = {}
            for (nm, lo, hi) in self._ivs(list(reads) + list(writes)):
                for (l2, h2, sm, tg) in self.dma_touch.get(nm, ()):
                    if lo < h2 and l2 < hi:
                        if best.get(id(sm), (None, -1))[1] < tg:
                            best[id(sm)] = (sm, tg)
            for sm, tg in best.values():
                wait.append((sm, tg))
        if track:
            self.opct[eng] += 1
            ct = self.opct[eng]
            R = [iv + (ct,) for iv in self._ivs(reads)]
            Wv = [iv + (ct,) for iv in self._ivs(writes)]
            tr, tw = self.dirty[eng]
            need_drain = (self._overlap(R, tw, eng) or self._overlap(Wv, tw, eng)
                          or self._overlap(Wv, tr, eng))
            if need_drain:
                k = (self.phase, eng)
                if os.environ.get("KERNEL_DRAIN_REPORT") == "2":
                    f = sys._getframe(2)
                    k = (self.phase, eng, f.f_lineno, sys._getframe(3).f_lineno)
                self.drainct[k] = self.drainct.get(k, 0) + 1
                tr.clear()
                tw.clear()
                tr, tw = self.dirty[eng]
            tr.extend(R)
            tw.extend(Wv)
        wd = self.waited[eng]
        best = {}
        for (s, v) in wait:
            assert v <= s.n, f"wait on {s.name}>={v} but planned only {s.n}"
            if v > wd.get(id(s), -1) and v > best.get(id(s), (None, -1))[1]:
                best[id(s)] = (s, v)
        wl = []
        for s, v in best.values():
            wd[id(s)] = v
            wl.append((s.h, v))

        def pre(e):
            for h, v in wl:
                e.wait_ge(h, v)
            if need_drain:
                e.drain()
        return pre

    def _fin(self, inst, inc):
        if inc is not None:
            inst.then_inc(inc.h, 1)

    def _bump(self, inc):
        if inc is not None:
            inc.n += 1

    def tt(self, out, a, b, op, wait=(), inc=None):
        pre = self._pre("vector", [a, b], [out], wait)
        self._bump(inc)
        def f(e):
            pre(e)
            self._fin(e.tensor_tensor(out, a, b, op), inc)
        self.on("vector", f)

    def ts(self, out, a, s1, s2=None, op0=ALU.mult, op1=None, wait=(), inc=None):
        rd = [a] + [x for x in (s1, s2) if hasattr(x, "tensor")]
        pre = self._pre("vector", rd, [out], wait)
        self._bump(inc)
        if op1 is None:
            def f(e):
                pre(e)
                self._fin(e.tensor_scalar(out, a, s1, None, op0), inc)
        else:
            def f(e):
                pre(e)
                self._fin(e.tensor_scalar(out, a, s1, s2, op0, op1), inc)
        self.on("vector", f)

    def stt(self, out, in0, scalar, in1, op0, op1, wait=(), inc=None):
        rd = [in0, in1] + ([scalar] if hasattr(scalar, "tensor") else [])
        pre = self._pre("vector", rd, [out], wait)
        self._bump(inc)
        def f(e):
            pre(e)
            self._fin(e.scalar_tensor_tensor(out, in0, scalar, in1, op0, op1), inc)
        self.on("vector", f)

    def scan(self, out, a, b, init, wait=(), inc=None):
        pre = self._pre("vector", [a, b, init], [out], wait)
        self._bump(inc)
        def f(e):
            pre(e)
            self._fin(e.tensor_tensor_scan(out, a, b, init, ALU.mult, ALU.add), inc)
        self.on("vector", f)

    def ttr(self, out, in0, in1, accum, wait=(), inc=None):
        pre = self._pre("vector", [in0, in1], [out, accum], wait)
        self._bump(inc)
        def f(e):
            pre(e)
            self._fin(e.tensor_tensor_reduce(out, in0, in1, 1.0, 0.0,
                                             ALU.mult, ALU.add, accum), inc)
        self.on("vector", f)

    def vcopy(self, out, src, wait=(), inc=None):
        pre = self._pre("vector", [src], [out], wait)
        self._bump(inc)
        def f(e):
            pre(e)
            self._fin(e.tensor_copy(out, src), inc)
        self.on("vector", f)

    def vred(self, out, src, op, absval=None, wait=(), inc=None):
        pre = self._pre("vector", [src], [out], wait)
        self._bump(inc)
        def f(e):
            pre(e)
            self._fin(e.tensor_reduce(out, src, AX.X, op, apply_absolute_value=absval), inc)
        self.on("vector", f)

    def vrecip(self, out, src, wait=(), inc=None):
        pre = self._pre("vector", [src], [out], wait)
        self._bump(inc)
        def f(e):
            pre(e)
            self._fin(e.reciprocal(out, src), inc)
        self.on("vector", f)

    def memset(self, out, val, eng="vector", wait=(), inc=None):
        pre = self._pre(eng, [], [out], wait)
        self._bump(inc)
        def f(e):
            pre(e)
            self._fin(e.memset(out, val), inc)
        self.on(eng, f)

    def act(self, out, src, func, bias=None, scale=None, wait=(), inc=None):
        rd = [src] + [x for x in (bias, scale) if hasattr(x, "tensor")]
        pre = self._pre("scalar", rd, [out], wait)
        self._bump(inc)
        kw = {}
        if bias is not None:
            kw["bias"] = bias
        if scale is not None:
            kw["scale"] = scale
        def f(e):
            pre(e)
            self._fin(e.activation(out, src, func, **kw), inc)
        self.on("scalar", f)

    def scopy(self, out, src, wait=(), inc=None):
        pre = self._pre("scalar", [src], [out], wait)
        self._bump(inc)
        def f(e):
            pre(e)
            self._fin(e.activation(out, src, AF.Copy), inc)
        self.on("scalar", f)

    def gtt(self, out, a, b, op, wait=(), inc=None):
        pre = self._pre("gpsimd", [a, b], [out], wait)
        self._bump(inc)
        def f(e):
            pre(e)
            self._fin(e.tensor_tensor(out, a, b, op), inc)
        self.on("gpsimd", f)

    def dma(self, eng, out, src, wait=(), inc=None, guard=True):
        w = list(wait)
        if guard and getattr(self, "guard_sems", None):
            have = {}
            for sm, v in w:
                have[id(sm)] = max(have.get(id(sm), 0), v)
            for sm in self.guard_sems:
                if sm.n > 0 and have.get(id(sm), -1) < sm.n:
                    w.append((sm, sm.n))
        pre = self._pre(eng, [src], [out], w, track=False)
        if inc is not None:
            inc.n += 16
            for iv in self._ivs([src, out]):
                self.dma_touch.setdefault(iv[0], []).append(
                    (iv[1], iv[2], inc, inc.n))
        def f(e):
            pre(e)
            d = e.dma_start(out, src)
            if inc is not None:
                d.then_inc(inc.h, 16)
        self.on(eng, f)

    def coll(self, kind, op, ins, outs, wait=(), inc=None):
        pre = self._pre("gpsimd", [], [], wait)
        self._bump(inc)
        def f(e):
            pre(e)
            c = e.collective_compute(kind, op, replica_groups=[list(range(NCORES))],
                                     ins=ins, outs=outs)
            if inc is not None:
                c.then_inc(inc.h, 1)
        self.on("gpsimd", f)

    def mm(self, out, pairs, wait=(), inc=None, ldw_once=False):
        rd = [p for pr in pairs for p in pr]
        pre = self._pre("tensor", rd, [out], wait)
        self._bump(inc)
        n = len(pairs)
        def f(e):
            pre(e)
            last = None
            for i, (l, r) in enumerate(pairs):
                if not (ldw_once and i > 0):
                    e.ldweights(l)
                last = e.matmul(out, l, r, start=(i == 0), stop=(i == n - 1))
            self._fin(last, inc)
        self.on("tensor", f)

    def barrier(self, sems):
        targets = [(s, s.n) for s in sems]
        for eng in ENGINES:
            for s, v in targets:
                self.wait(eng, s, v)

    def wait(self, eng, sem, val):
        assert val <= sem.n, f"{eng} wait {sem.name}>={val}, planned {sem.n}"
        if self.waited[eng].get(id(sem), -1) >= val:
            return
        self.waited[eng][id(sem)] = val
        def f(e):
            e.wait_ge(sem.h, val)
        self.on(eng, f)

    def finalize(self):
        blk = self.st.enter_context(self.nc.Block())
        qs = self.q

        @blk.sync
        def _(e):
            for f in qs["sync"]:
                f(e)

        @blk.vector
        def _(e):
            for f in qs["vector"]:
                f(e)

        @blk.scalar
        def _(e):
            for f in qs["scalar"]:
                f(e)

        @blk.tensor
        def _(e):
            for f in qs["tensor"]:
                f(e)

        @blk.gpsimd
        def _(e):
            for f in qs["gpsimd"]:
                f(e)


def _chunked(vec, p=128):
    n = vec.shape[0]
    return np.ascontiguousarray(vec.reshape(n // p, p).T)


def _qkvb_perm():
    perm = []
    for c0 in (0, DH, 2 * DH):
        for h in range(NH):
            for i in range(DH):
                perm.append(h * (3 * DH + 1) + c0 + i)
    for h in range(NH):
        perm.append(h * (3 * DH + 1) + 3 * DH)
    return np.array(perm)


def stage_inputs(inputs):
    f32 = np.float32
    g = {}
    g["w_in_f"] = np.ascontiguousarray(inputs["L0_attn_fwd_in_w"].T)
    g["w_in_b"] = np.ascontiguousarray(inputs["L0_attn_bwd_in_w"].T)
    g["w_xp_f"] = np.ascontiguousarray(inputs["L0_attn_fwd_xproj_w"].T)
    g["w_xp_b"] = np.ascontiguousarray(inputs["L0_attn_bwd_xproj_w"].T)
    g["w_dt_f"] = np.ascontiguousarray(inputs["L0_attn_fwd_dt_w"].T)
    g["w_dt_b"] = np.ascontiguousarray(inputs["L0_attn_bwd_dt_w"].T)
    g["w_out_f"] = np.ascontiguousarray(inputs["L0_attn_fwd_out_w"].T)
    g["w_out_b"] = np.ascontiguousarray(inputs["L0_attn_bwd_out_w"].T)
    g["w_mrg"] = np.ascontiguousarray(inputs["L0_attn_merge_w"].T)
    g["w1"] = np.ascontiguousarray(inputs["L0_ffn_W1"].T)
    g["w2"] = np.ascontiguousarray(inputs["L0_ffn_W2"].T)
    g["wqkv"] = np.ascontiguousarray(inputs["L1_attn_W_qkvb"][:, _qkvb_perm()])
    g["wo"] = np.ascontiguousarray(inputs["L1_attn_W_o"])
    g["wl1"] = np.ascontiguousarray(inputs["L1_ffn_weight"].T)
    for nm, key in [("convb_f", "L0_attn_fwd_conv_b"), ("convb_b", "L0_attn_bwd_conv_b"),
                    ("dtb_f", "L0_attn_fwd_dt_b"), ("dtb_b", "L0_attn_bwd_dt_b"),
                    ("dvec_f", "L0_attn_fwd_D"), ("dvec_b", "L0_attn_bwd_D"),
                    ("b1c", "L0_ffn_b1"), ("b2c", "L0_ffn_b2"),
                    ("lng1", "L0_attn_ln_g"), ("lnb1", "L0_attn_ln_b"),
                    ("mrgb", "L0_attn_merge_b"),
                    ("lng3", "L1_attn_ln_g"), ("lnb3", "L1_attn_ln_b"),
                    ("gam", "L1_ffn_gamma"), ("bet", "L1_ffn_beta")]:
        g[nm] = _chunked(np.asarray(inputs[key], f32))
    for nm, key in [("convw_f", "L0_attn_fwd_conv_w"), ("convw_b", "L0_attn_bwd_conv_w")]:
        w = np.asarray(inputs[key], f32)
        g[nm] = np.ascontiguousarray(w.reshape(NDC, 128, 4).transpose(1, 0, 2).reshape(128, NDC * 4))
    for nm, key in [("alog_f", "L0_attn_fwd_A_log"), ("alog_b", "L0_attn_bwd_A_log")]:
        a = np.asarray(inputs[key], f32)
        g[nm] = np.ascontiguousarray(a.reshape(NDC, 128, NS).transpose(1, 0, 2).reshape(128, NDC * NS))
    g["ones16"] = np.ones((128, 128), np.float16)
    g["cvec"] = np.tile(np.array([[EPS, 1.0, MAGIC, -MAGIC + 127.0, 0.0, 0.0, 0.0, 0.0]], np.float32), (128, 1))
    bd2 = np.zeros((128, 128), np.float16)
    bd2[:64, :64] = 1.0
    bd2[64:, 64:] = 1.0
    g["bd2"] = bd2
    selb = np.zeros((128, NS * 128), np.float16)
    for s_ in range(NS):
        selb[s_, s_ * 128:(s_ + 1) * 128] = 1.0
    g["selb"] = selb
    selh = np.zeros((128, 8 * 128), np.float16)
    for ht in range(8):
        selh[2 * ht, ht * 128:ht * 128 + 64] = 1.0
        selh[2 * ht + 1, ht * 128 + 64:(ht + 1) * 128] = 1.0
    g["selh"] = selh
    return g


def kernel(**inputs):
    g = stage_inputs(inputs)
    x = np.asarray(inputs["L0_attn_x"], np.float32)
    in_maps = []
    for c in range(NCORES):
        m = dict(g)
        m["x_dm"] = np.ascontiguousarray(x[c].T)
        in_maps.append(m)
    nc = build_program()
    res = run_bass_kernel_spmd(nc, in_maps, core_ids=list(range(NCORES)))
    out = np.empty((NCORES, T, D), np.float32)
    for c in range(NCORES):
        oc = np.asarray(res.results[c]["out_dm"])
        out_c = oc.reshape(D, NCORES, T // NCORES).transpose(1, 2, 0)
        out[:, c * (T // NCORES):(c + 1) * (T // NCORES), :] = out_c
    return out


def build_program():
    b = Build()
    nc = b.nc
    emit_all(b)
    b.finalize()
    if os.environ.get("KERNEL_DRAIN_REPORT") == "1":
        tot = 0
        for k, v in sorted(b.drainct.items(), key=lambda x: -x[1]):
            print("DRAIN %-32s %-8s %5d" % (k[0], k[1], v))
            tot += v
        print("DRAIN TOTAL", tot)
    return nc



def emit_all(b: Build):
    nc = b.nc
    AP = bass.AP

    def din(name, shape, dt=F32):
        return nc.dram_tensor(name, list(shape), dt, kind="ExternalInput").ap()

    X = din("x_dm", [D, T])
    OUT = nc.dram_tensor("out_dm", [D, T], F32, kind="ExternalOutput").ap()
    W = {}
    for nm, shape in [("w_in_f", (D, 2 * DI)), ("w_in_b", (D, 2 * DI)),
                      ("w_xp_f", (DI, 96)), ("w_xp_b", (DI, 96)),
                      ("w_dt_f", (DTR, DI)), ("w_dt_b", (DTR, DI)),
                      ("w_out_f", (DI, D)), ("w_out_b", (DI, D)),
                      ("w_mrg", (DI, D)), ("w1", (D, H)), ("w2", (H, D)),
                      ("wqkv", (D, 3088)), ("wo", (D, D)), ("wl1", (D, D))]:
        W[nm] = din(nm, shape)
    pp_d = {}
    for nm, w in [("convw_f", NDC * 4), ("convw_b", NDC * 4), ("convb_f", NDC),
                  ("convb_b", NDC), ("dtb_f", NDC), ("dtb_b", NDC),
                  ("dvec_f", NDC), ("dvec_b", NDC), ("alog_f", NDC * NS),
                  ("alog_b", NDC * NS), ("b1c", 32), ("b2c", 8), ("lng1", 8),
                  ("lnb1", 8), ("mrgb", 8), ("lng3", 8), ("lnb3", 8),
                  ("gam", 8), ("bet", 8)]:
        pp_d[nm] = din(nm, (128, w))
    ONES = din("ones16", (128, 128), F16)
    BD2 = din("bd2", (128, 128), F16)
    CVEC = din("cvec", (128, 8), F32)
    SELB = din("selb", (128, NS * 128), F16)
    SELH = din("selh", (128, 8 * 128), F16)

    dr = lambda nm, shape, dt=F16: nc.dram_tensor(nm, list(shape), dt)
    sp_xn = dr("sp_xn", [D, T]); sp_xm = dr("sp_xm", [D, T])
    sp_xi = dr("sp_xi", [DI, TT]); sp_silz = dr("sp_silz", [DI, TT])
    sp_yf = dr("sp_yf", [D, T]); sp_s = dr("sp_s", [D, T])
    sp_h = dr("sp_h", [H, T]); sp_wq2 = dr("sp_wq2", [H, D])
    sp_g = dr("sp_g", [D, T]); sp_outs = dr("sp_outs", [D, T])
    sp_g2 = dr("sp_g2", [D, T])
    sp_x2 = dr("sp_x2", [D, T])
    bounce = dr("bounce", [4, 128], F32)
    ar_i = [dr("ar%d_i" % i, [1, 2], F32) for i in range(3)]
    ar_o = [dr("ar%d_o" % i, [1, 2], F32) for i in range(3)]
    ag_i = dr("ag_i", [D * T], F16)
    ag_o = dr("ag_o", [NCORES * D * T], F16)

    A = b.sbuf("A", [128, NDC * (TT + 4)], F16)
    B = b.sbuf("B", [128, NDC * TT], F16)
    C = b.sbuf("C", [128, NDC * TT], F16)
    Dm = b.sbuf("Dm", [128, NDC * TT], F16)
    E = b.sbuf("E", [128, NDC * TT], F16)
    Fp = b.sbuf("Fp", [128, 5120], F32)
    def dslot(i, n=1):
        return Dm[:, i * TT:(i + n) * TT]
    def eslot(i, n=1):
        return E[:, i * 2048:(i + n) * 2048]
    XWc = NDC * TT
    OFF_M, OFF_A, OFF_C, OFF_R16 = 10240, 11264, 12288, 13312
    OFF_WDT = 3584
    def rep_ap(off, lo, n):
        return bass.AP(E, off + lo, [[XWc, 128], [1, n]])
    def rep_bc(off):
        return bass.AP(E, off, [[XWc, 128], [0, 8], [1, TT]])
    def r16row(j, lo, n, np_=1):
        return bass.AP(E, j * XWc + OFF_R16 + lo, [[XWc, np_], [1, n]])

    pp = {nm: b.sbuf("p_" + nm, [128, pp_d[nm].shape[1]], F32) for nm in pp_d}
    pAn = {d: b.sbuf("pAn" + d, [128, NDC * NS], F32) for d in "fb"}
    c_one = b.sbuf("c_one", [128, 128], F16)
    c_bd2 = b.sbuf("c_bd2", [128, 128], F16)
    c_selb = b.sbuf("c_selb", [128, NS * 128], F16)
    c_selh = b.sbuf("c_selh", [128, 8 * 128], F16)
    t_mm1 = b.sbuf("t_mm1", [128, 4], F32)
    t_hm = b.sbuf("t_hm", [128, 4], F32)
    t_sc = b.sbuf("t_sc", [1, 32], F32)
    t_srep = b.sbuf("t_srep", [128, 16], F32)
    t_sone = b.sbuf("t_sone", [1, 128], F32)
    t_st = b.sbuf("t_st", [128, NDC * NS], F32)
    t_halo = b.sbuf("t_halo", [128, NDC * 4], F16)
    t_ar = b.sbuf("t_ar", [1, 2], F32)
    t_cst = b.sbuf("t_cst", [128, 8], F32)
    t_wst = b.sbuf("t_wst", [128, 64], F32)
    t_scr = b.sbuf("t_scr", [128, 1024], F16)
    t_red = b.sbuf("t_red", [1, 128], F32)

    PS = [b.psum("ps%d" % i) for i in range(8)]

    LD = b.sem("LD"); ST = b.sem("ST"); GW = b.sem("GW")
    VE = b.sem("VE"); SE = b.sem("SE"); TE = b.sem("TE"); CC = b.sem("CC")
    b.guard_sems = [LD, ST, GW, VE, SE, TE]
    bank_free = {i: None for i in range(8)}

    def mm_bank(i, out, pairs, wait=(), inc=None, ldw_once=False,
                start=None, stop=None, no_ldw=False):
        w = list(wait)
        if start is None or start:
            if bank_free[i] is not None:
                w.append(bank_free[i])
                bank_free[i] = None
        rd = [p for pr in pairs for p in pr]
        pre = b._pre("tensor", rd, [out], w)
        b._bump(inc)
        n = len(pairs)
        st_ = True if start is None else start
        sp_ = True if stop is None else stop
        def f(e):
            pre(e)
            last = None
            for j, (l, r) in enumerate(pairs):
                if not no_ldw and not (ldw_once and j > 0):
                    e.ldweights(l)
                last = e.matmul(out, l, r,
                                start=(st_ and j == 0),
                                stop=(sp_ and j == n - 1))
            if inc is not None:
                last.then_inc(inc.h, 1)
        b.on("tensor", f)

    def bank_release(i, sem):
        bank_free[i] = (sem, sem.n)

    def bfree(tile, off, n_rep, inner):
        return AP(tile.tensor if hasattr(tile, "tensor") else tile,
                  (tile.offset if hasattr(tile, "offset") else 0) + off,
                  [[_pstep(tile), 128], [0, n_rep], [1, inner]])

    def _pstep(t):
        if hasattr(t, "ap"):
            return t.ap[0][0]
        return t.shape[1]

    taps = {}
    S_taps_marker = True
    for (nm, shape, dt) in _DEBUG_TAPS:
        taps[nm] = nc.dram_tensor(nm, list(shape), dt, kind="ExternalOutput").ap()

    def tap(nm, src_ap, wait=()):
        if nm in taps:
            b.dma("sync", taps[nm], src_ap, wait=wait, inc=ST)

    b.dma("sync", c_one[:, :], ONES, inc=LD)
    b.dma("sync", c_bd2[:, :], BD2, inc=LD)
    b.dma("sync", t_cst[:, :], CVEC, inc=LD)
    b.dma("sync", c_selb[:, :], SELB, inc=LD)
    b.dma("sync", c_selh[:, :], SELH, inc=LD)
    nc.const_aps.register(F32, EPS, t_cst[:, 0:1])
    nc.const_aps.register(F32, 1.0, t_cst[:, 1:2])
    nc.const_aps.register(F32, MAGIC, t_cst[:, 2:3])
    nc.const_aps.register(F32, -MAGIC + 127.0, t_cst[:, 3:4])
    for nm in pp:
        b.dma("sync", pp[nm][:, :], pp_d[nm], inc=LD)
    consts_ld = LD.n
    b.memset(t_sone[:, :], 1.0, eng="vector", inc=VE)
    b.memset(t_st[:, :], 0.0, eng="vector", inc=VE)
    b.memset(t_sc[:, :], 0.0, eng="vector", inc=VE)
    b.wait("scalar", LD, consts_ld)
    for d in "fb":
        b.act(Fp[:, 0:NDC * NS], pp["alog_" + d][:, :], AF.Exp,
              wait=[(VE, VE.n)], inc=SE)
        b.ts(pAn[d][:, :], Fp[:, 0:NDC * NS], -1.0, wait=[(SE, SE.n)], inc=VE)

    S = dict(locals())
    S["b"] = b
    b.phase = "ln1"; phase_ln1(S)
    b.phase = "wstats"; phase_wstats(S)
    for d_ in "fb":
        for tl_ in range(NTT):
            b.phase = "mamba"; phase_mamba(S, d_, tl_)
    phase_merge_check = None
    b.phase = "ffn"; phase_ffn(S)
    b.phase = "x1ag"; phase_x1_ag(S)
    b.phase = "l1"; phase_l1(S)

    b.wait("sync", ST, ST.n)
    b.wait("sync", LD, LD.n)
    b.wait("gpsimd", GW, GW.n)
    b.wait("vector", VE, VE.n)
    b.wait("scalar", SE, SE.n)
    b.wait("tensor", TE, TE.n)
    b.wait("gpsimd", CC, CC.n)


def _v(S, *names):
    return [S[n] for n in names]


def ap3(th, off, pstep, dims):
    return bass.AP(th, off, [[pstep, 128]] + dims)


def ln_stats(S, pairs_x, pairs_sq, with_rc, wx=(), wsq=()):
    b = S["b"]
    PS = S["PS"]; VE = S["VE"]; SE = S["SE"]; TE = S["TE"]
    Fp = S["Fp"]
    mm_bank, bank_release = S["mm_bank"], S["bank_release"]
    rep_ap = S["rep_ap"]
    OFF_M, OFF_A, OFF_C = S["OFF_M"], S["OFF_A"], S["OFF_C"]
    for f in range(2):
        sb_, qb, b2 = 6 + f, 4 + f, 0 + f
        sA = Fp[:, 0:512]
        sB = Fp[:, 512:1024]
        mm_bank(sb_, PS[sb_][:, 0:512], pairs_x(f), wait=list(wx), inc=TE,
                ldw_once=True)
        mm_bank(qb, PS[qb][:, 0:512], pairs_sq(f), wait=list(wsq), inc=TE,
                ldw_once=True)
        te_s = TE.n
        mmw = [(TE, te_s)]
        bf = S["bank_free"]
        if bf[b2] is not None:
            mmw.append(bf[b2]); bf[b2] = None
        b.ts(sA, PS[sb_][:, 0:512], 1.0 / D, wait=mmw)
        b.vcopy(rep_ap(OFF_M, f * 512, 512), sA)
        b.tt(PS[b2][:, 0:512], sA, sA, ALU.mult)
        b.ts(sB, PS[qb][:, 0:512], 1.0 / D)
        b.tt(PS[qb][:, 0:512], sB, PS[b2][:, 0:512], ALU.subtract, inc=VE)
        b.act(PS[b2][:, 0:512], PS[qb][:, 0:512], AF.Sqrt, bias=EPS,
              wait=[(VE, VE.n)], inc=SE)
        b.vrecip(sA, PS[b2][:, 0:512], wait=[(SE, SE.n)])
        b.vcopy(rep_ap(OFF_A, f * 512, 512), sA, inc=VE)
        if with_rc:
            b.tt(sB, sA, sA, ALU.mult)
            b.tt(PS[b2][:, 0:512], PS[qb][:, 0:512], sB, ALU.mult, inc=VE)
            b.act(PS[qb][:, 0:512], PS[b2][:, 0:512], AF.Sqrt, bias=EPS,
                  wait=[(VE, VE.n)], inc=SE)
            b.vrecip(sB, PS[qb][:, 0:512], wait=[(SE, SE.n)])
            b.tt(PS[b2][:, 0:512], sA, sB, ALU.mult)
            b.vcopy(rep_ap(OFF_C, f * 512, 512), PS[b2][:, 0:512], inc=VE)
        for i in (sb_, qb, b2):
            bank_release(i, VE)


def phase_ln1(S):
    b = S["b"]; AP = bass.AP
    A, B, C, Dm, Fp = _v(S, "A", "B", "C", "Dm", "Fp")
    X, sp_xn, sp_xm = _v(S, "X", "sp_xn", "sp_xm")
    PS, c_one, E = _v(S, "PS", "c_one", "E")
    t_mm1 = S["t_mm1"]
    rep_ap, rep_bc = _v(S, "rep_ap", "rep_bc")
    OFF_M, OFF_A, OFF_C = _v(S, "OFF_M", "OFF_A", "OFF_C")
    pp, t_ar, t_red, bounce = _v(S, "pp", "t_ar", "t_red", "bounce")
    LD, ST, GW, VE, SE, TE, CC = _v(S, "LD", "ST", "GW", "VE", "SE", "TE", "CC")
    mm_bank, bank_release = S["mm_bank"], S["bank_release"]
    ar_i, ar_o = S["ar_i"], S["ar_o"]
    tap = S["tap"]
    AW = NDC * (TT + 4)

    for tl in range(NTT):
        x16 = ap3(A, 0, AW, [[TT, 8], [1, TT]])
        sq = ap3(A, 8 * TT, AW, [[TT, 8], [1, TT]])
        cen = ap3(B, 0, NDC * TT, [[TT, 8], [1, TT]])
        xnb = ap3(B, 8 * TT, NDC * TT, [[TT, 8], [1, TT]])
        xm0 = ap3(C, 0, NDC * TT, [[TT, 8], [1, TT]])
        xmb = ap3(C, 8 * TT, NDC * TT, [[TT, 8], [1, TT]])
        b.dma("gpsimd", x16, AP(X.tensor, tl * TT, [[T, 128], [128 * T, 8], [1, TT]]),
              inc=GW)
        gw_x = GW.n
        b.wait("scalar", GW, gw_x)
        b.act(sq, x16, AF.Square, inc=SE)
        se_sq = SE.n
        px = lambda f: [(c_one[:, 0:128],
                         AP(A, ch * TT + f * 512, [[AW, 128], [1, 512]]))
                        for ch in range(8)]
        psq = lambda f: [(c_one[:, 0:128],
                          AP(A, 8 * TT + ch * TT + f * 512, [[AW, 128], [1, 512]]))
                         for ch in range(8)]
        ln_stats(S, px, psq, True,
                 wx=[(GW, gw_x), (LD, S["consts_ld"])], wsq=[(SE, se_sq)])
        b.wait("vector", TE, TE.n)
        b.tt(cen, x16, rep_bc(OFF_M), ALU.subtract)
        b.tt(xnb, cen, rep_bc(OFF_A), ALU.mult, inc=VE)
        b.dma("sync", AP(sp_xn, tl * TT, [[T, 128], [128 * T, 8], [1, TT]]),
              xnb, wait=[(VE, VE.n)], inc=ST)
        xnb_flat = AP(B, 8 * TT, [[NDC * TT, 128], [1, 8 * TT]])
        b.vred(t_mm1[:, 2 * tl:2 * tl + 1], xnb_flat, ALU.min)
        b.vred(t_mm1[:, 2 * tl + 1:2 * tl + 2], xnb_flat, ALU.max)
        b.tt(xm0, cen, rep_bc(OFF_C), ALU.mult)
        for ch in range(8):
            b.ts(AP(C, 8 * TT + ch * TT, [[NDC * TT, 128], [1, TT]]),
                 AP(C, ch * TT, [[NDC * TT, 128], [1, TT]]),
                 pp["lng1"][:, ch:ch + 1], pp["lnb1"][:, ch:ch + 1],
                 ALU.mult, ALU.add, inc=(VE if ch == 7 else None))
        b.dma("sync", AP(sp_xm, tl * TT, [[T, 128], [128 * T, 8], [1, TT]]),
              xmb, wait=[(VE, VE.n)], inc=ST)
        tap("tap_xn_t%d" % tl, xnb, wait=[(VE, VE.n)])
        tap("tap_xm_t%d" % tl, xmb, wait=[(VE, VE.n)])
    b.tt(t_mm1[:, 0:1], t_mm1[:, 0:1], t_mm1[:, 2:3], ALU.min)
    b.tt(t_mm1[:, 1:2], t_mm1[:, 1:2], t_mm1[:, 3:4], ALU.max, inc=VE)
    b.dma("sync", AP(bounce, 0, [[1, 1], [1, 128]]), t_mm1[:, 0:1],
          wait=[(VE, VE.n)], inc=ST)
    b.dma("sync", AP(bounce, 128, [[1, 1], [1, 128]]), t_mm1[:, 1:2],
          wait=[(VE, VE.n)], inc=ST)
    b.dma("sync", t_red[0:1, :], bounce.ap()[0:1, :], wait=[(ST, ST.n)], inc=LD)
    b.vred(t_ar[0:1, 0:1], t_red[0:1, :], ALU.min, wait=[(LD, LD.n)], inc=VE)
    b.ts(t_ar[0:1, 0:1], t_ar[0:1, 0:1], -1.0, inc=VE)
    b.dma("sync", t_red[0:1, :], bounce.ap()[1:2, :], wait=[(ST, ST.n)], inc=LD)
    b.vred(t_ar[0:1, 1:2], t_red[0:1, :], ALU.max, wait=[(LD, LD.n)], inc=VE)
    b.dma("sync", ar_i[0].ap()[0:1, :], t_ar[0:1, :], wait=[(VE, VE.n)], inc=ST)
    st_ar = ST.n
    b.wait("gpsimd", ST, st_ar)
    b.coll("AllReduce", ALU.max, [ar_i[0].ap().opt()], [ar_o[0].ap().opt()], inc=CC)
    S["cc_ar1"] = CC.n


def phase_wstats(S):
    b = S["b"]; AP = bass.AP
    Fp, W = _v(S, "Fp", "W")
    GW, VE, SE = _v(S, "GW", "VE", "SE")
    t_wst = S["t_wst"]
    slots = []
    k = 0
    for nm, rows, cols, red in [("w1", D, H, ALU.max), ("w2", H, D, ALU.max),
                                ("wl1", D, D, ALU.add)]:
        nt = (rows // 128) * (cols // 2048) if cols >= 2048 else (rows // 128)
        ctile = min(cols, 2048)
        per_row = cols // ctile
        for i in range(rows // 128):
            for j in range(per_row):
                buf = Fp[:, 1024 + (k % 2) * 2048:1024 + (k % 2) * 2048 + ctile]
                b.dma("gpsimd", buf,
                      AP(W[nm].tensor, i * 128 * cols + j * ctile,
                         [[cols, 128], [1, ctile]]),
                      wait=[(VE, VE.n)], inc=GW)
                b.vred(t_wst[:, k:k + 1], buf, red, absval=True,
                       wait=[(GW, GW.n)], inc=VE)
                k += 1
        slots.append((nm, k))
    S["wst_slots"] = slots
    S["ve_wst"] = VE.n


def phase_mamba(S, d, tl):
    b = S["b"]; AP = bass.AP
    A, B, C, Dm, E, Fp = _v(S, "A", "B", "C", "Dm", "E", "Fp")
    PS, c_one, c_selb, pp, pAn = _v(S, "PS", "c_one", "c_selb", "pp", "pAn")
    t_st, t_halo = _v(S, "t_st", "t_halo")
    OFF_WDT = S["OFF_WDT"]
    W, sp_xm, sp_xi, sp_silz, sp_yf, sp_s = _v(S, "W", "sp_xm", "sp_xi",
                                               "sp_silz", "sp_yf", "sp_s")
    sp_xn = S["sp_xn"]
    LD, ST, GW, VE, SE, TE = _v(S, "LD", "ST", "GW", "VE", "SE", "TE")
    mm_bank, bank_release, tap = S["mm_bank"], S["bank_release"], S["tap"]
    AW = NDC * (TT + 4)
    BW = CW = DW = EW = NDC * TT
    sfx = "_" + d
    b.barrier([LD, ST, GW, VE, SE, TE])
    axi = lambda ch, lo, n: AP(A, ch * (TT + 4) + lo, [[AW, 128], [1, n]])
    adl = lambda ch, lo, n: AP(A, ch * TT + lo, [[AW, 128], [1, n]])
    bsl = lambda ch, lo, n: AP(B, ch * TT + lo, [[BW, 128], [1, n]])
    csl = lambda ch, lo, n: AP(C, ch * TT + lo, [[CW, 128], [1, n]])
    dsl = lambda i, lo, n: AP(Dm, i * TT + lo, [[DW, 128], [1, n]])
    esl = lambda i, lo, n: AP(E, i * TT + lo, [[EW, 128], [1, n]])

    b.phase = b.phase.split(':')[0] + ':load_mamba_input_t'
    src_tl = tl if d == "f" else (NTT - 1 - tl)
    xs_raw = ap3(C, 0, CW, [[TT, 8], [1, TT]])
    b.dma("sync", xs_raw,
          AP(sp_xm, src_tl * TT, [[T, 128], [128 * T, 8], [1, TT]]), inc=LD)
    ld_xs = LD.n
    if d == "b":
        b.vcopy(ap3(C, 8 * TT, CW, [[TT, 8], [1, TT]]),
                AP(C, TT - 1, [[CW, 128], [TT, 8], [-1, TT]]),
                wait=[(LD, ld_xs)], inc=VE)
        xs_off = 8 * TT
        ve_xs = VE.n
        if tl == 0:
            tap("tap_xsrev_b0", ap3(C, 8 * TT, CW, [[TT, 8], [1, TT]]),
                wait=[(VE, ve_xs)])
    else:
        xs_off = 0
        ve_xs = None

    b.phase = b.phase.split(':')[0] + ':in_proj_xz_in_w_xs'
    wname = "w_in" + sfx
    wslots = [(0, None), (TT, None)]
    ve_xi = {}
    se_z = {}
    for po in range(32):
        eoff, busy = wslots[po % 2]
        wwait = [] if busy is None else [(TE, busy)]
        b.dma("gpsimd", ap3(E, eoff, EW, [[128, 8], [1, 128]]),
              AP(W[wname].tensor, po * 128,
                 [[2 * DI, 128], [128 * 2 * DI, 8], [1, 128]]),
              wait=wwait, inc=GW)
        gw_w = GW.n
        for f in range(2):
            bk = (po % 2) * 2 + f
            xw = [(GW, gw_w), (LD, ld_xs)]
            if ve_xs is not None:
                xw.append((VE, ve_xs))
            pairs = [(AP(E, eoff + pin * 128, [[EW, 128], [1, 128]]),
                      AP(C, xs_off + pin * TT + f * 512, [[CW, 128], [1, 512]]))
                     for pin in range(8)]
            mm_bank(bk, PS[bk][:, 0:512], pairs, wait=xw, inc=TE)
            te_g = TE.n
            if po < 16:
                b.vcopy(axi(po, 4 + f * 512, 512), PS[bk][:, 0:512],
                        wait=[(TE, te_g)], inc=VE)
                ve_xi[(po, f)] = VE.n
                bank_release(bk, VE)
            else:
                b.act(bsl(po - 16, f * 512, 512), PS[bk][:, 0:512], AF.Silu,
                      wait=[(TE, te_g)], inc=SE)
                se_z[(po - 16, f)] = SE.n
                bank_release(bk, SE)
        wslots[po % 2] = (eoff, TE.n)
    te_inp = TE.n
    for ch in range(NDC):
        b.dma("sync", AP(sp_silz, ch * 128 * TT, [[TT, 128], [1, TT]]),
              bsl(ch, 0, TT), wait=[(SE, se_z[(ch, 1)])], inc=ST)
    st_silz = ST.n

    b.phase = b.phase.split(':')[0] + ':conv_4_tap_causal'
    se_act = {}
    for ch in range(NDC):
        if tl == 0:
            b.memset(axi(ch, 0, 4), 0.0)
        else:
            b.vcopy(axi(ch, 0, 4), t_halo[:, ch * 4:(ch + 1) * 4])
        cw = pp["convw" + sfx]
        wx = [(VE, ve_xi[(ch, 1)]), (TE, te_inp)]
        if ch > 0:
            wx.append((SE, se_act[ch - 1]))
        b.ts(dsl(10, 0, TT), axi(ch, 1, TT), cw[:, ch * 4 + 0:ch * 4 + 1],
             wait=wx, op0=ALU.mult)
        b.ts(dsl(11, 0, TT), axi(ch, 2, TT), cw[:, ch * 4 + 1:ch * 4 + 2])
        b.ts(dsl(12, 0, TT), axi(ch, 3, TT), cw[:, ch * 4 + 2:ch * 4 + 3])
        b.ts(dsl(13, 0, TT), axi(ch, 4, TT), cw[:, ch * 4 + 3:ch * 4 + 4])
        b.tt(dsl(14, 0, TT), dsl(10, 0, TT), dsl(11, 0, TT), ALU.add)
        b.tt(dsl(15, 0, TT), dsl(12, 0, TT), dsl(13, 0, TT), ALU.add)
        b.vcopy(t_halo[:, ch * 4:(ch + 1) * 4], axi(ch, TT + 0, 4))
        b.tt(dsl(10, 0, TT), dsl(14, 0, TT), dsl(15, 0, TT), ALU.add, inc=VE)
        b.act(csl(ch, 0, TT), dsl(10, 0, TT), AF.Silu,
              bias=pp["convb" + sfx][:, ch:ch + 1], wait=[(VE, VE.n)], inc=SE)
        se_act[ch] = SE.n

    if d == "f" and tl == 0:
        tap("tap_xi_f0", ap3(C, 0, CW, [[TT, NDC], [1, TT]]),
            wait=[(SE, SE.n)])
    b.phase = b.phase.split(':')[0] + ':xproj_dt_64_b_16_c'
    b.dma("gpsimd", ap3(E, 2 * TT, EW, [[96, 16], [1, 96]]),
          AP(W["w_xp" + sfx].tensor, 0, [[96, 128], [128 * 96, 16], [1, 96]]),
          inc=GW)
    gw_xp = GW.n
    ve_xp = {}
    for f in range(2):
        pairs = [(AP(E, 2 * TT + pin * 96, [[EW, 128], [1, 64]]),
                  csl(pin, f * 512, 512)) for pin in range(NDC)]
        mm_bank(4, PS[4][0:64, 0:512], pairs,
                wait=[(GW, gw_xp), (SE, se_act[NDC - 1])], inc=TE)
        b.vcopy(bass.AP(Dm, 8 * TT + f * 512, [[DW, 64], [1, 512]]),
                PS[4][0:64, 0:512], wait=[(TE, TE.n)], inc=VE)
        bank_release(4, VE)
        pairs = [(AP(E, 2 * TT + pin * 96 + 64, [[EW, 128], [1, 16]]),
                  csl(pin, f * 512, 512)) for pin in range(NDC)]
        mm_bank(5, PS[5][0:16, 0:512], pairs, wait=[(GW, gw_xp)], inc=TE)
        b.vcopy(bass.AP(Dm, 9 * TT + f * 512, [[DW, 16], [1, 512]]),
                PS[5][0:16, 0:512], wait=[(TE, TE.n)], inc=VE)
        bank_release(5, VE)
        pairs = [(AP(E, 2 * TT + pin * 96 + 80, [[EW, 128], [1, 16]]),
                  csl(pin, f * 512, 512)) for pin in range(NDC)]
        mm_bank(5, PS[5][0:16, 0:512], pairs, wait=[(GW, gw_xp)], inc=TE)
        b.vcopy(bass.AP(Dm, 15 * TT + f * 512, [[DW, 16], [1, 512]]),
                PS[5][0:16, 0:512], wait=[(TE, TE.n)], inc=VE)
        bank_release(5, VE)
        ve_xp[f] = VE.n

    b.phase = b.phase.split(':')[0] + ':dt_proj_softplus_a'
    if tl == 0:
        b.dma("gpsimd", AP(E, OFF_WDT, [[EW, 64], [1, DI]]),
              AP(W["w_dt" + sfx].tensor, 0, [[DI, 64], [1, DI]]),
              wait=[(TE, TE.n)], inc=GW)
        S["gw_wdt_" + d] = GW.n
    gw_dt = S["gw_wdt_" + d]
    se_dl = {}
    for po in range(NDC):
        for f in range(2):
            bk = (po % 2) * 2 + f
            pairs = [(AP(E, OFF_WDT + po * 128, [[EW, 64], [1, 128]]),
                      bass.AP(Dm, 8 * TT + f * 512, [[DW, 64], [1, 512]]))]
            mm_bank(bk, PS[bk][:, 0:512], pairs,
                    wait=[(GW, gw_dt), (VE, ve_xp[f])], inc=TE)
            te_g = TE.n
            fpo = (po % 2) * 512
            b.act(Fp[:, fpo:fpo + 512], PS[bk][:, 0:512], AF.Exp,
                  bias=pp["dtb" + sfx][:, po:po + 1], wait=[(TE, te_g)])
            b.act(adl(po, f * 512, 512), Fp[:, fpo:fpo + 512], AF.Ln,
                  bias=1.0, inc=SE)
            se_dl[(po, f)] = SE.n
            bank_release(bk, SE)

    if d == "f" and tl == 0:
        tap("tap_dl_f0", ap3(A, 0, AW, [[TT, NDC], [1, TT]]), wait=[(SE, SE.n)])
        tap("tap_Brow_f0", bass.AP(Dm, 9 * TT, [[DW, 16], [1, TT]]),
            wait=[(VE, VE.n)])
        tap("tap_Crow_f0", bass.AP(Dm, 15 * TT, [[DW, 16], [1, TT]]),
            wait=[(VE, VE.n)])
    b.phase = b.phase.split(':')[0] + ':deltax_delta_xi_ac'
    ve_dx = {}
    for ch in range(NDC):
        b.tt(bsl(ch, 0, TT), adl(ch, 0, TT), csl(ch, 0, TT), ALU.mult,
             wait=[(SE, se_dl[(ch, 1)]), (ST, st_silz)], inc=VE)
        ve_dx[ch] = VE.n

    b.phase = b.phase.split(':')[0] + ':spill_xi_act_then'
    b.dma("sync", AP(sp_xi, 0, [[TT, 128], [128 * TT, NDC], [1, TT]]),
          ap3(C, 0, CW, [[TT, NDC], [1, TT]]),
          wait=[(SE, se_act[NDC - 1])], inc=ST)
    st_xi = ST.n
    if tl == 0:
        b.memset(t_st[:, :], 0.0)
    scan_ve = {}
    sa_se = {}
    for s in range(NS):
        for f in range(2):
            mm_bank(6, PS[6][:, 0:512],
                    [(c_selb[0:16, s * 128:(s + 1) * 128],
                      bass.AP(Dm, 9 * TT + f * 512, [[DW, 16], [1, 512]]))],
                    wait=[(VE, ve_xp[f])], inc=TE)
            b.vcopy(dsl(6, f * 512, 512), PS[6][:, 0:512], wait=[(TE, TE.n)], inc=VE)
            bank_release(6, VE)
            mm_bank(7, PS[7][:, 0:512],
                    [(c_selb[0:16, s * 128:(s + 1) * 128],
                      bass.AP(Dm, 15 * TT + f * 512, [[DW, 16], [1, 512]]))],
                    wait=[(VE, ve_xp[f])], inc=TE)
            b.vcopy(dsl(7, f * 512, 512), PS[7][:, 0:512], wait=[(TE, TE.n)], inc=VE)
            bank_release(7, VE)
        sc_ve = {}
        for i in range(NDC + 3):
            if i < NDC:
                dc = i
                pr = dc % 2
                aw = [(VE, sc_ve[(dc - 2, s)] if dc >= 2 else VE.n)]
                b.act(dsl(0 + pr, 0, TT), adl(dc, 0, TT), AF.Exp,
                      scale=pAn[d][:, dc * NS + s:dc * NS + s + 1],
                      wait=aw + [(SE, se_dl[(dc, 1)])], inc=SE)
                sa_se[(dc, s)] = SE.n
                b.tt(dsl(2 + pr, 0, TT), bsl(dc, 0, TT), dsl(6, 0, TT), ALU.mult,
                     wait=[(VE, ve_dx[dc])] if s == 0 else ())
            if 1 <= i <= NDC:
                dc = i - 1
                pr = dc % 2
                col = dc * NS + s
                b.scan(dsl(4 + pr, 0, TT), dsl(0 + pr, 0, TT), dsl(2 + pr, 0, TT),
                       t_st[:, col:col + 1], wait=[(SE, sa_se[(dc, s)])], inc=VE)
                sc_ve[(dc, s)] = VE.n
            if 2 <= i <= NDC + 1:
                dc = i - 2
                pr = dc % 2
                col = dc * NS + s
                tslot = dsl(8, 0, TT) if pr == 0 else dsl(14, 0, TT)
                b.tt(tslot, dsl(4 + pr, 0, TT), dsl(7, 0, TT), ALU.mult)
                b.vcopy(t_st[:, col:col + 1], dsl(4 + pr, TT - 1, 1))
            if 3 <= i <= NDC + 2:
                dc = i - 3
                pr = dc % 2
                tslot = dsl(8, 0, TT) if pr == 0 else dsl(14, 0, TT)
                if s == 0:
                    b.vcopy(csl(dc, 0, TT), tslot,
                            wait=[(ST, st_xi)] if dc == 0 else (), inc=VE)
                else:
                    b.tt(csl(dc, 0, TT), csl(dc, 0, TT), tslot, ALU.add, inc=VE)
                scan_ve[(dc, s)] = VE.n
    b.phase = b.phase.split(':')[0] + ':gating_y3_y_xi_d_s'
    for cp in range(0, NDC, 2):
        ld_g = {}
        for j in (0, 1):
            ch = cp + j
            b.dma("sync", dsl(10 + j, 0, TT),
                  AP(sp_xi, ch * 128 * TT, [[TT, 128], [1, TT]]),
                  wait=[(ST, st_xi)], inc=LD)
            b.dma("sync", dsl(12 + j, 0, TT),
                  AP(sp_silz, ch * 128 * TT, [[TT, 128], [1, TT]]),
                  wait=[(ST, st_silz)], inc=LD)
            ld_g[j] = LD.n
        for j in (0, 1):
            ch = cp + j
            b.ts(dsl(14 + j, 0, TT), dsl(10 + j, 0, TT),
                 pp["dvec" + sfx][:, ch:ch + 1], None, ALU.mult,
                 wait=[(LD, ld_g[j]), (VE, scan_ve[(ch, NS - 1)])])
        for j in (0, 1):
            ch = cp + j
            b.tt(dsl(10 + j, 0, TT), csl(ch, 0, TT), dsl(14 + j, 0, TT), ALU.add)
        for j in (0, 1):
            ch = cp + j
            b.tt(csl(ch, 0, TT), dsl(10 + j, 0, TT), dsl(12 + j, 0, TT),
                 ALU.mult, inc=VE)
            scan_ve[("g", ch)] = VE.n

    if d == "f" and tl == 0:
        tap("tap_y3_f0", ap3(C, 0, CW, [[TT, NDC], [1, TT]]),
            wait=[(VE, scan_ve[("g", NDC - 1)])])
    b.phase = b.phase.split(':')[0] + ':out_proj_yst_b_are'
    wname = "w_out" + sfx
    wslots = [(5632, None), (7680, None)]
    ve_yst = {}
    for po in range(8):
        eoff, busy = wslots[po % 2]
        wwait = [(TE, TE.n)] if busy is None else [(TE, busy)]
        b.dma("gpsimd", ap3(E, eoff, EW, [[128, 16], [1, 128]]),
              AP(W[wname].tensor, po * 128,
                 [[D, 128], [128 * D, 16], [1, 128]]),
              wait=wwait, inc=GW)
        gw_w = GW.n
        for f in range(2):
            bk = (po % 2) * 2 + f
            pairs = [(AP(E, eoff + pin * 128, [[EW, 128], [1, 128]]),
                      csl(pin, f * 512, 512)) for pin in range(NDC)]
            mm_bank(bk, PS[bk][:, 0:512], pairs,
                    wait=[(GW, gw_w), (VE, scan_ve[("g", NDC - 1)])], inc=TE)
            te_g = TE.n
            if d == "f":
                b.vcopy(bsl(po, f * 512, 512), PS[bk][:, 0:512],
                        wait=[(TE, te_g)], inc=VE)
            else:
                b.vcopy(AP(B, po * TT + (TT - 1 - f * 512), [[BW, 128], [-1, 512]]),
                        PS[bk][:, 0:512], wait=[(TE, te_g)], inc=VE)
            ve_yst[(po, f)] = VE.n
            bank_release(bk, VE)
        wslots[po % 2] = (eoff, TE.n)
    if d == "f":
        if tl == 0:
            tap("tap_yf_t0", ap3(B, 0, BW, [[TT, 8], [1, TT]]),
                wait=[(VE, ve_yst[(7, 1)])])
        for po in range(8):
            b.dma("sync", AP(sp_yf, tl * TT + po * 128 * T, [[T, 128], [1, TT]]),
                  bsl(po, 0, TT), wait=[(VE, ve_yst[(po, 1)])], inc=ST)
        S["st_yf_%d" % tl] = ST.n
        return

    if d == "b" and tl == 1:
        tap("tap_yb_o0", ap3(B, 0, BW, [[TT, 8], [1, TT]]),
            wait=[(VE, ve_yst[(7, 1)])])
    b.phase = b.phase.split(':')[0] + ':merge_bwd_only_s_l'
    otl = NTT - 1 - tl
    for po in range(8):
        b.dma("sync", AP(B, 8 * TT + po * TT, [[BW, 128], [1, TT]]),
              AP(sp_yf, otl * TT + po * 128 * T, [[T, 128], [1, TT]]),
              wait=[(ST, S["st_yf_%d" % otl])], inc=LD)
    ld_yfr = LD.n
    wslots = [(5632, None), (7680, None)]
    for po in range(8):
        eoff, busy = wslots[po % 2]
        wwait = [(TE, TE.n)] if busy is None else [(TE, busy)]
        b.dma("gpsimd", ap3(E, eoff, EW, [[128, 16], [1, 128]]),
              AP(W["w_mrg"].tensor, po * 128, [[D, 128], [128 * D, 16], [1, 128]]),
              wait=wwait, inc=GW)
        gw_w = GW.n
        b.dma("sync", dsl(15, 0, TT),
              AP(sp_xn, otl * TT + po * 128 * T, [[T, 128], [1, TT]]), inc=LD)
        ld_xn = LD.n
        for f in range(2):
            bk = (po % 2) * 2 + f
            pairs = [(AP(E, eoff + pin * 128, [[EW, 128], [1, 128]]),
                      AP(B, 8 * TT + pin * TT + f * 512, [[BW, 128], [1, 512]]))
                     for pin in range(8)]
            pairs += [(AP(E, eoff + (8 + pin) * 128, [[EW, 128], [1, 128]]),
                       bsl(pin, f * 512, 512)) for pin in range(8)]
            mm_bank(bk, PS[bk][:, 0:512], pairs,
                    wait=[(GW, gw_w), (LD, ld_yfr), (VE, ve_yst[(7, 1)])], inc=TE)
            te_g = TE.n
            b.ts(dsl(14, f * 512, 512), PS[bk][:, 0:512],
                 pp["mrgb"][:, po:po + 1], None, ALU.add,
                 wait=[(TE, te_g), (LD, ld_xn)])
            b.tt(dsl(14, f * 512, 512), dsl(14, f * 512, 512),
                 dsl(15, f * 512, 512), ALU.add, inc=VE)
            bank_release(bk, VE)
        b.dma("sync", AP(sp_s, otl * TT + po * 128 * T, [[T, 128], [1, TT]]),
              dsl(14, 0, TT), wait=[(VE, VE.n)], inc=ST)
    S["st_s_%d" % otl] = ST.n
    if ("tap_s_t%d" % otl) in S["taps"]:
        b.dma("sync", ap3(B, 8 * TT, BW, [[TT, 8], [1, TT]]),
              AP(sp_s, otl * TT, [[T, 128], [128 * T, 8], [1, TT]]),
              wait=[(ST, ST.n)], inc=LD)
        tap("tap_s_t%d" % otl, ap3(B, 8 * TT, BW, [[TT, 8], [1, TT]]),
            wait=[(LD, LD.n)])




def _scalar_chain_quant(b, S, t_ar_ap, cols, sym=False):
    t_sc = S["t_sc"]
    c_inv, c_lo, c_hi, c_s = cols
    r = lambda c: t_sc[0:1, c:c + 1]
    b.tt(r(c_s), t_ar_ap[0:1, 0:1], t_ar_ap[0:1, 1:2], ALU.add)
    b.ts(r(c_s), r(c_s), 1.0 / 255.0, 1e-8, ALU.mult, ALU.max)
    b.vrecip(r(c_inv), r(c_s))
    b.tt(r(c_lo), t_ar_ap[0:1, 0:1], r(c_inv), ALU.mult)
    b.ts(r(c_lo), r(c_lo), MAGIC, MAGIC, ALU.add, ALU.subtract)
    b.ts(r(c_lo), r(c_lo), -128.0, -128.0, ALU.add, ALU.max)
    b.ts(r(c_lo), r(c_lo), 127.0, None, ALU.min)
    b.ts(r(c_hi), r(c_lo), -1.0, 127.0, ALU.mult, ALU.add)
    b.ts(r(c_lo), r(c_lo), -1.0, -128.0, ALU.mult, ALU.add)


def _finalize_col_stat(b, S, part_ap, op, c_dst, wait):
    AP = bass.AP
    bounce, t_red, t_sc = _v(S, "bounce", "t_red", "t_sc")
    LD, ST, VE = _v(S, "LD", "ST", "VE")
    b.vred(S["t_wst"][:, 63:64], part_ap, op, wait=wait, inc=VE)
    b.dma("sync", AP(bounce, 3 * 128, [[1, 1], [1, 128]]),
          S["t_wst"][:, 63:64], wait=[(VE, VE.n)], inc=ST)
    b.dma("sync", t_red[0:1, :], bounce.ap()[3:4, :], wait=[(ST, ST.n)], inc=LD)
    b.vred(t_sc[0:1, c_dst:c_dst + 1], t_red[0:1, :], op, wait=[(LD, LD.n)], inc=VE)


def _rep_cols(b, S, c0, c1):
    PS, t_sone, t_sc, t_srep = _v(S, "PS", "t_sone", "t_sc", "t_srep")
    TE, VE = _v(S, "TE", "VE")
    S["mm_bank"](4, PS[4][:, 0:c1 - c0], [(t_sone[0:1, :], t_sc[0:1, c0:c1])],
                 wait=[(VE, VE.n)], inc=TE, no_ldw=True)
    b.vcopy(t_srep[:, c0:c1], PS[4][:, 0:c1 - c0], wait=[(TE, TE.n)], inc=VE)
    S["bank_release"](4, VE)


def phase_ffn(S):
    b = S["b"]; AP = bass.AP
    A, B, Dm, E, Fp = _v(S, "A", "B", "Dm", "E", "Fp")
    PS, pp, t_sc, t_srep, t_hm, t_wst = _v(S, "PS", "pp", "t_sc", "t_srep",
                                           "t_hm", "t_wst")
    W, sp_xn, sp_h, sp_wq2, sp_g = _v(S, "W", "sp_xn", "sp_h", "sp_wq2", "sp_g")
    t_ar, ar_i, ar_o = _v(S, "t_ar", "ar_i", "ar_o")
    LD, ST, GW, VE, SE, TE, CC = _v(S, "LD", "ST", "GW", "VE", "SE", "TE", "CC")
    mm_bank, bank_release = S["mm_bank"], S["bank_release"]
    AW = NDC * (TT + 4)
    b.barrier([LD, ST, GW, VE, SE, TE])

    _finalize_col_stat(b, S, S["t_wst"][:, 0:16], ALU.max, 11, [(VE, S["ve_wst"])])
    b.ts(t_sc[0:1, 11:12], t_sc[0:1, 11:12], 1.0 / 127.0, 1e-8, ALU.mult, ALU.max)
    b.vrecip(t_sc[0:1, 10:11], t_sc[0:1, 11:12])
    _finalize_col_stat(b, S, S["t_wst"][:, 16:48], ALU.max, 13, [(VE, S["ve_wst"])])
    b.ts(t_sc[0:1, 13:14], t_sc[0:1, 13:14], 1.0 / 127.0, 1e-8, ALU.mult, ALU.max)
    b.vrecip(t_sc[0:1, 12:13], t_sc[0:1, 13:14])
    b.dma("gpsimd", t_ar[0:1, :], ar_o[0].ap()[0:1, :],
          wait=[(CC, S["cc_ar1"])], inc=GW)
    b.wait("vector", GW, GW.n)
    _scalar_chain_quant(b, S, t_ar, (0, 1, 2, 16))
    b.tt(t_sc[0:1, 3:4], t_sc[0:1, 16:17], t_sc[0:1, 11:12], ALU.mult, inc=VE)
    _rep_cols(b, S, 0, 4)
    _rep_cols(b, S, 10, 14)
    b.ts(t_srep[:, 14:15], t_srep[:, 1:2], -1.0, -MAGIC, ALU.mult, ALU.add,
         inc=VE)
    ve_bias1 = VE.n

    b.dma("sync", ap3(A, 0, AW, [[T, 8], [1, T]]),
          AP(sp_xn, 0, [[T, 128], [128 * T, 8], [1, T]]), inc=LD)
    ld_q = LD.n
    sr = lambda c: t_srep[:, c:c + 1]
    ve_pq = None
    for ch in range(8):
        qs = lambda lo, n: AP(A, ch * T + lo, [[AW, 128], [1, n]])
        b.act(Fp[:, 0:2048], qs(0, 2048), AF.Relu, scale=sr(0), bias=MAGIC,
              wait=[(LD, ld_q), (VE, VE.n)] if ch == 0 else ())
        b.act(Fp[:, 2048:4096], Fp[:, 0:2048], AF.Relu, bias=t_srep[:, 14:15],
              wait=([(VE, ve_bias1)] if ch == 0 else
                    [(VE, ve_pq)]), inc=SE)
        b.ts(qs(0, 2048), Fp[:, 2048:4096], sr(1), sr(2), ALU.add, ALU.min,
             wait=[(SE, SE.n)], inc=VE)
        ve_pq = VE.n
    ve_q1 = VE.n

    wslots = [(0, None), (TT, None)]
    stage_busy = {}
    for po in range(32):
        eoff, busy = wslots[po % 2]
        fo = (po % 2) * 1024
        b.dma("gpsimd", ap3(Fp, 2048 + fo, 5120, [[128, 8], [1, 128]]),
              AP(W["w1"].tensor, po * 128, [[H, 128], [128 * H, 8], [1, 128]]),
              wait=[(VE, ve_q1)] if busy is None else [(TE, busy)], inc=GW)
        gw_w = GW.n
        wq_w = [(GW, gw_w)]
        if po == 0:
            wq_w.append((VE, VE.n))
        if busy is not None:
            wq_w.append((TE, busy))
        b.act(Fp[:, 0:1024], Fp[:, 2048 + fo:2048 + fo + 1024], AF.Relu,
              scale=sr(10), bias=MAGIC, wait=wq_w)
        b.act(Fp[:, 1024:2048], Fp[:, 0:1024], AF.Relu, bias=-MAGIC + 127.0,
              wait=([] if ve_pq is None else [(VE, ve_pq)]), inc=SE)
        b.ts(AP(E, eoff, [[NDC * TT, 128], [1, 1024]]),
             Fp[:, 1024:2048], -127.0, 127.0, ALU.add, ALU.min,
             wait=[(SE, SE.n)], inc=VE)
        ve_pq = ve_wq = VE.n
        for f in range(4):
            bk = (po % 2) * 2 + (f % 2)
            pairs = [(AP(E, eoff + pin * 128, [[NDC * TT, 128], [1, 128]]),
                      AP(A, pin * T + f * 512, [[AW, 128], [1, 512]]))
                     for pin in range(8)]
            mm_bank(bk, PS[bk][:, 0:512], pairs,
                    wait=[(VE, max(ve_wq, ve_q1))], inc=TE)
            te_g = TE.n
            sg = (f % 2)
            stg = AP(Dm, (14 + sg) * TT + (f // 2) * 512, [[NDC * TT, 128], [1, 512]])
            sw = [(TE, te_g)]
            if (po, sg, f // 2) != (0, 0, 0) and ((14 + sg), f // 2) in stage_busy:
                sw.append((ST, stage_busy[((14 + sg), f // 2)]))
            b.act(stg, PS[bk][:, 0:512], AF.Gelu, scale=sr(3),
                  bias=pp["b1c"][:, po:po + 1], wait=sw, inc=SE)
            bank_release(bk, SE)
            b.dma("sync", AP(sp_h, po * 128 * T + f * 512, [[T, 128], [1, 512]]),
                  stg, wait=[(SE, SE.n)], inc=ST)
            stage_busy[((14 + sg), f // 2)] = ST.n
            hj = po * 4 + f
            b.vred(Fp[:, 4096 + hj:4097 + hj], stg, ALU.min, wait=[(SE, SE.n)])
            b.vred(Fp[:, 4224 + hj:4225 + hj], stg, ALU.max, inc=VE)
        wslots[po % 2] = (eoff, TE.n)
    b.vred(t_hm[:, 0:1], Fp[:, 4096:4224], ALU.min)
    b.vred(t_hm[:, 1:2], Fp[:, 4224:4352], ALU.max, inc=VE)

    bounce, t_red = _v(S, "bounce", "t_red")
    b.dma("sync", AP(bounce, 0, [[1, 1], [1, 128]]), t_hm[:, 0:1],
          wait=[(VE, VE.n)], inc=ST)
    b.dma("sync", AP(bounce, 128, [[1, 1], [1, 128]]), t_hm[:, 1:2],
          wait=[(VE, VE.n)], inc=ST)
    b.dma("sync", t_red[0:1, :], bounce.ap()[0:1, :], wait=[(ST, ST.n)], inc=LD)
    b.vred(t_ar[0:1, 0:1], t_red[0:1, :], ALU.min, wait=[(LD, LD.n)], inc=VE)
    b.ts(t_ar[0:1, 0:1], t_ar[0:1, 0:1], -1.0, inc=VE)
    b.dma("sync", t_red[0:1, :], bounce.ap()[1:2, :], wait=[(ST, ST.n)], inc=LD)
    b.vred(t_ar[0:1, 1:2], t_red[0:1, :], ALU.max, wait=[(LD, LD.n)], inc=VE)
    b.dma("sync", ar_i[1].ap()[0:1, :], t_ar[0:1, :], wait=[(VE, VE.n)], inc=ST)
    b.wait("gpsimd", ST, ST.n)
    b.coll("AllReduce", ALU.max, [ar_i[1].ap().opt()], [ar_o[1].ap().opt()], inc=CC)
    b.dma("gpsimd", t_ar[0:1, :], ar_o[1].ap()[0:1, :], wait=[(CC, CC.n)], inc=GW)
    b.wait("vector", GW, GW.n)
    _scalar_chain_quant(b, S, t_ar, (4, 5, 6, 17))
    b.tt(t_sc[0:1, 7:8], t_sc[0:1, 17:18], t_sc[0:1, 13:14], ALU.mult, inc=VE)
    _rep_cols(b, S, 4, 8)
    b.ts(t_srep[:, 15:16], t_srep[:, 5:6], -1.0, -MAGIC, ALU.mult, ALU.add,
         inc=VE)
    ve_bias2 = VE.n

    for rch in range(32):
        fo = (rch % 2) * 1024
        b.dma("gpsimd", Fp[:, 2048 + fo:2048 + fo + 1024],
              AP(W["w2"].tensor, rch * 128 * D, [[D, 128], [1, D]]),
              wait=[(VE, VE.n)], inc=GW)
        b.act(Fp[:, 0:1024], Fp[:, 2048 + fo:2048 + fo + 1024], AF.Relu,
              scale=sr(12), bias=MAGIC, wait=[(GW, GW.n)])
        b.act(Fp[:, 1024:2048], Fp[:, 0:1024], AF.Relu, bias=-MAGIC + 127.0,
              wait=[(VE, ve_pq)], inc=SE)
        eoff = (rch % 2) * 1024
        b.ts(AP(E, eoff, [[NDC * TT, 128], [1, 1024]]), Fp[:, 1024:2048],
             -127.0, 127.0, ALU.add, ALU.min, wait=[(SE, SE.n)], inc=VE)
        ve_pq = VE.n
        b.dma("sync", AP(sp_wq2, rch * 128 * D, [[D, 128], [1, D]]),
              AP(E, eoff, [[NDC * TT, 128], [1, 1024]]),
              wait=[(VE, VE.n)], inc=ST)
    st_wq2 = ST.n

    for t4 in range(4):
        b.dma("sync", ap3(B, 0, NDC * TT, [[512, 32], [1, 512]]),
              AP(sp_h, t4 * 512, [[T, 128], [128 * T, 32], [1, 512]]),
              wait=[(TE, TE.n)], inc=LD)
        ld_h = LD.n
        for g8 in range(8):
            qs = lambda lo, n: AP(B, g8 * 2048 + lo, [[NDC * TT, 128], [1, n]])
            b.act(Fp[:, 0:2048], qs(0, 2048), AF.Relu, scale=sr(4), bias=MAGIC,
                  wait=[(LD, ld_h), (VE, VE.n), (GW, GW.n)] if g8 == 0 else ())
            b.act(Fp[:, 2048:4096], Fp[:, 0:2048], AF.Relu,
                  bias=t_srep[:, 15:16],
                  wait=[(VE, ve_bias2 if (t4 == 0 and g8 == 0) else ve_pq)],
                  inc=SE)
            b.ts(qs(0, 2048), Fp[:, 2048:4096], sr(5), sr(6), ALU.add, ALU.min,
                 wait=[(SE, SE.n)], inc=VE)
            ve_pq = VE.n
        ve_q2 = VE.n
        wping = [(0, None), (1024, None)]
        for pin in range(32):
            eoff, busy = wping[pin % 2]
            b.dma("sync", AP(E, eoff, [[NDC * TT, 128], [1, D]]),
                  AP(sp_wq2, pin * 128 * D, [[D, 128], [1, D]]),
                  wait=[(ST, st_wq2)] + ([] if busy is None else [(TE, busy)]),
                  inc=LD)
            ld_w = LD.n
            for po in range(8):
                mm_bank(po, PS[po][:, 0:512],
                        [(AP(E, eoff + po * 128, [[NDC * TT, 128], [1, 128]]),
                          AP(B, pin * 512, [[NDC * TT, 128], [1, 512]]))],
                        wait=[(LD, ld_w), (VE, ve_q2)],
                        start=(pin == 0), stop=(pin == 31),
                        inc=(TE if po == 7 else None))
            wping[pin % 2] = (eoff, TE.n)
        te_f2 = TE.n
        for po in range(8):
            sg = po % 2
            stg = AP(Dm, (14 + sg) * TT, [[NDC * TT, 128], [1, 512]])
            sw = [(TE, te_f2)]
            if ("f2", sg) in stage_busy:
                sw.append((ST, stage_busy[("f2", sg)]))
            b.act(stg, PS[po][:, 0:512], AF.Silu, scale=sr(7),
                  bias=pp["b2c"][:, po:po + 1], wait=sw, inc=SE)
            bank_release(po, SE)
            b.dma("sync", AP(sp_g, po * 128 * T + t4 * 512, [[T, 128], [1, 512]]),
                  stg, wait=[(SE, SE.n)], inc=ST)
            stage_busy[("f2", sg)] = ST.n
    S["st_g"] = ST.n


def phase_x1_ag(S):
    b = S["b"]; AP = bass.AP
    A, B, C = _v(S, "A", "B", "C")
    X, sp_s, sp_g, ag_i, ag_o = _v(S, "X", "sp_s", "sp_g", "ag_i", "ag_o")
    LD, ST, GW, VE, SE, TE, CC = _v(S, "LD", "ST", "GW", "VE", "SE", "TE", "CC")
    AW = NDC * (TT + 4)
    b.barrier([LD, ST, GW, VE, SE, TE])
    for tl in range(NTT):
        b.dma("gpsimd", ap3(C, 0, NDC * TT, [[TT, 8], [1, TT]]),
              AP(X.tensor, tl * TT, [[T, 128], [128 * T, 8], [1, TT]]), inc=GW)
        b.dma("sync", ap3(C, 8 * TT, NDC * TT, [[TT, 8], [1, TT]]),
              AP(sp_s, tl * TT, [[T, 128], [128 * T, 8], [1, TT]]),
              wait=[(ST, S["st_s_%d" % tl])], inc=LD)
        b.dma("sync", ap3(B, 0, NDC * TT, [[TT, 8], [1, TT]]),
              AP(sp_g, tl * TT, [[T, 128], [128 * T, 8], [1, TT]]),
              wait=[(ST, S["st_g"])], inc=LD)
        b.tt(ap3(B, 8 * TT, NDC * TT, [[TT, 8], [1, TT]]),
             ap3(C, 8 * TT, NDC * TT, [[TT, 8], [1, TT]]),
             ap3(B, 0, NDC * TT, [[TT, 8], [1, TT]]), ALU.mult,
             wait=[(LD, LD.n), (GW, GW.n)])
        b.tt(ap3(A, 0, AW, [[TT, 8], [1, TT]]),
             ap3(C, 0, NDC * TT, [[TT, 8], [1, TT]]),
             ap3(B, 8 * TT, NDC * TT, [[TT, 8], [1, TT]]), ALU.add, inc=VE)
        b.dma("sync", AP(ag_i, tl * TT, [[T, 128], [128 * T, 8], [1, TT]]),
              ap3(A, 0, AW, [[TT, 8], [1, TT]]), wait=[(VE, VE.n)], inc=ST)
        if tl == 0:
            S["tap"]("tap_g_t0", ap3(B, 0, NDC * TT, [[TT, 8], [1, TT]]),
                     wait=[(VE, VE.n)])
            S["tap"]("tap_x1_t0", ap3(A, 0, AW, [[TT, 8], [1, TT]]),
                     wait=[(VE, VE.n)])
    b.wait("gpsimd", ST, ST.n)
    b.coll("AllGather", ALU.bypass, [ag_i.ap().opt()], [ag_o.ap().opt()], inc=CC)
    S["cc_ag"] = CC.n


def phase_l1(S):
    b = S["b"]; AP = bass.AP
    A, B, C, Dm, E, Fp = _v(S, "A", "B", "C", "Dm", "E", "Fp")
    PS, pp, c_one, c_bd2 = _v(S, "PS", "pp", "c_one", "c_bd2")
    t_sc, t_srep, t_wst, t_scr = _v(S, "t_sc", "t_srep", "t_wst", "t_scr")
    t_ar = S["t_ar"]
    rep_ap, rep_bc = _v(S, "rep_ap", "rep_bc")
    OFF_M, OFF_A, OFF_C = _v(S, "OFF_M", "OFF_A", "OFF_C")
    W, ag_o, sp_g2, sp_outs, OUT = _v(S, "W", "ag_o", "sp_g2", "sp_outs", "OUT")
    ar_i, ar_o = _v(S, "ar_i", "ar_o")
    LD, ST, GW, VE, SE, TE, CC = _v(S, "LD", "ST", "GW", "VE", "SE", "TE", "CC")
    mm_bank, bank_release, tap = S["mm_bank"], S["bank_release"], S["tap"]
    AW = NDC * (TT + 4)
    XW = NDC * TT
    sp_x2 = S["sp_x2"]
    b.barrier([LD, ST, GW, VE, SE, TE])

    cc_ag = S["cc_ag"]
    def _gather(e, cc_h=CC.h, gw_h=GW.h):
        e.wait_ge(cc_h, cc_ag)
        rk = e.cc_rank([list(range(NCORES))])
        for i in e.Switch(rk, NCORES):
            for rr in range(NCORES):
                e.dma_start(AP(sp_x2, rr * 256, [[T, 128], [128 * T, 8], [1, 256]]),
                            AP(ag_o, rr * D * T + i * 256,
                               [[T, 128], [128 * T, 8], [1, 256]])).then_inc(gw_h, 16)
    b.on("gpsimd", _gather)
    GW.n += NCORES * 16
    b.dma("sync", ap3(A, 0, AW, [[2048, 8], [1, 2048]]),
          AP(sp_x2, 0, [[T, 128], [128 * T, 8], [1, T]]),
          wait=[(GW, GW.n)], inc=LD)
    ld_x2 = LD.n

    x2s = lambda ch, lo, n: AP(A, ch * 2048 + lo, [[AW, 128], [1, n]])
    xn2s = lambda ch, lo, n: AP(Dm, ch * 2048 + lo, [[XW, 128], [1, n]])
    sq = lambda ch, lo, n: AP(B, ch * 1024 + lo, [[XW, 128], [1, n]])
    for hl in range(2):
        b.act(ap3(B, 0, XW, [[1024, 8], [1, 1024]]),
              ap3(A, hl * 1024, AW, [[2048, 8], [1, 1024]]), AF.Square,
              wait=[(LD, ld_x2), (VE, VE.n)], inc=SE)
        se_sq = SE.n
        px = lambda f: [(c_one[:, 0:128], x2s(ch, hl * 1024 + f * 512, 512))
                        for ch in range(8)]
        psq = lambda f: [(c_one[:, 0:128], sq(ch, f * 512, 512))
                         for ch in range(8)]
        ln_stats(S, px, psq, True, wx=[(LD, ld_x2)], wsq=[(SE, se_sq)])
        b.wait("vector", TE, TE.n)
        b.tt(ap3(B, 8 * TT, XW, [[1024, 8], [1, 1024]]),
             ap3(A, hl * 1024, AW, [[2048, 8], [1, 1024]]),
             rep_bc(OFF_M), ALU.subtract)
        b.tt(ap3(Dm, hl * 1024, XW, [[2048, 8], [1, 1024]]),
             ap3(B, 8 * TT, XW, [[1024, 8], [1, 1024]]), rep_bc(OFF_A), ALU.mult)
        b.tt(ap3(B, 0, XW, [[1024, 8], [1, 1024]]),
             ap3(B, 8 * TT, XW, [[1024, 8], [1, 1024]]), rep_bc(OFF_C), ALU.mult)
        for ch in range(8):
            b.ts(AP(A, ch * 2048 + hl * 1024, [[AW, 128], [1, 1024]]),
                 AP(B, ch * 1024, [[XW, 128], [1, 1024]]),
                 pp["gam"][:, ch:ch + 1], pp["bet"][:, ch:ch + 1],
                 ALU.mult, ALU.add, inc=(VE if ch == 7 else None))
    ve_xf = VE.n
    tap("tap_xn2", ap3(Dm, 0, XW, [[2048, 8], [1, 2048]]), wait=[(VE, ve_xf)])

    b.vred(t_wst[:, 62:63], AP(A, 0, [[AW, 128], [1, 8 * 2048]]),
           ALU.max, absval=True, wait=[(VE, ve_xf)], inc=VE)
    bounce, t_red = _v(S, "bounce", "t_red")
    b.dma("sync", AP(bounce, 2 * 128, [[1, 1], [1, 128]]), t_wst[:, 62:63],
          wait=[(VE, VE.n)], inc=ST)
    b.dma("sync", t_red[0:1, :], bounce.ap()[2:3, :], wait=[(ST, ST.n)], inc=LD)
    b.vred(t_ar[0:1, 0:1], t_red[0:1, :], ALU.max, wait=[(LD, LD.n)])
    b.vcopy(t_ar[0:1, 1:2], t_ar[0:1, 0:1], inc=VE)
    b.dma("sync", ar_i[2].ap()[0:1, :], t_ar[0:1, :], wait=[(VE, VE.n)], inc=ST)
    b.wait("gpsimd", ST, ST.n)
    b.coll("AllReduce", ALU.max, [ar_i[2].ap().opt()], [ar_o[2].ap().opt()], inc=CC)
    b.dma("gpsimd", t_ar[0:1, :], ar_o[2].ap()[0:1, :], wait=[(CC, CC.n)], inc=GW)

    _finalize_col_stat(b, S, t_wst[:, 48:56], ALU.add, 14, [(VE, S["ve_wst"])])
    b.ts(t_sc[0:1, 14:15], t_sc[0:1, 14:15], 0.7 / (D * D), None, ALU.mult, inc=VE)
    _rep_cols(b, S, 14, 15)
    es = lambda i, lo=0, n=2048: AP(E, i * 2048 + lo, [[XW, 128], [1, n]])
    wq1 = lambda ch, lo, n: AP(E, ch * 1024 + lo, [[XW, 128], [1, n]])
    for ch in range(8):
        fo = (ch % 2) * 1024
        b.dma("gpsimd", Fp[:, 2048 + fo:2048 + fo + 1024],
              AP(W["wl1"].tensor, ch * 128 * D, [[D, 128], [1, D]]),
              wait=[(VE, VE.n)], inc=GW)
        gw_w = GW.n
        b.wait("scalar", GW, gw_w)
        b.act(es(4, 0, 1024), Fp[:, 2048 + fo:2048 + fo + 1024], AF.Abs, inc=SE)
        b.act(es(4, 1024, 1024), Fp[:, 2048 + fo:2048 + fo + 1024], AF.Sign, inc=SE)
        se_aw = SE.n
        b.ts(es(5, 0, 1024), es(4, 0, 1024), t_srep[:, 14:15], None, ALU.is_gt,
             wait=[(SE, se_aw)])
        b.vred(t_wst[:, ch:ch + 1], es(5, 0, 1024), ALU.add)
        b.tt(es(5, 1024, 1024), es(4, 0, 1024), es(5, 0, 1024), ALU.mult)
        b.vred(t_wst[:, 8 + ch:9 + ch], es(5, 1024, 1024), ALU.add)
        b.tt(wq1(ch, 0, 1024), es(4, 1024, 1024), es(5, 0, 1024), ALU.mult, inc=VE)
    ve_wq1 = VE.n
    _finalize_col_stat(b, S, t_wst[:, 0:8], ALU.add, 20, [(VE, ve_wq1)])
    _finalize_col_stat(b, S, t_wst[:, 8:16], ALU.add, 21, [(VE, ve_wq1)])
    b.ts(t_sc[0:1, 20:21], t_sc[0:1, 20:21], 1.0, None, ALU.max)
    b.vrecip(t_sc[0:1, 22:23], t_sc[0:1, 20:21])
    b.tt(t_sc[0:1, 23:24], t_sc[0:1, 21:22], t_sc[0:1, 22:23], ALU.mult)
    b.wait("vector", GW, GW.n)
    b.ts(t_sc[0:1, 18:19], t_ar[0:1, 0:1], 1.0 / 127.0, 1e-8, ALU.mult, ALU.max)
    b.vrecip(t_sc[0:1, 8:9], t_sc[0:1, 18:19])
    b.tt(t_sc[0:1, 9:10], t_sc[0:1, 18:19], t_sc[0:1, 23:24], ALU.mult, inc=VE)
    _rep_cols(b, S, 8, 10)
    sr = lambda c: t_srep[:, c:c + 1]

    tap("tap_xf", ap3(A, 0, AW, [[2048, 8], [1, 2048]]), wait=[(VE, ve_xf)])
    tap("tap_sc", S["t_sc"][0:1, :], wait=[(VE, VE.n)])
    ve_pq = None
    for ch in range(8):
        xfs = lambda lo, n: AP(A, ch * 2048 + lo, [[AW, 128], [1, n]])
        b.act(Fp[:, 0:2048], xfs(0, 2048), AF.Relu, scale=sr(8), bias=MAGIC,
              wait=[(VE, VE.n), (GW, GW.n)] if ch == 0 else ())
        b.act(Fp[:, 2048:4096], Fp[:, 0:2048], AF.Relu, bias=-MAGIC + 127.0,
              wait=([] if ve_pq is None else [(VE, ve_pq)]), inc=SE)
        b.ts(xfs(0, 2048), Fp[:, 2048:4096], -127.0, 127.0, ALU.add, ALU.min,
             wait=[(SE, SE.n)], inc=VE)
        ve_pq = VE.n
    ve_q3 = VE.n
    tap("tap_q3", ap3(A, 0, AW, [[2048, 8], [1, 2048]]), wait=[(VE, ve_q3)])
    tap("tap_wl1q", ap3(E, 0, XW, [[1024, 8], [1, 1024]]), wait=[(VE, ve_q3)])
    stage_busy = {}
    for f in range(4):
        for pin in range(8):
            for po in range(8):
                mm_bank(po, PS[po][:, 0:512],
                        [(AP(E, pin * 1024 + po * 128, [[XW, 128], [1, 128]]),
                          AP(A, pin * 2048 + f * 512, [[AW, 128], [1, 512]]))],
                        wait=[(VE, ve_q3)], start=(pin == 0), stop=(pin == 7),
                        inc=(TE if pin == 7 else None))
        te_g = TE.n
        for po in range(8):
            sg = po % 2
            stg = t_scr[:, sg * 512:(sg + 1) * 512]
            sw = [(TE, te_g)]
            if ("g2", sg) in stage_busy:
                sw.append((ST, stage_busy[("g2", sg)]))
            b.act(stg, PS[po][:, 0:512], AF.Silu, scale=sr(9), wait=sw, inc=SE)
            bank_release(po, SE)
            b.dma("sync", AP(sp_g2, po * 128 * T + f * 512, [[T, 128], [1, 512]]),
                  stg, wait=[(SE, SE.n)], inc=ST)
            stage_busy[("g2", sg)] = ST.n
    st_g2 = ST.n

    te_ffn = TE.n
    wping = [(0, None), (1024, None)]
    ve_qkv = {}
    se_beta = {}
    for po in range(25):
        eoff, busy = wping[po % 2]
        ncols = 128 if po < 24 else 16
        b.dma("gpsimd", AP(E, eoff, [[XW, 128], [128, 8], [1, ncols]]),
              AP(W["wqkv"].tensor, po * 128,
                 [[3088, 128], [128 * 3088, 8], [1, ncols]]),
              wait=([(TE, te_ffn)] if busy is None else [(TE, busy)]), inc=GW)
        gw_w = GW.n
        for f in range(4):
            bk = (po % 2) * 2 + (f % 2)
            pairs = [(AP(E, eoff + pin * 128, [[XW, 128], [1, ncols]]),
                      AP(Dm, pin * 2048 + f * 512, [[XW, 128], [1, 512]]))
                     for pin in range(8)]
            mm_bank(bk, PS[bk][0:ncols, 0:512], pairs,
                    wait=[(GW, gw_w), (VE, ve_xf)], inc=TE)
            te_g = TE.n
            if po < 8:
                b.vcopy(AP(A, po * 2048 + f * 512, [[AW, 128], [1, 512]]),
                        PS[bk][:, 0:512], wait=[(TE, te_g), (TE, te_ffn)], inc=VE)
                ve_qkv[("q", po, f)] = VE.n
                bank_release(bk, VE)
            elif po < 16:
                b.vcopy(AP(B, (po - 8) * 2048 + f * 512, [[XW, 128], [1, 512]]),
                        PS[bk][:, 0:512], wait=[(TE, te_g)], inc=VE)
                ve_qkv[("k", po - 8, f)] = VE.n
                bank_release(bk, VE)
            elif po < 24:
                b.vcopy(AP(C, (po - 16) * 2048 + f * 512, [[XW, 128], [1, 512]]),
                        PS[bk][:, 0:512], wait=[(TE, te_g)], inc=VE)
                ve_qkv[("v", po - 16, f)] = VE.n
                bank_release(bk, VE)
            else:
                b.act(AP(E, 7 * 2048 + f * 512, [[XW, 16], [1, 512]]),
                      PS[bk][0:16, 0:512], AF.Sigmoid,
                      wait=[(TE, te_g)], inc=SE)
                se_beta[f] = SE.n
                bank_release(bk, SE)
        wping[po % 2] = (eoff, TE.n)

    es2 = lambda lo, n: AP(E, 2 * 2048 + lo, [[XW, 128], [1, n]])
    es3 = lambda lo, n: AP(E, 4 * 2048 + lo, [[XW, 128], [1, n]])
    bd2cols = AP(c_bd2, 0, [[128, 128], [64, 2]])
    ve_norm = {}
    for which, Ar in (("q", A), ("k", B)):
        Wd = AW if which == "q" else XW
        for ch in range(8):
            sl = lambda lo, n: AP(Ar, ch * 2048 + lo, [[Wd, 128], [1, n]])
            wv = [(VE, ve_qkv[(which, ch, 3)])]
            b.ts(es2(0, 2048), sl(0, 2048), 0.0, None, ALU.min, wait=wv, inc=VE)
            b.wait("scalar", VE, VE.n)
            b.act(es3(0, 2048), es2(0, 2048), AF.Exp, inc=SE)
            b.stt(es2(0, 2048), sl(0, 2048), 0.0, es3(0, 2048), ALU.max, ALU.add,
                  wait=[(SE, SE.n)], inc=VE)
            ve_e = VE.n
            for f in range(4):
                mm_bank(6, PS[6][0:2, 0:512],
                        [(bd2cols, es2(f * 512, 512))],
                        wait=[(VE, ve_e)], inc=TE)
                mw = [(TE, TE.n)]
                bf = S["bank_free"]
                for bx in (4, 5):
                    if bf[bx] is not None:
                        mw.append(bf[bx]); bf[bx] = None
                b.vrecip(PS[5][0:2, 0:512], PS[6][0:2, 0:512], wait=mw)
                if which == "q":
                    b.ts(PS[4][0:2, 0:512], PS[5][0:2, 0:512], 0.125, None,
                         ALU.mult)
                    b.vcopy(AP(E, 9728, [[XW, 2], [1, 512]]), PS[4][0:2, 0:512],
                            inc=VE)
                else:
                    b.vcopy(AP(E, 9728, [[XW, 2], [1, 512]]), PS[5][0:2, 0:512],
                            inc=VE)
                bank_release(6, VE)
                bank_release(5, VE)
                bank_release(4, VE)
                mm_bank(7, PS[7][:, 0:512],
                        [(c_bd2[0:2, :], AP(E, 9728, [[XW, 2], [1, 512]]))],
                        wait=[(VE, VE.n)], inc=TE)
                b.tt(sl(f * 512, 512), es2(f * 512, 512), PS[7][:, 0:512],
                     ALU.mult, wait=[(TE, TE.n)], inc=VE)
                bank_release(7, VE)
            ve_norm[(which, ch)] = VE.n

    el = lambda i, lo, n: AP(E, i * 2048 + lo, [[XW, 128], [1, n]])
    st_outs = {}
    for ht in range(8):
        qs = lambda lo, n: AP(A, ht * 2048 + lo, [[AW, 128], [1, n]])
        ks = lambda lo, n: AP(B, ht * 2048 + lo, [[XW, 128], [1, n]])
        vs = lambda lo, n: AP(C, ht * 2048 + lo, [[XW, 128], [1, n]])
        kbc = lambda u, nrep: AP(B, ht * 2048 + u * 256,
                                 [[XW, 128], [0, nrep], [1, 256]])
        dbc = lambda u, nrep: AP(E, 0 * 2048 + u * 256,
                                 [[XW, 128], [0, nrep], [1, 256]])
        for f in range(4):
            mm_bank(5, PS[5][:, 0:512],
                    [(S["c_selh"][0:16, ht * 128:(ht + 1) * 128],
                      AP(E, 7 * 2048 + f * 512, [[XW, 16], [1, 512]]))],
                    wait=[(SE, se_beta[f]), (VE, VE.n)], inc=TE)
            b.vcopy(el(2, f * 512, 512), PS[5][:, 0:512], wait=[(TE, TE.n)], inc=VE)
            bank_release(5, VE)
        b.tt(el(1, 0, 2048), vs(0, 2048), el(2, 0, 2048), ALU.mult,
             wait=[(VE, ve_qkv[("v", ht, 3)])])
        b.vcopy(el(0, 0, 256), el(1, 0, 256))
        prev_mm = None
        for u in range(7):
            span = (7 - u) * 256
            base = (u + 1) * 256
            for c0 in range(0, span, 512):
                n = min(512, span - c0)
                pw = [(VE, ve_norm[("k", ht)])] if (u == 0 and c0 == 0) else []
                if prev_mm is not None:
                    pw.append((TE, prev_mm))
                b.tt(t_scr[:, 0:n], ks(base + c0, n), kbc(u, n // 256), ALU.mult,
                     wait=pw, inc=VE)
                mm_bank(4, PS[4][:, 0:n], [(c_bd2[:, :], t_scr[:, 0:n])],
                        wait=[(VE, VE.n)], inc=TE)
                prev_mm = TE.n
                b.tt(t_scr[:, 512:512 + n], dbc(u, n // 256), PS[4][:, 0:n],
                     ALU.mult, wait=[(TE, prev_mm)], inc=VE)
                bank_release(4, VE)
                if u == 0:
                    b.vcopy(el(3, base + c0, n), t_scr[:, 512:512 + n])
                else:
                    b.tt(el(3 + (u % 2), base + c0, n),
                         el(3 + ((u + 1) % 2), base + c0, n),
                         t_scr[:, 512:512 + n], ALU.add)
            acur = 3 + (u % 2) if u > 0 else 3
            b.tt(t_scr[:, 0:256], el(acur, base, 256), el(2, base, 256), ALU.mult)
            b.tt(el(0, base, 256), el(1, base, 256), t_scr[:, 0:256],
                 ALU.subtract, inc=VE)
        for u in range(8):
            span = (8 - u) * 256
            base = u * 256
            for c0 in range(0, span, 512):
                n = min(512, span - c0)
                pw = [(VE, ve_norm[("q", ht)])] if (u == 0 and c0 == 0) else []
                if prev_mm is not None:
                    pw.append((TE, prev_mm))
                b.tt(t_scr[:, 0:n], qs(base + c0, n), kbc(u, n // 256), ALU.mult,
                     wait=pw, inc=VE)
                mm_bank(4, PS[4][:, 0:n], [(c_bd2[:, :], t_scr[:, 0:n])],
                        wait=[(VE, VE.n)], inc=TE)
                prev_mm = TE.n
                b.tt(t_scr[:, 512:512 + n], dbc(u, n // 256), PS[4][:, 0:n],
                     ALU.mult, wait=[(TE, prev_mm)], inc=VE)
                bank_release(4, VE)
                if u == 0:
                    b.vcopy(el(5, base + c0, n), t_scr[:, 512:512 + n])
                else:
                    b.tt(el(5 + (u % 2), base + c0, n),
                         el(5 + ((u + 1) % 2), base + c0, n),
                         t_scr[:, 512:512 + n], ALU.add)
            if u > 0:
                b.vcopy(el(5 + (u % 2), 0, base), el(5 + ((u + 1) % 2), 0, base),
                        inc=(VE if u == 7 else None))
        b.dma("sync", AP(sp_outs, ht * 128 * T, [[T, 128], [1, T]]),
              el(6, 0, 2048), wait=[(VE, VE.n)], inc=ST)
        st_outs[ht] = ST.n

    if "tap_outs" in S["taps"]:
        b.dma("sync", ap3(B, 0, XW, [[2048, 8], [1, 2048]]),
              AP(sp_outs, 0, [[T, 128], [128 * T, 8], [1, T]]),
              wait=[(ST, st_outs[7])], inc=LD)
        tap("tap_outs", ap3(B, 0, XW, [[2048, 8], [1, 2048]]), wait=[(LD, LD.n)])
    wping = [(0, None), (1024, None)]
    ve_x3 = {}
    for f in range(4):
        b.dma("sync", ap3(B, 0, XW, [[512, 8], [1, 512]]),
              AP(sp_outs, f * 512, [[T, 128], [128 * T, 8], [1, 512]]),
              wait=[(ST, st_outs[7]), (TE, TE.n)], inc=LD)
        ld_o = LD.n
        for pin in range(8):
            eoff, busy = wping[pin % 2]
            b.dma("gpsimd", AP(E, eoff, [[XW, 128], [1, D]]),
              AP(W["wo"].tensor, pin * 128 * D, [[D, 128], [1, D]]),
              wait=[(TE, TE.n)] if busy is None else [(TE, busy)], inc=GW)
            gw_w = GW.n
            for po in range(8):
                mm_bank(po, PS[po][:, 0:512],
                        [(AP(E, eoff + po * 128, [[XW, 128], [1, 128]]),
                          AP(B, pin * 512, [[XW, 128], [1, 512]]))],
                        wait=[(GW, gw_w), (LD, ld_o)],
                        start=(pin == 0), stop=(pin == 7),
                        inc=(TE if po == 7 else None))
            wping[pin % 2] = (eoff, TE.n)
        te_g = TE.n
        for po in range(8):
            b.tt(AP(C, po * 2048 + f * 512, [[XW, 128], [1, 512]]),
                 PS[po][:, 0:512],
                 AP(Dm, po * 2048 + f * 512, [[XW, 128], [1, 512]]), ALU.add,
                 wait=[(TE, te_g)], inc=VE)
            bank_release(po, VE)
        ve_x3[f] = VE.n

    tap("tap_x3", ap3(C, 0, XW, [[2048, 8], [1, 2048]]),
        wait=[(VE, ve_x3[3])])
    for hl in range(2):
        b.act(ap3(B, 0, XW, [[1024, 8], [1, 1024]]),
              ap3(C, hl * 1024, XW, [[2048, 8], [1, 1024]]), AF.Square,
              wait=[(VE, ve_x3[3]), (VE, VE.n)], inc=SE)
        se_sq = SE.n
        px = lambda f: [(c_one[:, 0:128],
                         AP(C, ch * 2048 + hl * 1024 + f * 512, [[XW, 128], [1, 512]]))
                        for ch in range(8)]
        psq = lambda f: [(c_one[:, 0:128],
                          AP(B, ch * 1024 + f * 512, [[XW, 128], [1, 512]]))
                         for ch in range(8)]
        ln_stats(S, px, psq, False, wx=[(VE, ve_x3[3])], wsq=[(SE, se_sq)])
        b.wait("vector", TE, TE.n)
        cen2 = ap3(B, 8 * TT, XW, [[1024, 8], [1, 1024]])
        b.tt(cen2, ap3(C, hl * 1024, XW, [[2048, 8], [1, 1024]]),
             rep_bc(OFF_M), ALU.subtract)
        b.tt(ap3(B, 0, XW, [[1024, 8], [1, 1024]]), cen2, rep_bc(OFF_A), ALU.mult)
        for ch in range(8):
            b.ts(AP(B, 8 * TT + ch * 1024, [[XW, 128], [1, 1024]]),
                 AP(B, ch * 1024, [[XW, 128], [1, 1024]]),
                 pp["lng3"][:, ch:ch + 1], pp["lnb3"][:, ch:ch + 1],
                 ALU.mult, ALU.add, inc=(VE if ch == 7 else None))
        if hl == 0:
            b.vcopy(ap3(Dm, 8 * TT, XW, [[1024, 8], [1, 1024]]),
                    ap3(B, 8 * TT, XW, [[1024, 8], [1, 1024]]), inc=VE)
    ve_s2 = VE.n
    tap("tap_s2a", ap3(Dm, 8 * TT, XW, [[1024, 8], [1, 1024]]), wait=[(VE, ve_s2)])
    tap("tap_s2b", ap3(B, 8 * TT, XW, [[1024, 8], [1, 1024]]), wait=[(VE, ve_s2)])

    b.dma("sync", ap3(A, 0, AW, [[2048, 8], [1, 2048]]),
          AP(sp_x2, 0, [[T, 128], [128 * T, 8], [1, T]]),
          wait=[(VE, ve_s2)], inc=LD)
    b.dma("sync", ap3(C, 0, XW, [[2048, 8], [1, 2048]]),
          AP(sp_g2, 0, [[T, 128], [128 * T, 8], [1, T]]),
          wait=[(ST, st_g2), (VE, ve_s2)], inc=LD)
    ld_fin = LD.n
    tap("tap_g2", ap3(C, 0, XW, [[2048, 8], [1, 2048]]), wait=[(LD, ld_fin)])
    stage_busy2 = {}
    for ch in range(8):
        fo = (ch % 2) * 2048
        sw = [(LD, ld_fin)]
        if ("fin", ch % 2) in stage_busy2:
            sw.append((ST, stage_busy2[("fin", ch % 2)]))
        for hl in range(2):
            s2t, s2base = (Dm, 8 * TT) if hl == 0 else (B, 8 * TT)
            b.tt(t_scr[:, 0:1024],
                 AP(s2t, s2base + ch * 1024, [[XW, 128], [1, 1024]]),
                 AP(C, ch * 2048 + hl * 1024, [[XW, 128], [1, 1024]]), ALU.mult,
                 wait=sw if hl == 0 else ())
            b.tt(Fp[:, fo + hl * 1024:fo + (hl + 1) * 1024],
                 AP(A, ch * 2048 + hl * 1024, [[AW, 128], [1, 1024]]),
                 t_scr[:, 0:1024], ALU.add, inc=VE)
        b.dma("sync", AP(S["OUT"].tensor, ch * 128 * T, [[T, 128], [1, T]]),
              Fp[:, fo:fo + 2048], wait=[(VE, VE.n)], inc=ST)
        stage_busy2[("fin", ch % 2)] = ST.n
